# version 15
# baseline (speedup 1.0000x reference)
import contextlib
import numpy as np
import ml_dtypes
import concourse.bass as bass
import concourse.mybir as mybir
from concourse.bass_utils import run_bass_kernel_spmd

F32 = mybir.dt.float32
BF16 = mybir.dt.bfloat16
AF = mybir.ActivationFunctionType
ALU = mybir.AluOpType
BF = ml_dtypes.bfloat16

KC = 16
TP = 2560
NCHUNK = 5
NJ = 12
NI = 64
NRI = 8
EPS = 1e-5
INV_D = 1.0 / 2048.0

_DEBUG_TAPS = []

ENG_SEM = {"tensor": "s_pe", "scalar": "s_act", "vector": "s_dve",
           "gpsimd": "s_pool", "sync": "s_sy"}
NDS = 4
DMA_SEMS = {q: [f"d_{q[:2]}{i}" for i in range(NDS)] for q in ("sync", "gpsimd", "scalar")}
ALL_SEMS = sorted(set(list(ENG_SEM.values())
                      + [s for v in DMA_SEMS.values() for s in v]))


class Res:
    __slots__ = ("w", "r")

    def __init__(self):
        self.w = None
        self.r = []


class Sched:
    def __init__(self):
        self.ops = []
        self.cnt = {}
        self.pending = {}
        self.dma_idx = {q: 0 for q in DMA_SEMS}
        self.phase = "init"
        self.dbg_map = {e: [] for e in ENG_SEM}

    def _mk(self, eng, emit, waits, inc_sem, inc_by):
        if eng in self.pending:
            waits = list(waits) + self.pending.pop(eng)
        self.cnt[inc_sem] = self.cnt.get(inc_sem, 0) + inc_by
        tok = (inc_sem, self.cnt[inc_sem], eng)
        best = {}
        for wt in waits:
            if wt is None:
                continue
            s, v, peng = wt
            if best.get(s, -1) < v:
                best[s] = v
        wl = sorted(best.items())

        ph = self.phase

        def run(e, emit=emit, wl=wl, inc=(inc_sem, inc_by), ph=ph, eng=eng):
            n0 = e.bass.n_instructions() if hasattr(e, "bass") else None
            for s, v in wl:
                e.wait_ge(_SEMS[s], v)
            inst = emit(e)
            inst.then_inc(_SEMS[inc[0]], inc[1])
            n1 = e.bass.n_instructions() if n0 is not None else None
            self.dbg_map[eng].append((n1 - n0 if n0 is not None else 1 + len(wl), ph))
        self.ops.append((eng, run))
        return tok

    def _deps(self, reads, writes, waits):
        ws = list(waits)
        for r in reads:
            if r.w is not None:
                ws.append(r.w)
        for r in writes:
            if r.w is not None:
                ws.append(r.w)
            ws.extend(r.r)
        return ws

    def _note(self, tok, reads, writes):
        for r in reads:
            r.r.append(tok)
        for r in writes:
            r.w = tok
            r.r = []

    def op(self, eng, emit, reads=(), writes=(), waits=()):
        tok = self._mk(eng, emit, self._deps(reads, writes, waits), ENG_SEM[eng], 1)
        self._note(tok, reads, writes)
        return tok

    def dma(self, eng, emit, reads=(), writes=(), waits=()):
        sem = DMA_SEMS[eng][self.dma_idx[eng] % NDS]
        self.dma_idx[eng] += 1
        ws = self._deps(reads, writes, waits)
        prior = self.cnt.get(sem, 0)
        if prior > 0:
            ws = list(ws) + [(sem, prior, "_dma")]
        tok = self._mk(eng, emit, ws, sem, 16)
        self._note(tok, reads, writes)
        return tok

    def snapshot(self):
        return [(s, self.cnt.get(s, 0), "_f") for s in ALL_SEMS if self.cnt.get(s, 0) > 0]

    def fence(self):
        snap = self.snapshot()
        for e in ENG_SEM:
            self.pending[e] = list(snap)

    def fence_from(self, snap):
        for e in ENG_SEM:
            self.pending[e] = list(snap)


_SEMS = None


def _host_shared(inp):
    g = {}
    in_proj_W = inp["L0_attn_in_proj_W"]
    A_log = inp["L0_attn_A_log"]
    pj = np.arange(128)
    g["lnw_cols"] = np.ascontiguousarray(inp["L0_attn_ln_w"].reshape(16, 128).T)
    g["lnb_cols"] = np.ascontiguousarray(inp["L0_attn_ln_b"].reshape(16, 128).T)
    g["inpT"] = np.ascontiguousarray(
        in_proj_W.reshape(96, 16, 128).transpose(2, 1, 0).reshape(128, 16 * 96))
    g["outT"] = np.ascontiguousarray(inp["L0_attn_out_proj_W"].T)
    g["xprojT"] = np.ascontiguousarray(inp["L0_attn_x_proj_W"].T)
    g["dtprojT"] = np.ascontiguousarray(inp["L0_attn_dt_proj_W"].T)
    g["dtb_col"] = np.ascontiguousarray(inp["L0_attn_dt_proj_b"].reshape(96, 1))
    g["D_col"] = np.ascontiguousarray(inp["L0_attn_D_param"].reshape(96, 1))
    g["convw"] = np.ascontiguousarray(inp["L0_attn_conv_w"].reshape(96, 3))
    g["convb_col"] = np.ascontiguousarray(inp["L0_attn_conv_b"].reshape(96, 1))
    acols = np.empty((128, 12), np.float32)
    for j in range(12):
        acols[:, j] = A_log[8 * j + pj // 16, pj % 16]
    g["Acols"] = acols
    selD = np.zeros((96, 12 * 128), np.float32)
    for j in range(12):
        selD[8 * j + pj // 16, 128 * j + pj] = 1.0
    g["selD"] = selD
    selB = np.zeros((38, 128), np.float32)
    selC = np.zeros((38, 128), np.float32)
    selB[6 + pj % 16, pj] = 1.0
    selC[22 + pj % 16, pj] = 1.0
    g["selB"] = selB
    g["selC"] = selC
    selR = np.zeros((128, 12 * 96), np.float32)
    for j in range(12):
        selR[pj, 96 * j + (8 * j + pj // 16)] = 1.0
    g["selR"] = selR
    sel8 = np.zeros((8, 8 * 128), np.float32)
    for tt in range(8):
        sel8[tt, 128 * tt:128 * (tt + 1)] = 1.0
    g["sel8"] = sel8
    g["ident"] = np.eye(128, dtype=np.float32)
    g["cconsts"] = np.repeat(np.array([[1e-5, 1.0 / 2048.0, -1.0 / 2048.0, -0.5,
                                        np.float32(1.0) + np.float32(1e-5),
                                        np.float32(1e-5) * np.float32(1e-5)]],
                                      np.float32), 128, axis=0)
    g["ones1x128"] = np.ones((1, 128), np.float32)
    g["ones1x128bf"] = np.ones((1, 128), BF)
    g["ones128bf"] = np.ones((128, 1), BF)
    rw = inp["L0_ffn_router_weight"]
    g["routerT"] = np.ascontiguousarray(
        rw.reshape(8, 16, 128).transpose(2, 1, 0).reshape(128, 128))
    g["Wg"] = np.ascontiguousarray(
        inp["L0_ffn_sh_gate"].reshape(64, 128, 16, 128).transpose(0, 3, 2, 1).reshape(64, 128, 2048))
    g["Wu"] = np.ascontiguousarray(
        inp["L0_ffn_sh_up"].reshape(64, 128, 16, 128).transpose(0, 3, 2, 1).reshape(64, 128, 2048))
    g["Wd"] = np.ascontiguousarray(
        inp["L0_ffn_sh_down"].reshape(16, 128, 64, 128).transpose(0, 3, 2, 1).reshape(16, 128, 8192))
    g["Wrg"] = np.ascontiguousarray(
        inp["L0_ffn_r_gate"].reshape(8, 128, 16, 128).transpose(0, 3, 2, 1).reshape(8, 128, 2048))
    g["Wru"] = np.ascontiguousarray(
        inp["L0_ffn_r_up"].reshape(8, 128, 16, 128).transpose(0, 3, 2, 1).reshape(8, 128, 2048))
    g["Wrd"] = np.ascontiguousarray(
        inp["L0_ffn_r_down"].reshape(16, 128, 8, 128).transpose(0, 3, 2, 1).reshape(16, 128, 1024))
    return g


def _host_percore(x, c):
    b, h = c // 2, c % 2
    xt = x[b].T
    ctx = np.zeros((2048, TP), np.float32)
    mask = np.zeros((1, TP), np.float32)
    if h == 0:
        ctx[:, 1024:2048] = xt[:, 0:1024]
        ctx[:, 2048] = xt[:, 1024]
        mask[0, 1024:2049] = 1.0
    else:
        ctx[:, 0:2048] = xt[:, 0:2048]
        mask[0, 0:2048] = 1.0
    return {"xT": np.ascontiguousarray(ctx),
            "xmask": np.ascontiguousarray(mask.astype(BF))}


def build_program(debug_taps=()):
    global _SEMS
    nc = bass.Bass("TRN2", target_bir_lowering=False, debug=False)

    def din(name, shape, dt=F32):
        return nc.dram_tensor(name, list(shape), dt, kind="ExternalInput").ap()

    xT = din("xT", (2048, TP))
    xmask = din("xmask", (1, TP), BF16)
    lnw_h = din("lnw_cols", (128, 16))
    lnb_h = din("lnb_cols", (128, 16))
    inpT_h = din("inpT", (128, 16 * 96))
    outT_h = din("outT", (96, 2048))
    xprojT_h = din("xprojT", (96, 38))
    dtprojT_h = din("dtprojT", (6, 96))
    dtb_h = din("dtb_col", (96, 1))
    Dcol_h = din("D_col", (96, 1))
    convw_h = din("convw", (96, 3))
    convb_h = din("convb_col", (96, 1))
    Acols_h = din("Acols", (128, 12))
    selD_h = din("selD", (96, 12 * 128))
    selB_h = din("selB", (38, 128))
    selC_h = din("selC", (38, 128))
    selR_h = din("selR", (128, 12 * 96))
    sel8_h = din("sel8", (8, 8 * 128))
    ident_h = din("ident", (128, 128))
    cconsts_h = din("cconsts", (128, 6))
    ones1x128_h = din("ones1x128", (1, 128))
    ones1x128bf_h = din("ones1x128bf", (1, 128), BF16)
    ones128bf_h = din("ones128bf", (128, 1), BF16)
    routerT_h = din("routerT", (128, 128))
    Wg_h = din("Wg", (64, 128, 2048))
    Wu_h = din("Wu", (64, 128, 2048))
    Wd_h = din("Wd", (16, 128, 8192))
    Wrg_h = din("Wrg", (8, 128, 2048))
    Wru_h = din("Wru", (8, 128, 2048))
    Wrd_h = din("Wrd", (16, 128, 1024))

    out_dev = nc.dram_tensor("out", [2048, 1024], F32, kind="ExternalOutput").ap()
    x2_st = nc.dram_tensor("x2_stash", [16, 128, 1024], F32).ap()
    fr_st = nc.dram_tensor("fr_stash", [16, 128, 1024], F32).ap()
    f_st = nc.dram_tensor("f_stash", [16, 128, 1024], F32).ap()

    taps = {}
    for name, shape in debug_taps:
        taps[name] = nc.dram_tensor("dbg_" + name, list(shape), F32,
                                    kind="ExternalOutput").ap()

    BASEA = 16512
    LIMIT = 229344

    def at(name, shape, dt, off):
        per = int(np.prod(shape[1:])) * mybir.dt.size(dt)
        assert BASEA <= off and off + per <= LIMIT, f"sbuf {name} [{off},{off + per})"
        return nc.alloc_sbuf_tensor_at(name, list(shape), dt, offset=off).ap()

    ones1x128 = at("s_o1x", (1, 128), F32, BASEA)
    ones128bf = at("s_o128b", (128, 1), BF16, BASEA + 512)
    cconsts_sb = at("s_cconsts", (128, 6), F32, BASEA + 544)
    for _i, _v in enumerate([EPS, INV_D, -INV_D, -0.5,
                             float(np.float32(1.0) + np.float32(1e-5)),
                             float(np.float32(1e-5) * np.float32(1e-5))]):
        nc.const_aps.register(F32, _v, cconsts_sb[:, _i:_i + 1])
    P_END = BASEA + 1024

    RW0 = P_END
    mask_sb = at("s_mask", (1, TP), BF16, RW0)
    r1own = at("s_r1own", (1, 1024), F32, RW0 + 5120)
    nm1r1bf = at("s_nm1r1b", (1, 1024), BF16, RW0 + 9216)
    comb_bc = at("s_combbc", (128, 1024), F32, RW0)
    gs_t = [at(f"s_gs{i}", (128, 512), BF16, RW0 + 4096 + 1024 * i) for i in range(2)]
    us_t = [at(f"s_us{i}", (128, 512), BF16, RW0 + 6144 + 1024 * i) for i in range(2)]
    RW_END = RW0 + 12 * 1024

    WB0 = RW_END
    gbuf = [at(f"s_gbuf{i}", (128, 2048), BF16, WB0 + 4096 * i) for i in range(2)]
    ubuf = [at(f"s_ubuf{i}", (128, 2048), BF16, WB0 + 8192 + 4096 * i) for i in range(2)]
    xsq_bf = [at(f"s_xsq{i}", (128, 512), BF16, WB0 + 1024 * i) for i in range(2)]
    nmu_bf = [at(f"s_nmub{i}", (1, 512), BF16, WB0 + 2048 + 1024 * i) for i in range(2)]
    du_bf = at("s_dub", (96, 512), BF16, WB0 + 4096)
    m96k = [at(f"s_m96k{i}", (96, 512), F32, WB0 + 5120 + 2048 * i) for i in range(2)]
    rdbuf = [at(f"s_rdbuf{i}", (128, 1024), BF16, WB0 + 2048 * i) for i in range(2)]
    frbuf = [at(f"s_frbuf{i}", (128, 1024), F32, WB0 + 4096 * (1 + i)) for i in range(2)]
    eprows = at("s_eprows", (1, 8 * 512), F32, WB0 + 12288)
    WB_END = WB0 + 16 * 1024

    B1 = WB_END
    o1 = [B1]

    def atb1(name, shape, dt):
        per = int(np.prod(shape[1:])) * mybir.dt.size(dt)
        off = (o1[0] + 31) & ~31
        o1[0] = off + per
        assert o1[0] <= B1 + 64 * 1024, f"B1 overflow {name}"
        return at(name, shape, dt, off)

    xp_all = atb1("s_xp", (96, 2624), F32)
    u_ch = [atb1(f"s_u{i}", (96, 512), F32) for i in range(2)]
    u_bf = atb1("s_ubf", (96, 512), BF16)
    dus_bf = atb1("s_dusb", (96, 512), BF16)
    dbc_bf = atb1("s_dbcb", (38, 512), BF16)
    dA_b = [atb1(f"s_dA{i}", (128, 512), F32) for i in range(2)]
    dBu_b = [atb1(f"s_dBu{i}", (128, 512), F32) for i in range(2)]
    xs_b = [atb1(f"s_xs{i}", (128, 512), F32) for i in range(2)]
    Bp_bc = [atb1(f"s_Bp{i}", (128, 512), F32) for i in range(2)]
    C_bc = [atb1(f"s_C{i}", (128, 512), F32) for i in range(2)]
    rr96_bc = atb1("s_rr96", (96, 512), F32)
    rowsA = atb1("s_rows", (1, 10 * 512), F32)
    bc_a = atb1("s_bca", (128, 512), F32)
    bc_b = atb1("s_bcb", (128, 512), F32)
    prod_b = dA_b
    inpT_f32 = at("s_inpTf", (128, 16 * 96), F32,
                  nc.lookup_mloc(dA_b[0].tensor).addr)
    x_half = at("s_xhalf", (128, 16 * 512), F32,
                nc.lookup_mloc(xp_all.tensor).addr)
    assert (nc.lookup_mloc(rowsA.tensor).addr
            >= nc.lookup_mloc(x_half.tensor).addr + 32 * 1024), "x_half overlaps rows"
    rh_tmp = at("s_rhtmp", (128, 1024), BF16, B1)
    r_h_sb = at("s_rh", (128, 8 * 1024), BF16, B1 + 2048)
    frstage = [at(f"s_frstg{i}", (128, 1024), F32, B1 + 18432 + 4096 * i) for i in range(2)]

    B2 = B1 + 64 * 1024
    h_all = at("s_h", (128, 64 * 1024), BF16, B1)
    o2 = [B2 + 4096]

    def atb2(name, shape, dt):
        per = int(np.prod(shape[1:])) * mybir.dt.size(dt)
        off = (o2[0] + 31) & ~31
        o2[0] = off + per
        assert o2[0] <= B2 + 64 * 1024, f"B2 overflow {name}"
        return at(name, shape, dt, off)

    ident = atb2("s_ident", (128, 128), F32)
    ones1x128bf = atb2("s_o1xb", (1, 128), BF16)
    routerTbf = atb2("s_routerTb", (128, 128), BF16)
    xprojTbf = atb2("s_xprojTb", (96, 38), BF16)
    dtprojTbf = atb2("s_dtprojTb", (6, 96), BF16)
    lnw_sb = atb2("s_lnw", (128, 16), F32)
    lnb_sb = atb2("s_lnb", (128, 16), F32)
    dtb_sb = atb2("s_dtb", (96, 1), F32)
    D_sb = atb2("s_D", (96, 1), F32)
    convw_sb = atb2("s_convw", (96, 3), F32)
    convb_sb = atb2("s_convb", (96, 1), F32)
    nconvb_sb = atb2("s_nconvb", (96, 1), F32)
    c0_sb = atb2("s_c0", (96, 1), F32)
    Acols_sb = atb2("s_Acols", (128, 12), F32)
    A2 = atb2("s_A2", (128, 12), F32)
    state_cols = atb2("s_state", (128, 12), F32)
    wpsum_bf = atb2("s_wpsum", (1, 96), BF16)
    sel8_sb = atb2("s_sel8", (8, 8 * 128), F32)
    cmb8 = atb2("s_cmb8", (8, 128), F32)
    outTbf = atb2("s_outTb", (96, 2048), BF16)
    inpTbf = atb2("s_inpTb", (128, 16 * 96), BF16)
    selDbf = atb2("s_selDb", (96, 12 * 128), BF16)
    selBbf = atb2("s_selBb", (38, 128), BF16)
    selCbf = atb2("s_selCb", (38, 128), BF16)
    selR_sb = atb2("s_selR", (128, 12 * 96), F32)
    y_f = atb2("s_yf", (96, 1024), F32)
    y_bf = atb2("s_ybf", (96, 1024), BF16)
    lgt_sb = atb2("s_lgt", (8, 1024), F32)
    lgt_tm = atb2("s_lgttm", (128, 64), F32)
    rt_m1 = atb2("s_rtm1", (128, 8), F32)
    rt_t = atb2("s_rtt", (128, 8), F32)
    rt_t2 = atb2("s_rtt2", (128, 8), F32)
    rows4 = atb2("s_rows4", (1, 4 * 512), F32)
    cast_bf = [atb2(f"s_castb{i}", (128, 512), BF16) for i in range(2)]
    cast_bf2 = [atb2(f"s_castb2{i}", (128, 512), BF16) for i in range(2)]
    x2t = [atb2(f"s_x2t{i}", (128, 512), F32) for i in range(3)]
    bc_b2 = atb2("s_bcb2", (128, 512), F32)
    bc_a2 = atb2("s_bca2", (128, 512), F32)
    comb_tm = atb2("s_combtm", (128, 8), F32)

    B3 = B2 + 64 * 1024
    assert B3 + 32 * 1024 <= LIMIT, f"arena end {B3 + 32 * 1024} > {LIMIT}"
    xcb = [at(f"s_xcb{i}", (128, 16 * 512), BF16, B3 + 16384 * i) for i in range(2)]
    mr_sb = at("s_mr", (128, 16 * 512), F32, B3)
    z4_sb = at("s_z4", (128, 16 * 1024), BF16, B3)
    dbuf = [at(f"s_dbuf{i}", (128, 4096), BF16, B3 + 8192 * i) for i in range(2)]
    f_stage = [at(f"s_fstg{i}", (128, 1024), F32, B3 + 16384 + 4096 * i) for i in range(2)]
    f_bf = at("s_fbf", (128, 1024), BF16, B3 + 24576)
    f2_bf = at("s_f2bf", (128, 1024), BF16, B3 + 26624)
    ep_x2 = [at(f"s_epx2{i}", (128, 1024), F32, B3 + 4096 * i) for i in range(3)]
    ep_f = [at(f"s_epf{i}", (128, 1024), F32, B3 + 12288 + 4096 * i) for i in range(3)]
    ep_o = [at(f"s_epo{i}", (128, 1024), F32, B3 + 24576 + 2048 * 0 + 4096 * i) for i in range(2)]

    pb = [nc.alloc_psum_tensor(f"pb{i}", [128, 512], F32).ap() for i in range(8)]

    es = contextlib.ExitStack()
    _SEMS = {n: es.enter_context(nc.semaphore(n)) for n in ALL_SEMS}
    S = Sched()

    def row(i):
        return rowsA[:, 512 * i:512 * (i + 1)]

    def eprow(i):
        return eprows[:, 512 * i:512 * (i + 1)]

    r_pb = [Res() for _ in range(8)]
    r_xcb = [Res(), Res()]
    r_xsq = [Res(), Res()]
    r_nmu = [Res(), Res()]
    r_rows = [Res() for _ in range(10)]
    r_rows4 = [Res() for _ in range(4)]
    r_eprows = [Res() for _ in range(8)]
    r_r1own = Res()
    r_nm1r1 = Res()
    r_xp = [Res() for _ in range(NCHUNK)]
    r_u = [Res(), Res()]
    r_ubf = Res()
    r_dus = Res()
    r_dbc = Res()
    r_dub = Res()
    r_m96 = [Res(), Res()]
    r_rr96 = Res()
    r_bca = Res()
    r_bcb = Res()
    r_bcb2 = Res()
    r_bca2 = Res()
    r_dA = [Res(), Res()]
    r_dBu = [Res(), Res()]
    r_xs = [Res(), Res()]
    r_Bp = [Res(), Res()]
    r_C = [Res(), Res()]
    r_state = [Res() for _ in range(NJ)]
    r_yf = Res()
    r_ybf = Res()
    r_xhalf = Res()
    r_mr = Res()
    r_x2t = [Res(), Res(), Res()]
    r_x2st = [Res() for _ in range(KC)]
    r_cast = [Res(), Res()]
    r_cast2 = [Res(), Res()]
    r_zs = Res()
    r_z4 = Res()
    r_lgt = Res()
    r_lgttm = Res()
    r_rt = Res()
    r_cmb8 = Res()
    r_combtm = Res()
    r_combbc = Res()
    r_gbuf = [Res(), Res()]
    r_ubuf = [Res(), Res()]
    r_gs = [Res(), Res()]
    r_us = [Res(), Res()]
    r_rhtmp = Res()
    r_rh = [Res() for _ in range(NRI)]
    r_frstage = [Res(), Res()]
    r_frst = [Res() for _ in range(KC)]
    r_h = [Res() for _ in range(NI)]
    r_dbuf = [Res(), Res()]
    r_rdbuf = [Res(), Res()]
    r_frbuf = [Res(), Res()]
    r_fstage = [Res(), Res()]
    r_fst = [Res() for _ in range(KC)]
    r_fbf = Res()
    r_f2bf = Res()
    r_epx2 = [Res(), Res(), Res()]
    r_epf = [Res(), Res(), Res()]
    r_epo = [Res(), Res()]
    r_misc = Res()

    out_toks = []

    def tap(name, ap_src, waits, reads=(), q="sync"):
        if name in taps:
            out_toks.append(S.dma(q,
                                  lambda e, a=ap_src, d=taps[name]: e.dma_start(out=d, in_=a),
                                  reads=reads, waits=waits))

    S.phase = "prolog"
    const_toks = []
    for dst, src in [(ident, ident_h), (ones1x128, ones1x128_h),
                     (ones1x128bf, ones1x128bf_h), (ones128bf, ones128bf_h),
                     (lnw_sb, lnw_h), (lnb_sb, lnb_h), (dtb_sb, dtb_h),
                     (D_sb, Dcol_h), (convw_sb, convw_h), (convb_sb, convb_h),
                     (Acols_sb, Acols_h), (selR_sb, selR_h), (inpT_f32, inpT_h),
                     (mask_sb, xmask), (sel8_sb, sel8_h),
                     (cconsts_sb, cconsts_h)]:
        const_toks.append(S.dma("sync", lambda e, d=dst, s=src: e.dma_start(out=d, in_=s)))
    for dst, src in [(outTbf, outT_h), (xprojTbf, xprojT_h), (dtprojTbf, dtprojT_h),
                     (selDbf, selD_h), (selBbf, selB_h), (selCbf, selC_h),
                     (routerTbf, routerT_h)]:
        const_toks.append(S.dma("gpsimd", lambda e, d=dst, s=src: e.dma_start(out=d, in_=s)))
    CW = const_toks

    t_xpad = S.op("vector", lambda e: e.memset(xp_all[0:96, 0:1], 0.0))
    tk = S.op("scalar", lambda e: e.activation(A2[:], Acols_sb[:], AF.Exp), waits=CW)
    t_A2 = S.op("vector", lambda e: e.tensor_scalar_mul(A2[:], A2[:], -0.001), waits=[tk])
    t_nconvb = S.op("scalar", lambda e: e.activation(nconvb_sb[:], convb_sb[:], AF.Copy,
                                                     scale=-1.0), waits=CW)

    def mm_c0(e):
        for k in range(KC):
            i = e.matmul(pb[5][0:96, 0:1], inpT_f32[:, 96 * k:96 * (k + 1)],
                         lnb_sb[:, k:k + 1], start=(k == 0), stop=(k == KC - 1))
        return i
    tk = S.op("tensor", mm_c0, writes=[r_pb[5]], waits=CW)
    t_c0 = S.op("scalar", lambda e: e.activation(c0_sb[:], pb[5][0:96, 0:1], AF.Copy),
                reads=[r_pb[5]], waits=[tk])

    def mk_inpbf(e):
        for k in range(KC):
            i = e.activation(inpTbf[:, 96 * k:96 * (k + 1)], inpT_f32[:, 96 * k:96 * (k + 1)],
                             AF.Copy, scale=lnw_sb[:, k:k + 1])
        return i
    t_inpbf = S.op("scalar", mk_inpbf, waits=CW)
    for rr_ in (r_dA[0], r_dA[1], r_dBu[0]):
        rr_.w = t_inpbf

    def mm_wpsum(e):
        for k in range(KC):
            i = e.matmul(pb[6][0:1, 0:96], ones128bf[:], inpTbf[:, 96 * k:96 * (k + 1)],
                         start=(k == 0), stop=(k == KC - 1))
        return i
    tk = S.op("tensor", mm_wpsum, writes=[r_pb[6]], waits=[t_inpbf] + CW)
    t_wpsum = S.op("scalar", lambda e: e.activation(wpsum_bf[:], pb[6][0:1, 0:96], AF.Copy),
                   reads=[r_pb[6]], waits=[tk])

    S.phase = "mixer"
    x3 = xT.rearrange("(k p) t -> p k t", p=128)
    t_xbf = [None] * NCHUNK

    def load_chunk(c):
        cc = c % 2
        dst = xcb[cc].rearrange("p (k t) -> p k t", t=512)
        t_xbf[c] = S.dma("gpsimd",
                         lambda e, d=dst, c=c: e.dma_start(out=d,
                                                           in_=x3[:, :, 512 * c:512 * (c + 1)]),
                         writes=[r_xcb[cc]])

    load_chunk(0)
    load_chunk(1)
    R_MU, R_M2, R_V, R_T1, R_R1, R_RR, R_NMU, R_T2 = range(8)
    tok_xpc = [None] * NCHUNK

    for c in range(NCHUNK):
        cc = c % 2
        S.phase = "mstat"
        def mm_sx(e, cc=cc):
            for k in range(KC):
                i = e.matmul(pb[0][0:1, :], ones128bf[:], xcb[cc][:, 512 * k:512 * (k + 1)],
                             start=(k == 0), stop=(k == KC - 1))
            return i
        t_sx = S.op("tensor", mm_sx, reads=[r_xcb[cc]], writes=[r_pb[0]],
                    waits=[t_xbf[c]] + CW)
        for k in range(KC):
            kk = k % 2
            tsq = S.op("scalar", lambda e, k=k, kk=kk, cc=cc: e.activation(
                           xsq_bf[kk][:], xcb[cc][:, 512 * k:512 * (k + 1)], AF.Square),
                       reads=[r_xcb[cc]], writes=[r_xsq[kk]], waits=[t_xbf[c]])
            tmm = S.op("tensor", lambda e, k=k, kk=kk: e.matmul(
                           pb[1][0:1, :], ones128bf[:], xsq_bf[kk][:],
                           start=(k == 0), stop=(k == KC - 1)),
                       reads=[r_xsq[kk]], writes=[r_pb[1]] if k == 0 else [], waits=[tsq])
            if k > 0:
                r_pb[1].w = tmm
        t_sx2 = tmm

        tA = S.op("scalar", lambda e: e.activation(row(R_MU), pb[0][0:1, :], AF.Copy,
                                                   scale=INV_D),
                  reads=[r_pb[0]], writes=[r_rows[R_MU]], waits=[t_sx])
        tB = S.op("scalar", lambda e: e.activation(row(R_M2), pb[1][0:1, :], AF.Copy,
                                                   scale=INV_D),
                  reads=[r_pb[1]], writes=[r_rows[R_M2]], waits=[t_sx2])
        tL = S.op("scalar", lambda e: e.activation(row(R_NMU), pb[0][0:1, :], AF.Copy,
                                                   scale=-INV_D),
                  reads=[r_pb[0]], writes=[r_rows[R_NMU]], waits=[t_sx])
        tC = S.op("vector", lambda e: e.tensor_mul(row(R_T1), row(R_MU), row(R_MU)),
                  reads=[r_rows[R_MU]], writes=[r_rows[R_T1]], waits=[tA])
        tD = S.op("vector", lambda e: e.tensor_sub(row(R_V), row(R_M2), row(R_T1)),
                  reads=[r_rows[R_M2], r_rows[R_T1]], writes=[r_rows[R_V]], waits=[tB])
        tE = S.op("scalar", lambda e: e.activation(row(R_T1), row(R_V), AF.Ln,
                                                   scale=float(np.float32(1.0) + np.float32(1e-5)),
                                                   bias=float(np.float32(1e-5) * np.float32(1e-5))),
                  reads=[r_rows[R_V]], writes=[r_rows[R_T1]], waits=[tD])
        tK = S.op("scalar", lambda e: e.activation(row(R_RR), row(R_T1), AF.Exp, scale=-0.5),
                  reads=[r_rows[R_T1]], writes=[r_rows[R_RR]], waits=[tE])
        tM = S.op("scalar", lambda e, cc=cc: e.activation(nmu_bf[cc][:], row(R_NMU), AF.Copy),
                  reads=[r_rows[R_NMU]], writes=[r_nmu[cc]], waits=[tL])
        if c in (2, 3):
            off = 512 * (c - 2)
            tE2 = S.op("scalar", lambda e: e.activation(row(R_T2), row(R_V), AF.Ln, bias=EPS),
                       reads=[r_rows[R_V]], writes=[r_rows[R_T2]], waits=[tD])
            tF = S.op("scalar", lambda e, off=off: e.activation(
                          r1own[:, off:off + 512], row(R_T2), AF.Exp, scale=-0.5),
                      reads=[r_rows[R_T2]], writes=[r_r1own] if c == 3 else [], waits=[tE2])
            S.op("vector", lambda e, off=off: e.scalar_tensor_tensor(
                     nm1r1bf[:, off:off + 512], row(R_MU), -1.0, r1own[:, off:off + 512],
                     ALU.mult, ALU.mult),
                 reads=[r_rows[R_MU], r_r1own], writes=[r_nm1r1] if c == 3 else [],
                 waits=[tF])

        def mm_ip(e, cc=cc):
            for k in range(KC):
                e.matmul(pb[2][0:96, :], inpTbf[:, 96 * k:96 * (k + 1)],
                         xcb[cc][:, 512 * k:512 * (k + 1)], start=(k == 0), stop=False)
            return e.matmul(pb[2][0:96, :], wpsum_bf[:], nmu_bf[cc][:], start=False, stop=True)
        t_ip = S.op("tensor", mm_ip, reads=[r_xcb[cc], r_nmu[cc]], writes=[r_pb[2]],
                    waits=[t_xbf[c], t_inpbf, t_wpsum, tM])
        if c + 2 < NCHUNK:
            load_chunk(c + 2)

        t_rr = S.op("tensor", lambda e: e.matmul(pb[3][0:96, :], ones1x128[0:1, 0:96],
                                                 row(R_RR), start=True, stop=True),
                    reads=[r_rows[R_RR]], writes=[r_pb[3]], waits=[tK] + CW)
        t_rre = S.op("scalar", lambda e: e.activation(rr96_bc[:], pb[3][0:96, :], AF.Copy),
                     reads=[r_pb[3]], writes=[r_rr96], waits=[t_rr])
        t_mk = S.op("tensor", lambda e, c=c: e.matmul(pb[4][0:96, :], ones1x128bf[0:1, 0:96],
                                                      mask_sb[:, 512 * c:512 * (c + 1)],
                                                      start=True, stop=True),
                    writes=[r_pb[4]], waits=CW)
        t_mke = S.op("scalar", lambda e, cc=cc: e.activation(m96k[cc][:], pb[4][0:96, :],
                                                             AF.Copy),
                     reads=[r_pb[4]], writes=[r_m96[cc]], waits=[t_mk])
        if c == 2:
            tap("mu2", rowsA[:, 512 * R_MU:512 * (R_MU + 1)], [tA],
                reads=[r_rows[R_MU]])
            tap("rr2", rr96_bc[0:96, :], [t_rre], reads=[r_rr96])

        xpc = xp_all[0:96, 1 + 512 * c:1 + 512 * (c + 1)]
        t1_ = S.op("vector", lambda e, xpc=xpc: e.tensor_mul(xpc, pb[2][0:96, :], rr96_bc[:]),
                   reads=[r_pb[2], r_rr96], writes=[r_xp[c]], waits=[t_ip, t_rre])
        if c == 2:
            tap("xpraw2", xp_all[0:96, 1 + 512 * c:1 + 512 * (c + 1)], [t1_],
                reads=[r_xp[c]])
        t2_ = S.op("vector", lambda e, xpc=xpc, cc=cc: e.tensor_mul(xpc, xpc, m96k[cc][:]),
                   reads=[r_m96[cc]], writes=[r_xp[c]], waits=[t1_, t_mke])
        t3_ = S.op("vector", lambda e, xpc=xpc, cc=cc: e.scalar_tensor_tensor(
                       xpc, m96k[cc][:], c0_sb[:], xpc, ALU.mult, ALU.add),
                   reads=[r_m96[cc]], writes=[r_xp[c]], waits=[t2_, t_c0])
        tok_xpc[c] = t3_
        tap(f"xp{c}", xp_all[0:96, 1 + 512 * c:1 + 512 * (c + 1)], [t3_],
            reads=[r_xp[c]])

        S.phase = "postxp"
        pc = c - 1
        if pc < 0:
            continue
        pcc = pc % 2
        upre = u_ch[pcc][:, :]
        x_l = xp_all[0:96, 512 * pc + 0:512 * pc + 512]
        x_m = xp_all[0:96, 512 * pc + 1:512 * pc + 513]
        x_r = xp_all[0:96, 512 * pc + 2:512 * pc + 514]
        ta = S.op("vector", lambda e, x_l=x_l, upre=upre: e.tensor_scalar(
                      upre, x_l, convw_sb[:, 0:1], None, ALU.mult),
                  reads=[r_xp[pc]], writes=[r_u[pcc]], waits=[tok_xpc[pc], t_xpad] + CW)
        tb = S.op("vector", lambda e, x_m=x_m, upre=upre: e.scalar_tensor_tensor(
                      upre, x_m, convw_sb[:, 1:2], upre, ALU.mult, ALU.add),
                  reads=[r_xp[pc]], writes=[r_u[pcc]], waits=[ta])
        tc_ = S.op("vector", lambda e, x_r=x_r, upre=upre: e.scalar_tensor_tensor(
                       upre, x_r, convw_sb[:, 2:3], upre, ALU.mult, ALU.add),
                   reads=[r_xp[pc], r_xp[c]], writes=[r_u[pcc]], waits=[tb, tok_xpc[c]])
        td = S.op("scalar", lambda e, upre=upre: e.activation(
                      bc_a[0:96, :], upre, AF.Exp, scale=-1.0, bias=nconvb_sb[:]),
                  reads=[r_u[pcc]], writes=[r_bca], waits=[tc_, t_nconvb])
        te = S.op("vector", lambda e: e.tensor_scalar_add(bc_a[0:96, :], bc_a[0:96, :], 1.0),
                  writes=[r_bca], waits=[td])
        tf_ = S.op("vector", lambda e: e.reciprocal(bc_b[0:96, :], bc_a[0:96, :]),
                   reads=[r_bca], writes=[r_bcb], waits=[te])
        th = S.op("vector", lambda e, upre=upre: e.scalar_tensor_tensor(
                      upre, upre, convb_sb[:], bc_b[0:96, :], ALU.add, ALU.mult),
                  reads=[r_bcb], writes=[r_u[pcc]], waits=[tf_, td])
        ti = S.op("scalar", lambda e, upre=upre: e.activation(u_bf[:], upre, AF.Copy),
                  reads=[r_u[pcc]], writes=[r_ubf], waits=[th])
        tap(f"u{pc}", u_ch[pcc][0:96, :], [th], reads=[r_u[pcc]])

        tj = S.op("tensor", lambda e: e.matmul(pb[5][0:38, :], xprojTbf[:], u_bf[:],
                                               start=True, stop=True),
                  reads=[r_ubf], writes=[r_pb[5]], waits=[ti])
        tk_ = S.op("scalar", lambda e: e.activation(dbc_bf[:], pb[5][0:38, :], AF.Copy),
                   reads=[r_pb[5]], writes=[r_dbc], waits=[tj])
        tl = S.op("tensor", lambda e: e.matmul(pb[5][0:96, :], dtprojTbf[:], dbc_bf[0:6, :],
                                               start=True, stop=True),
                  reads=[r_dbc], writes=[r_pb[5]], waits=[tk_])
        tm_ = S.op("scalar", lambda e: e.activation(bc_a[0:96, :], pb[5][0:96, :], AF.Abs,
                                                    bias=dtb_sb[:]),
                   reads=[r_pb[5]], writes=[r_bca], waits=[tl])
        tn = S.op("scalar", lambda e: e.activation(bc_a[0:96, :], bc_a[0:96, :], AF.Exp,
                                                   scale=-1.0),
                  writes=[r_bca], waits=[tm_])
        to_ = S.op("scalar", lambda e: e.activation(bc_a[0:96, :], bc_a[0:96, :], AF.Ln,
                                                    bias=1.0),
                   writes=[r_bca], waits=[tn])
        tp_ = S.op("scalar", lambda e: e.activation(bc_b[0:96, :], pb[5][0:96, :], AF.Relu,
                                                    bias=dtb_sb[:]),
                   reads=[r_pb[5]], writes=[r_bcb], waits=[tm_])
        tq = S.op("vector", lambda e: e.tensor_add(dus_bf[:], bc_b[0:96, :], bc_a[0:96, :]),
                  reads=[r_bca, r_bcb], writes=[r_dus], waits=[to_, tp_])
        tr_ = S.op("vector", lambda e, pcc=pcc: e.scalar_tensor_tensor(
                       du_bf[:], dus_bf[:], 0.001, u_ch[pcc][:, :], ALU.mult, ALU.mult),
                   reads=[r_dus, r_u[pcc]], writes=[r_dub], waits=[tq])
        ts_ = S.op("vector", lambda e, pcc=pcc: e.tensor_mul(du_bf[:], du_bf[:], m96k[pcc][:]),
                   reads=[r_m96[pcc]], writes=[r_dub], waits=[tr_])

        tt = S.op("tensor", lambda e: e.matmul(pb[6][:, :], selBbf[:], dbc_bf[:],
                                               start=True, stop=True),
                  reads=[r_dbc], writes=[r_pb[6]], waits=[tk_] + CW)
        tu_ = S.op("scalar", lambda e, pcc=pcc: e.activation(Bp_bc[pcc][:], pb[6][:, :],
                                                             AF.Copy),
                   reads=[r_pb[6]], writes=[r_Bp[pcc]], waits=[tt])
        if pc >= 2:
            tv = S.op("tensor", lambda e: e.matmul(pb[6][:, :], selCbf[:], dbc_bf[:],
                                                   start=True, stop=True),
                      reads=[r_dbc], writes=[r_pb[6]], waits=[tu_])
            tw_ = S.op("scalar", lambda e, pcc=pcc: e.activation(C_bc[pcc][:], pb[6][:, :],
                                                                 AF.Copy),
                       reads=[r_pb[6]], writes=[r_C[pcc]], waits=[tv])

        y_last = None
        S.phase = "jloop"
        for j in range(NJ):
            jb = j % 2
            bs = 3 if jb == 0 else 6
            bs2 = 4 if jb == 0 else 7
            tj1 = S.op("tensor", lambda e, j=j, bs=bs: e.matmul(
                           pb[bs][:, :], selDbf[:, 128 * j:128 * (j + 1)], dus_bf[:],
                           start=True, stop=True),
                       reads=[r_dus], writes=[r_pb[bs]], waits=[tq])
            tj2 = S.op("scalar", lambda e, j=j, jb=jb, bs=bs: e.activation(
                           dA_b[jb][:], pb[bs][:, :], AF.Exp, scale=A2[:, j:j + 1]),
                       reads=[r_pb[bs]], writes=[r_dA[jb]], waits=[tj1, t_A2])
            tj3 = S.op("tensor", lambda e, j=j, bs2=bs2: e.matmul(
                           pb[bs2][:, :], selDbf[:, 128 * j:128 * (j + 1)], du_bf[:],
                           start=True, stop=True),
                       reads=[r_dub], writes=[r_pb[bs2]], waits=[ts_])
            tj4 = S.op("vector", lambda e, jb=jb, pcc=pcc, bs2=bs2: e.tensor_mul(
                           dBu_b[jb][:], pb[bs2][:, :], Bp_bc[pcc][:]),
                       reads=[r_pb[bs2], r_Bp[pcc]], writes=[r_dBu[jb]], waits=[tj3, tu_])
            init = 0.0 if pc == 0 else state_cols[:, j:j + 1]
            tj5 = S.op("vector", lambda e, jb=jb, init=init: e.tensor_tensor_scan(
                           xs_b[jb][:], dA_b[jb][:], dBu_b[jb][:], init, ALU.mult, ALU.add),
                       reads=[r_dA[jb], r_dBu[jb]] + ([r_state[j]] if pc > 0 else []),
                       writes=[r_xs[jb]], waits=[tj2, tj4])
            tj6 = S.op("vector", lambda e, j=j, jb=jb: e.tensor_copy(
                           state_cols[:, j:j + 1], xs_b[jb][:, 511:512]),
                       reads=[r_xs[jb]], writes=[r_state[j]], waits=[tj5])
            if pc >= 2:
                tj7 = S.op("vector", lambda e, jb=jb, pcc=pcc: e.tensor_mul(
                               prod_b[jb][:], xs_b[jb][:], C_bc[pcc][:]),
                           reads=[r_xs[jb], r_C[pcc]], writes=[r_dA[jb]], waits=[tw_, tj6])
                tj8 = S.op("tensor", lambda e, j=j, jb=jb: e.matmul(
                               pb[2][0:96, :], selR_sb[:, 96 * j:96 * (j + 1)], prod_b[jb][:],
                               start=(j == 0), stop=(j == NJ - 1)),
                           reads=[r_dA[jb]], writes=[r_pb[2]] if j == 0 else [],
                           waits=[tj7] + CW)
                if j > 0:
                    r_pb[2].w = tj8
                y_last = tj8

        if pc >= 2:
            off = 512 * (pc - 2)
            ty1 = S.op("vector", lambda e, off=off, pcc=pcc: e.scalar_tensor_tensor(
                           y_f[:, off:off + 512], u_ch[pcc][:, :], D_sb[:],
                           pb[2][0:96, :], ALU.mult, ALU.add),
                       reads=[r_pb[2], r_u[pcc]], writes=[r_yf], waits=[y_last] + CW)
            S.op("scalar", lambda e, off=off: e.activation(
                     y_bf[:, off:off + 512], y_f[:, off:off + 512], AF.Copy),
                 reads=[r_yf], writes=[r_ybf], waits=[ty1])
    tap("yf", y_f[0:96, :], [r_ybf.w], reads=[r_yf])

    S.phase = "O"
    S.fence()

    def stats_tile(src_f32, first, last, wtok, par=0):
        cb = cast_bf if par == 0 else cast_bf2
        rc = r_cast if par == 0 else r_cast2
        tca = S.op("scalar", lambda e, s=src_f32, cb=cb: e.activation(cb[0][:], s, AF.Copy),
                   writes=[rc[0]], waits=wtok)
        tcb = S.op("scalar", lambda e, s=src_f32, cb=cb: e.activation(cb[1][:], s, AF.Square),
                   writes=[rc[1]], waits=wtok)
        ts1 = S.op("tensor", lambda e, first=first, last=last, cb=cb: e.matmul(
                       pb[0][0:1, :], ones128bf[:], cb[0][:], start=first, stop=last),
                   reads=[rc[0]], writes=[r_pb[0]] if first else [], waits=[tca] + CW)
        if not first:
            r_pb[0].w = ts1
        ts2 = S.op("tensor", lambda e, first=first, last=last, cb=cb: e.matmul(
                       pb[1][0:1, :], ones128bf[:], cb[1][:], start=first, stop=last),
                   reads=[rc[1]], writes=[r_pb[1]] if first else [], waits=[tcb] + CW)
        if not first:
            r_pb[1].w = ts2
        return ts1, ts2

    def rstd_rows(dst_r_res, dst_r, dst_nm_res, dst_nm, rowset):
        mu, m2, v, t1 = rowset
        tA = S.op("scalar", lambda e: e.activation(mu, pb[0][0:1, :], AF.Copy, scale=INV_D),
                  reads=[r_pb[0]], writes=[r_rows[R_MU]])
        tB = S.op("scalar", lambda e: e.activation(m2, pb[1][0:1, :], AF.Copy, scale=INV_D),
                  reads=[r_pb[1]], writes=[r_rows[R_M2]])
        tC = S.op("vector", lambda e: e.tensor_mul(t1, mu, mu),
                  reads=[r_rows[R_MU]], writes=[r_rows[R_T1]], waits=[tA])
        tD = S.op("vector", lambda e: e.tensor_sub(v, m2, t1),
                  reads=[r_rows[R_M2], r_rows[R_T1]], writes=[r_rows[R_V]], waits=[tB, tC])
        tE = S.op("scalar", lambda e: e.activation(t1, v, AF.Ln, bias=EPS),
                  reads=[r_rows[R_V]], writes=[r_rows[R_T1]], waits=[tD])
        tF = S.op("scalar", lambda e: e.activation(dst_r, t1, AF.Exp, scale=-0.5),
                  reads=[r_rows[R_T1]], writes=[dst_r_res], waits=[tE])
        tG = S.op("vector", lambda e: e.scalar_tensor_tensor(dst_nm, mu, -1.0, dst_r,
                                                             ALU.mult, ALU.mult),
                  reads=[r_rows[R_MU], dst_r_res], writes=[dst_nm_res], waits=[tF])
        return tF, tG

    def bcast(dstbuf, dst_res, src_row, src_res, wtoks, bank, npart=128):
        tb1 = S.op("tensor", lambda e: e.matmul(pb[bank][0:npart, :], ones1x128[0:1, 0:npart],
                                                src_row, start=True, stop=True),
                   reads=[src_res], writes=[r_pb[bank]], waits=wtoks + CW)
        tb2 = S.op("scalar", lambda e: e.activation(dstbuf[0:npart, :], pb[bank][0:npart, :],
                                                    AF.Copy),
                   reads=[r_pb[bank]], writes=[dst_res], waits=[tb1])
        return tb2

    xh3 = x_half.rearrange("p (k t) -> p k t", t=512)
    ln4_toks = []

    for hc in range(2):
        off = 512 * hc
        t_xh = S.dma("sync", lambda e, off=off: e.dma_start(
                         out=xh3[:, :, :], in_=x3[:, :, 1024 + off:1024 + off + 512]),
                     writes=[r_xhalf])
        t_r1b = bcast(bc_a, r_bca, r1own[:, off:off + 512], r_r1own, [], 3)
        for mc in range(KC):
            bank = 2 if mc % 2 == 0 else 5

            def mm_op1(e, mc=mc, off=off, bank=bank):
                e.matmul(pb[bank][:, :], outTbf[:, 128 * mc:128 * (mc + 1)],
                         y_bf[:, off:off + 512], start=True, stop=False)
                return e.matmul(pb[bank][:, :], ones1x128bf[:], nm1r1bf[:, off:off + 512],
                                start=False, stop=True)
            t_op = S.op("tensor", mm_op1, reads=[r_ybf, r_nm1r1], writes=[r_pb[bank]],
                        waits=CW)
            bcb_i = bc_b if mc % 2 == 0 else bc_b2
            r_bcb_i = r_bcb if mc % 2 == 0 else r_bcb2
            t_s = S.op("vector", lambda e, mc=mc, bcb_i=bcb_i: e.tensor_mul(
                           bcb_i[:, :], x_half[:, 512 * mc:512 * (mc + 1)], bc_a[:, :]),
                       reads=[r_xhalf, r_bca], writes=[r_bcb_i], waits=[t_xh, t_r1b])
            t_mr = S.op("vector", lambda e, mc=mc, bank=bank, bcb_i=bcb_i: e.tensor_add(
                            mr_sb[:, 512 * mc:512 * (mc + 1)], pb[bank][:, :], bcb_i[:, :]),
                        reads=[r_pb[bank], r_bcb_i], writes=[r_mr] if mc == 0 else [],
                        waits=[t_op, t_s])
            if mc > 0:
                r_mr.w = t_mr
            stats_tile(mr_sb[:, 512 * mc:512 * (mc + 1)], mc == 0, mc == KC - 1, [t_mr],
                       par=mc % 2)
        tap(f"mr{hc}", mr_sb[:, :], [r_mr.w], reads=[r_mr])
        t_r3, t_nm3 = rstd_rows(r_rows[R_R1], row(R_R1), r_rows[R_T2], row(R_T2),
                                (row(R_MU), row(R_M2), row(R_V), row(R_T1)))
        t_r3b = bcast(bc_a, r_bca, row(R_R1), r_rows[R_R1], [t_r3], 3)
        t_nm3b = bcast(bc_b, r_bcb, row(R_T2), r_rows[R_T2], [t_nm3], 4)
        for k in range(KC):
            kk = k % 2
            tx1 = S.op("vector", lambda e, k=k: e.tensor_mul(
                           mr_sb[:, 512 * k:512 * (k + 1)], mr_sb[:, 512 * k:512 * (k + 1)],
                           bc_a[:, :]),
                       reads=[r_bca], writes=[r_mr], waits=[t_r3b])
            tx2 = S.op("gpsimd", lambda e, k=k: e.tensor_add(
                           mr_sb[:, 512 * k:512 * (k + 1)], mr_sb[:, 512 * k:512 * (k + 1)],
                           bc_b[:, :]),
                       reads=[r_bcb], writes=[r_mr], waits=[tx1, t_nm3b])
            tx3 = S.op("vector", lambda e, k=k, kk=kk: e.tensor_add(
                           x2t[kk][:], mr_sb[:, 512 * k:512 * (k + 1)],
                           x_half[:, 512 * k:512 * (k + 1)]),
                       reads=[r_mr, r_xhalf], writes=[r_x2t[kk]], waits=[tx2, t_xh])
            tst = S.dma("scalar", lambda e, k=k, kk=kk, off=off: e.dma_start(
                            out=x2_st[k, :, off:off + 512], in_=x2t[kk][:]),
                        reads=[r_x2t[kk]], waits=[tx3])
            r_x2st[k] = tst if hc == 0 else tst
            cb4 = cast_bf if k % 2 == 0 else cast_bf2
            rc4 = r_cast if k % 2 == 0 else r_cast2
            tca = S.op("scalar", lambda e, kk=kk, cb4=cb4: e.activation(
                           cb4[0][:], x2t[kk][:], AF.Copy),
                       reads=[r_x2t[kk]], writes=[rc4[0]], waits=[tx3])
            tcb = S.op("scalar", lambda e, kk=kk, cb4=cb4: e.activation(
                           cb4[1][:], x2t[kk][:], AF.Square),
                       reads=[r_x2t[kk]], writes=[rc4[1]], waits=[tx3])
            ts1 = S.op("tensor", lambda e, k=k, cb4=cb4: e.matmul(
                           pb[6][0:1, :], ones128bf[:], cb4[0][:],
                           start=(k == 0), stop=(k == KC - 1)),
                       reads=[rc4[0]], writes=[r_pb[6]] if k == 0 else [], waits=[tca] + CW)
            if k > 0:
                r_pb[6].w = ts1
            ts2 = S.op("tensor", lambda e, k=k, cb4=cb4: e.matmul(
                           pb[7][0:1, :], ones128bf[:], cb4[1][:],
                           start=(k == 0), stop=(k == KC - 1)),
                       reads=[rc4[1]], writes=[r_pb[7]] if k == 0 else [], waits=[tcb] + CW)
            if k > 0:
                r_pb[7].w = ts2
        def row4(i):
            return rows4[:, 512 * i:512 * (i + 1)]
        s_r4, s_nm4 = 2 * hc, 2 * hc + 1
        tA = S.op("scalar", lambda e: e.activation(row(R_MU), pb[6][0:1, :], AF.Copy,
                                                   scale=INV_D),
                  reads=[r_pb[6]], writes=[r_rows[R_MU]], waits=[ts1])
        tB = S.op("scalar", lambda e: e.activation(row(R_M2), pb[7][0:1, :], AF.Copy,
                                                   scale=INV_D),
                  reads=[r_pb[7]], writes=[r_rows[R_M2]], waits=[ts2])
        tC = S.op("vector", lambda e: e.tensor_mul(row(R_T1), row(R_MU), row(R_MU)),
                  reads=[r_rows[R_MU]], writes=[r_rows[R_T1]], waits=[tA])
        tD = S.op("vector", lambda e: e.tensor_sub(row(R_V), row(R_M2), row(R_T1)),
                  reads=[r_rows[R_M2], r_rows[R_T1]], writes=[r_rows[R_V]], waits=[tB, tC])
        tE = S.op("scalar", lambda e: e.activation(row(R_T1), row(R_V), AF.Ln, bias=EPS),
                  reads=[r_rows[R_V]], writes=[r_rows[R_T1]], waits=[tD])
        tF = S.op("scalar", lambda e, s_r4=s_r4, row4=row4: e.activation(
                      row4(s_r4), row(R_T1), AF.Exp, scale=-0.5),
                  reads=[r_rows[R_T1]], writes=[r_rows4[s_r4]], waits=[tE])
        tG = S.op("vector", lambda e, s_r4=s_r4, s_nm4=s_nm4, row4=row4:
                  e.scalar_tensor_tensor(
                      row4(s_nm4), row(R_MU), -1.0, row4(s_r4), ALU.mult, ALU.mult),
                  reads=[r_rows[R_MU], r_rows4[s_r4]], writes=[r_rows4[s_nm4]], waits=[tF])
        ln4_toks.append((tF, tG))

    S.fence()
    zb_a = [bc_a, bc_a2]
    zb_b = [bc_b, bc_b2]
    zr_a = [r_bca, r_bca2]
    zr_b = [r_bcb, r_bcb2]
    for hc in range(2):
        bcast(zb_a[hc], zr_a[hc], rows4[:, 512 * (2 * hc):512 * (2 * hc + 1)],
              r_rows4[2 * hc], [ln4_toks[hc][0]], 3)
        bcast(zb_b[hc], zr_b[hc], rows4[:, 512 * (2 * hc + 1):512 * (2 * hc + 2)],
              r_rows4[2 * hc + 1], [ln4_toks[hc][1]], 4)
    r_z4t = [None] * KC
    for k in range(KC):
        for hc in range(2):
            off = 512 * hc
            kk = (2 * k + hc) % 3
            trl = S.dma("scalar" if hc == 0 else "sync",
                        lambda e, k=k, kk=kk, off=off: e.dma_start(
                            out=x2t[kk][:], in_=x2_st[k, :, off:off + 512]),
                        writes=[r_x2t[kk]], waits=[r_x2st[k]])
            tz1 = S.op("vector", lambda e, kk=kk, hc=hc: e.tensor_mul(
                           x2t[kk][:], x2t[kk][:], zb_a[hc][:, :]),
                       reads=[zr_a[hc]], writes=[r_x2t[kk]], waits=[trl])
            tz2 = S.op("vector", lambda e, k=k, kk=kk, off=off, hc=hc: e.tensor_add(
                           z4_sb[:, 1024 * k + off:1024 * k + off + 512], x2t[kk][:],
                           zb_b[hc][:, :]),
                       reads=[r_x2t[kk], zr_b[hc]],
                       writes=[r_z4] if (hc == 0 and k == 0) else [],
                       waits=[tz1])
            if not (hc == 0 and k == 0):
                r_z4.w = tz2
        r_z4t[k] = tz2
    tap("x2st", x2_st[:, :, :].rearrange("a b c -> (a b) c"), list(r_x2st))
    tap("z4t", z4_sb[:, :], [r_z4.w], reads=[r_z4], q="gpsimd")

    S.phase = "router"
    snapO = S.snapshot()
    S.fence_from(snapO)
    t_lge = [None, None]
    for hc in range(2):
        off = 512 * hc

        for k in range(KC):
            t_lg = S.op("tensor", lambda e, k=k, off=off: e.matmul(
                            pb[5][0:8, :], routerTbf[:, 8 * k:8 * (k + 1)],
                            z4_sb[:, 1024 * k + off:1024 * k + off + 512],
                            start=(k == 0), stop=(k == KC - 1)),
                        writes=[r_pb[5]] if k == 0 else [],
                        waits=CW + [r_z4t[k]])
            if k > 0:
                r_pb[5].w = t_lg
        t_lge[hc] = S.op("scalar", lambda e, off=off: e.activation(
                             lgt_sb[:, off:off + 512], pb[5][0:8, :], AF.Copy),
                         reads=[r_pb[5]], writes=[r_lgt], waits=[t_lg])
    for tt in range(8):
        t_tr = S.op("tensor", lambda e, tt=tt: e.transpose(
                        pb[7][:, 0:8], lgt_sb[:, 128 * tt:128 * (tt + 1)], ident[0:8, 0:8]),
                    reads=[r_lgt], writes=[r_pb[7]], waits=t_lge + CW)
        S.op("scalar", lambda e, tt=tt: e.activation(lgt_tm[:, 8 * tt:8 * (tt + 1)],
                                                     pb[7][:, 0:8], AF.Copy),
             reads=[r_pb[7]], writes=[r_lgttm])
    for tt in range(8):
        lg8 = lgt_tm[:, 8 * tt:8 * (tt + 1)]
        tm1 = S.op("vector", lambda e, lg8=lg8, tt=tt: e.reduce_max(
                       rt_m1[:, tt:tt + 1], lg8, axis=mybir.AxisListType.X),
                   reads=[r_lgttm], writes=[r_rt], waits=[])
        tmsk = S.op("vector", lambda e, lg8=lg8, tt=tt: e.tensor_scalar(
                        rt_t[:], lg8, rt_m1[:, tt:tt + 1], None, ALU.is_equal),
                    reads=[r_lgttm], writes=[r_rt], waits=[tm1])
        tmv = S.op("vector", lambda e, lg8=lg8: e.scalar_tensor_tensor(
                       rt_t[:], rt_t[:], -1e30, lg8, ALU.mult, ALU.add),
                   reads=[r_lgttm], writes=[r_rt], waits=[tmsk])
        tm2 = S.op("vector", lambda e: e.reduce_max(rt_t2[:, 0:1], rt_t[:],
                                                    axis=mybir.AxisListType.X),
                   writes=[r_rt], waits=[tmv])
        tdl = S.op("vector", lambda e, tt=tt: e.tensor_sub(rt_t2[:, 1:2], rt_t2[:, 0:1],
                                                           rt_m1[:, tt:tt + 1]),
                   writes=[r_rt], waits=[tm2])
        tsg = S.op("scalar", lambda e: e.activation(rt_t2[:, 2:3], rt_t2[:, 1:2], AF.Exp),
                   reads=[r_rt], writes=[r_rt], waits=[tdl])
        tT_ = S.op("vector", lambda e: e.tensor_scalar_add(rt_t2[:, 3:4], rt_t2[:, 2:3], 1.0),
                   writes=[r_rt], waits=[tsg])
        tin = S.op("vector", lambda e: e.reciprocal(rt_t2[:, 4:5], rt_t2[:, 3:4]),
                   writes=[r_rt], waits=[tT_])
        tcm = S.op("vector", lambda e, tt=tt: e.scalar_tensor_tensor(
                       comb_tm[:, tt:tt + 1], rt_t2[:, 2:3], rt_t2[:, 4:5], rt_t2[:, 4:5],
                       ALU.mult, ALU.add),
                   reads=[r_rt], writes=[r_combtm], waits=[tin])
    t_ct = S.op("tensor", lambda e: e.transpose(pb[7][0:8, 0:128], comb_tm[:], ident[:]),
                reads=[r_combtm], writes=[r_pb[7]], waits=CW)
    t_ce = S.op("scalar", lambda e: e.activation(cmb8[:], pb[7][0:8, 0:128], AF.Copy),
                reads=[r_pb[7]], writes=[r_cmb8], waits=[t_ct])
    for half in range(2):
        def mm_cb(e, half=half):
            for q in range(4):
                tt = 4 * half + q
                i = e.matmul(pb[3][:, 128 * q:128 * (q + 1)],
                             sel8_sb[:, 128 * tt:128 * (tt + 1)], cmb8[:],
                             start=True, stop=True)
            return i
        t_cb = S.op("tensor", mm_cb, reads=[r_cmb8], writes=[r_pb[3]], waits=[t_ce] + CW)
        S.op("scalar", lambda e, half=half: e.activation(
                 comb_bc[:, 512 * half:512 * (half + 1)], pb[3][:, :], AF.Copy),
             reads=[r_pb[3]], writes=[r_combbc])

    tap("combt", comb_bc[:, :], [r_combbc.w], reads=[r_combbc])
    tap("lgtt", lgt_tm[:, :], [r_lgttm.w], reads=[r_lgttm])

    S.phase = "routed"
    S.fence_from(snapO)
    t_wrg = [None] * NRI
    t_wru = [None] * NRI

    def sched_rdma(ir):
        ib = ir % 2
        t_wrg[ir] = S.dma("gpsimd", lambda e, ir=ir, ib=ib: e.dma_start(
                              out=gbuf[ib][:], in_=Wrg_h[ir]), writes=[r_gbuf[ib]])
        t_wru[ir] = S.dma("gpsimd", lambda e, ir=ir, ib=ib: e.dma_start(
                              out=ubuf[ib][:], in_=Wru_h[ir]), writes=[r_ubuf[ib]])

    sched_rdma(0)
    sched_rdma(1)
    for ir in range(NRI):
        ib = ir % 2
        for half in range(2):
            bg = (0 if ib == 0 else 4) + half
            bu = (2 if ib == 0 else 6) + half
            hof = 512 * half

            if ir < 2:
                for k in range(KC):
                    t_g = S.op("tensor", lambda e, ib=ib, hof=hof, bg=bg, k=k: e.matmul(
                                   pb[bg][:, :], gbuf[ib][:, 128 * k:128 * (k + 1)],
                                   z4_sb[:, 1024 * k + hof:1024 * k + hof + 512],
                                   start=(k == 0), stop=(k == KC - 1)),
                               reads=[r_gbuf[ib]] if k == 0 else [],
                               writes=[r_pb[bg]] if k == 0 else [],
                               waits=[t_wrg[ir], r_z4t[k]])
                    if k > 0:
                        r_pb[bg].w = t_g
                if True:
                    r_gbuf[ib].r.append(t_g)
                for k in range(KC):
                    t_u = S.op("tensor", lambda e, ib=ib, hof=hof, bu=bu, k=k: e.matmul(
                                   pb[bu][:, :], ubuf[ib][:, 128 * k:128 * (k + 1)],
                                   z4_sb[:, 1024 * k + hof:1024 * k + hof + 512],
                                   start=(k == 0), stop=(k == KC - 1)),
                               reads=[r_ubuf[ib]] if k == 0 else [],
                               writes=[r_pb[bu]] if k == 0 else [],
                               waits=[t_wru[ir], r_z4t[k]])
                    if k > 0:
                        r_pb[bu].w = t_u
                if True:
                    r_ubuf[ib].r.append(t_u)
            else:
                def mm_rg(e, ib=ib, hof=hof, bg=bg):
                    for k in range(KC):
                        i = e.matmul(pb[bg][:, :], gbuf[ib][:, 128 * k:128 * (k + 1)],
                                     z4_sb[:, 1024 * k + hof:1024 * k + hof + 512],
                                     start=(k == 0), stop=(k == KC - 1))
                    return i
                t_g = S.op("tensor", mm_rg, reads=[r_gbuf[ib], r_z4], writes=[r_pb[bg]],
                           waits=[t_wrg[ir]])

                def mm_ru(e, ib=ib, hof=hof, bu=bu):
                    for k in range(KC):
                        i = e.matmul(pb[bu][:, :], ubuf[ib][:, 128 * k:128 * (k + 1)],
                                     z4_sb[:, 1024 * k + hof:1024 * k + hof + 512],
                                     start=(k == 0), stop=(k == KC - 1))
                    return i
                t_u = S.op("tensor", mm_ru, reads=[r_ubuf[ib], r_z4], writes=[r_pb[bu]],
                           waits=[t_wru[ir]])
            tsl = S.op("scalar", lambda e, half=half, bg=bg: e.activation(
                           gs_t[half][:], pb[bg][:, :], AF.Silu),
                       reads=[r_pb[bg]], writes=[r_gs[half]], waits=[t_g])
            tuc = S.op("scalar", lambda e, half=half, bu=bu: e.activation(
                           us_t[half][:], pb[bu][:, :], AF.Copy),
                       reads=[r_pb[bu]], writes=[r_us[half]], waits=[t_u])
            tmul = S.op("gpsimd", lambda e, half=half, hof=hof: e.tensor_mul(
                            rh_tmp[:, hof:hof + 512], gs_t[half][:], us_t[half][:]),
                        reads=[r_gs[half], r_us[half]], writes=[r_rhtmp], waits=[tsl, tuc])
            S.op("vector", lambda e, ir=ir, hof=hof: e.tensor_mul(
                     r_h_sb[:, 1024 * ir + hof:1024 * ir + hof + 512],
                     rh_tmp[:, hof:hof + 512], comb_bc[:, hof:hof + 512]),
                 reads=[r_rhtmp, r_combbc], writes=[r_rh[ir]], waits=[tmul])
        if ir + 2 < NRI:
            sched_rdma(ir + 2)
    tap("rht", r_h_sb[:, :], [r_rh[NRI - 1].w], reads=list(r_rh), q="gpsimd")
    S.fence_from(snapO)
    for dt in range(KC):
        db = dt % 2
        t_wrd = S.dma("gpsimd", lambda e, dt=dt, db=db: e.dma_start(
                          out=rdbuf[db][:], in_=Wrd_h[dt]), writes=[r_rdbuf[db]])
        for half in range(2):
            bank = 2 * db + half
            hof = 512 * half

            def mm_rd(e, db=db, hof=hof, bank=bank):
                for ikr in range(NRI):
                    i = e.matmul(pb[bank][:, :], rdbuf[db][:, 128 * ikr:128 * (ikr + 1)],
                                 r_h_sb[:, 1024 * ikr + hof:1024 * ikr + hof + 512],
                                 start=(ikr == 0), stop=(ikr == NRI - 1))
                return i
            t_rd = S.op("tensor", mm_rd,
                        reads=[r_rdbuf[db]] + [r_rh[i_] for i_ in range(NRI)],
                        writes=[r_pb[bank]], waits=[t_wrd])
            t_ev = S.op("scalar", lambda e, db=db, hof=hof, bank=bank: e.activation(
                            frstage[db][:, hof:hof + 512], pb[bank][:, :], AF.Copy),
                        reads=[r_pb[bank]], writes=[r_frstage[db]] if half == 0 else [],
                        waits=[t_rd])
            if half == 1:
                r_frstage[db].w = t_ev
        r_frst[dt] = S.dma("scalar", lambda e, dt=dt, db=db: e.dma_start(
                               out=fr_st[dt], in_=frstage[db][:]),
                           reads=[r_frstage[db]], waits=[])

    tap("frst", fr_st[:, :, :].rearrange("a b c -> (a b) c"), list(r_frst))

    S.phase = "gu"
    S.fence()
    t_stash_done = r_x2st[KC - 1]
    t_wg = [None] * NI
    t_wu = [None] * NI

    def sched_wdma(i):
        ib = i % 2
        t_wg[i] = S.dma("gpsimd", lambda e, i=i, ib=ib: e.dma_start(
                            out=gbuf[ib][:], in_=Wg_h[i]), writes=[r_gbuf[ib]])
        t_wu[i] = S.dma("gpsimd", lambda e, i=i, ib=ib: e.dma_start(
                            out=ubuf[ib][:], in_=Wu_h[i]), writes=[r_ubuf[ib]])

    sched_wdma(0)
    sched_wdma(1)
    for i in range(NI):
        ib = i % 2
        for half in range(2):
            bg = (0 if ib == 0 else 4) + half
            bu = (2 if ib == 0 else 6) + half
            hof = 512 * half

            def mm_g(e, ib=ib, hof=hof, bg=bg):
                for k in range(KC):
                    i2 = e.matmul(pb[bg][:, :], gbuf[ib][:, 128 * k:128 * (k + 1)],
                                  z4_sb[:, 1024 * k + hof:1024 * k + hof + 512],
                                  start=(k == 0), stop=(k == KC - 1))
                return i2
            t_g = S.op("tensor", mm_g, reads=[r_gbuf[ib], r_z4], writes=[r_pb[bg]],
                       waits=[t_wg[i]])

            def mm_u(e, ib=ib, hof=hof, bu=bu):
                for k in range(KC):
                    i2 = e.matmul(pb[bu][:, :], ubuf[ib][:, 128 * k:128 * (k + 1)],
                                  z4_sb[:, 1024 * k + hof:1024 * k + hof + 512],
                                  start=(k == 0), stop=(k == KC - 1))
                return i2
            t_u = S.op("tensor", mm_u, reads=[r_ubuf[ib], r_z4], writes=[r_pb[bu]],
                       waits=[t_wu[i]])
            tsl = S.op("scalar", lambda e, half=half, bg=bg: e.activation(
                           gs_t[half][:], pb[bg][:, :], AF.Silu),
                       reads=[r_pb[bg]], writes=[r_gs[half]], waits=[t_g])
            tuc = S.op("scalar", lambda e, half=half, bu=bu: e.activation(
                           us_t[half][:], pb[bu][:, :], AF.Copy),
                       reads=[r_pb[bu]], writes=[r_us[half]], waits=[t_u])
            S.op("gpsimd", lambda e, i=i, hof=hof, half=half: e.tensor_mul(
                     h_all[:, 1024 * i + hof:1024 * i + hof + 512],
                     gs_t[half][:], us_t[half][:]),
                 reads=[r_gs[half], r_us[half]], writes=[r_h[i]], waits=[tsl, tuc])
        if i + 2 < NI:
            sched_wdma(i + 2)

    S.phase = "down"
    S.fence()
    t_wd = [[None, None] for _ in range(KC)]
    t_fr = [None] * KC

    def sched_ddma(dt, hh):
        t_wd[dt][hh] = S.dma("gpsimd", lambda e, dt=dt, hh=hh: e.dma_start(
                                 out=dbuf[hh][:], in_=Wd_h[dt, :, 4096 * hh:4096 * (hh + 1)]),
                             writes=[r_dbuf[hh]])

    def sched_fdma(dt):
        db = dt % 2
        t_fr[dt] = S.dma("scalar", lambda e, dt=dt, db=db: e.dma_start(
                             out=frbuf[db][:], in_=fr_st[dt]),
                         writes=[r_frbuf[db]], waits=[r_frst[dt]])

    sched_ddma(0, 0)
    sched_ddma(0, 1)
    sched_fdma(0)
    sched_fdma(1)
    for dt in range(KC):
        db = dt % 2
        b0, b1 = 2 * db, 2 * db + 1

        def mm_dA(e, b0=b0, b1=b1):
            for ik in range(32):
                e.matmul(pb[b0][:, :], dbuf[0][:, 128 * ik:128 * (ik + 1)],
                         h_all[:, 1024 * ik:1024 * ik + 512],
                         start=(ik == 0), stop=False)
                i2 = e.matmul(pb[b1][:, :], dbuf[0][:, 128 * ik:128 * (ik + 1)],
                              h_all[:, 1024 * ik + 512:1024 * (ik + 1)],
                              start=(ik == 0), stop=False)
            return i2
        t_dA = S.op("tensor", mm_dA,
                    reads=[r_dbuf[0]] + [r_h[i_] for i_ in range(32)],
                    writes=[r_pb[b0], r_pb[b1]], waits=[t_wd[dt][0]])
        if dt + 1 < KC:
            sched_ddma(dt + 1, 0)

        def mm_dB(e, b0=b0, b1=b1):
            for ik in range(32, NI):
                e.matmul(pb[b0][:, :], dbuf[1][:, 128 * (ik - 32):128 * (ik - 31)],
                         h_all[:, 1024 * ik:1024 * ik + 512],
                         start=False, stop=(ik == NI - 1))
                i2 = e.matmul(pb[b1][:, :], dbuf[1][:, 128 * (ik - 32):128 * (ik - 31)],
                              h_all[:, 1024 * ik + 512:1024 * (ik + 1)],
                              start=False, stop=(ik == NI - 1))
            return i2
        t_dB = S.op("tensor", mm_dB,
                    reads=[r_dbuf[1]] + [r_h[i_] for i_ in range(32, NI)],
                    writes=[], waits=[t_wd[dt][1], t_dA])
        r_pb[b0].w = t_dB
        r_pb[b1].w = t_dB
        if dt + 1 < KC:
            sched_ddma(dt + 1, 1)
        for half in range(2):
            bank = 2 * db + half
            hof = 512 * half
            t_ev = S.op("vector", lambda e, db=db, hof=hof, bank=bank: e.tensor_add(
                            f_stage[db][:, hof:hof + 512], pb[bank][:, :],
                            frbuf[db][:, hof:hof + 512]),
                        reads=[r_pb[bank], r_frbuf[db]],
                        writes=[r_fstage[db]] if half == 0 else [],
                        waits=[t_dB, t_fr[dt]])
            if half == 1:
                r_fstage[db].w = t_ev
            tca = S.op("scalar", lambda e, db=db, hof=hof: e.activation(
                           f_bf[:, hof:hof + 512], f_stage[db][:, hof:hof + 512], AF.Copy),
                       reads=[r_fstage[db]], writes=[r_fbf], waits=[t_ev])
            tcb = S.op("scalar", lambda e, db=db, hof=hof: e.activation(
                           f2_bf[:, hof:hof + 512], f_stage[db][:, hof:hof + 512], AF.Square),
                       reads=[r_fstage[db]], writes=[r_f2bf], waits=[t_ev])
            ts1 = S.op("tensor", lambda e, dt=dt, hof=hof, half=half: e.matmul(
                           pb[4 + half][0:1, :], ones128bf[:], f_bf[:, hof:hof + 512],
                           start=(dt == 0), stop=(dt == KC - 1)),
                       reads=[r_fbf], writes=[r_pb[4 + half]] if dt == 0 else [],
                       waits=[tca] + CW)
            if dt > 0:
                r_pb[4 + half].w = ts1
            ts2 = S.op("tensor", lambda e, dt=dt, hof=hof, half=half: e.matmul(
                           pb[6 + half][0:1, :], ones128bf[:], f2_bf[:, hof:hof + 512],
                           start=(dt == 0), stop=(dt == KC - 1)),
                       reads=[r_f2bf], writes=[r_pb[6 + half]] if dt == 0 else [],
                       waits=[tcb] + CW)
            if dt > 0:
                r_pb[6 + half].w = ts2
        r_fst[dt] = S.dma("scalar", lambda e, dt=dt, db=db: e.dma_start(
                              out=f_st[dt], in_=f_stage[db][:]),
                          reads=[r_fstage[db]], waits=[])
        if dt + 2 < KC:
            sched_fdma(dt + 2)

    S.phase = "ep"
    S.fence()
    ep_r5 = [eprow(0), eprow(1)]
    ep_nm5 = [eprow(2), eprow(3)]
    for half in range(2):
        tA = S.op("scalar", lambda e, half=half: e.activation(
                      eprow(4), pb[4 + half][0:1, :], AF.Copy, scale=INV_D),
                  reads=[r_pb[4 + half]], writes=[r_eprows[4]])
        tB = S.op("scalar", lambda e, half=half: e.activation(
                      eprow(5), pb[6 + half][0:1, :], AF.Copy, scale=INV_D),
                  reads=[r_pb[6 + half]], writes=[r_eprows[5]])
        tC = S.op("vector", lambda e: e.tensor_mul(eprow(6), eprow(4), eprow(4)),
                  reads=[r_eprows[4]], writes=[r_eprows[6]], waits=[tA])
        tD = S.op("vector", lambda e: e.tensor_sub(eprow(7), eprow(5), eprow(6)),
                  reads=[r_eprows[5], r_eprows[6]], writes=[r_eprows[7]], waits=[tB, tC])
        tE = S.op("scalar", lambda e: e.activation(eprow(6), eprow(7), AF.Ln, bias=EPS),
                  reads=[r_eprows[7]], writes=[r_eprows[6]], waits=[tD])
        tF = S.op("scalar", lambda e, half=half: e.activation(
                      ep_r5[half], eprow(6), AF.Exp, scale=-0.5),
                  reads=[r_eprows[6]], writes=[r_eprows[half]], waits=[tE])
        S.op("vector", lambda e, half=half: e.scalar_tensor_tensor(
                 ep_nm5[half], eprow(4), -1.0, ep_r5[half], ALU.mult, ALU.mult),
             reads=[r_eprows[4], r_eprows[half]], writes=[r_eprows[2 + half]], waits=[tF])
    r5_bc = comb_bc
    nm5_bc = frbuf[0]
    for half in range(2):
        hof = 512 * half
        tb1 = S.op("tensor", lambda e, half=half: e.matmul(
                       pb[0][:, :], ones1x128[:], ep_r5[half], start=True, stop=True),
                   reads=[r_eprows[half]], writes=[r_pb[0]], waits=CW)
        S.op("scalar", lambda e, hof=hof: e.activation(r5_bc[:, hof:hof + 512],
                                                       pb[0][:, :], AF.Copy),
             reads=[r_pb[0]], writes=[r_combbc], waits=[tb1])
        tb2 = S.op("tensor", lambda e, half=half: e.matmul(
                       pb[1][:, :], ones1x128[:], ep_nm5[half], start=True, stop=True),
                   reads=[r_eprows[2 + half]], writes=[r_pb[1]], waits=CW)
        S.op("scalar", lambda e, hof=hof: e.activation(nm5_bc[:, hof:hof + 512],
                                                       pb[1][:, :], AF.Copy),
             reads=[r_pb[1]], writes=[r_frbuf[0]], waits=[tb2])
    for k in range(KC):
        kk = k % 3
        ko = k % 2
        t_lx = S.dma("scalar", lambda e, k=k, kk=kk: e.dma_start(
                         out=ep_x2[kk][:], in_=x2_st[k]), writes=[r_epx2[kk]],
                     waits=[r_x2st[k]])
        t_lf = S.dma("sync", lambda e, k=k, kk=kk: e.dma_start(
                         out=ep_f[kk][:], in_=f_st[k]), writes=[r_epf[kk]],
                     waits=[r_fst[k]])
        t_e1 = S.op("vector", lambda e, kk=kk: e.tensor_mul(
                        ep_f[kk][:], ep_f[kk][:], r5_bc[:, :]),
                    reads=[r_combbc], writes=[r_epf[kk]], waits=[t_lf])
        t_e2 = S.op("gpsimd", lambda e, kk=kk: e.tensor_add(
                        ep_f[kk][:], ep_f[kk][:], nm5_bc[:, :]),
                    reads=[r_frbuf[0]], writes=[r_epf[kk]], waits=[t_e1])
        t_e3 = S.op("vector", lambda e, kk=kk, ko=ko: e.tensor_add(
                        ep_o[ko][:], ep_f[kk][:], ep_x2[kk][:]),
                    reads=[r_epf[kk], r_epx2[kk]], writes=[r_epo[ko]], waits=[t_e2, t_lx])
        out_toks.append(S.dma("sync", lambda e, k=k, ko=ko: e.dma_start(
                                  out=out_dev[128 * k:128 * (k + 1), :], in_=ep_o[ko][:]),
                              reads=[r_epo[ko]], waits=[t_e3]))

    final_waits = {}
    for (s, v, _) in out_toks:
        final_waits[s] = max(final_waits.get(s, 0), v)
    fw = list(final_waits.items())

    def final_op(e, fw=fw):
        for s, v in fw:
            e.wait_ge(_SEMS[s], v)
        return e.nop()
    S.cnt["s_sy"] = S.cnt.get("s_sy", 0) + 1
    S.ops.append(("sync", lambda e: final_op(e).then_inc(_SEMS["s_sy"], 1)))

    with nc.Block() as block:
        @block.sync
        def _(e):
            for eng, fn in S.ops:
                if eng == "sync":
                    fn(e)

        @block.gpsimd
        def _(e):
            for eng, fn in S.ops:
                if eng == "gpsimd":
                    fn(e)

        @block.tensor
        def _(e):
            for eng, fn in S.ops:
                if eng == "tensor":
                    fn(e)

        @block.vector
        def _(e):
            for eng, fn in S.ops:
                if eng == "vector":
                    fn(e)

        @block.scalar
        def _(e):
            for eng, fn in S.ops:
                if eng == "scalar":
                    fn(e)

    es.close()
    nc._dbg_sched = S
    return nc


_PROG_CACHE = {}


def kernel(**inputs):
    key = tuple(sorted(n for n, _ in _DEBUG_TAPS))
    if key not in _PROG_CACHE:
        _PROG_CACHE[key] = build_program(list(_DEBUG_TAPS))
    nc = _PROG_CACHE[key]
    g = _host_shared(inputs)
    x = inputs["L0_attn_feat_patch"]
    in_maps = []
    for c in range(8):
        m = dict(g)
        m.update(_host_percore(x, c))
        in_maps.append(m)
    res = run_bass_kernel_spmd(nc, in_maps, core_ids=list(range(8)))
    out = np.empty((4, 2048, 2048), np.float32)
    for c in range(8):
        b, h = c // 2, c % 2
        out[b, 1024 * h:1024 * (h + 1), :] = res.results[c]["out"].T
    if _DEBUG_TAPS:
        kernel._dbg = [{k: v for k, v in r.items() if k.startswith("dbg_")}
                       for r in res.results]
    return out


# version 16
# speedup vs baseline: 1.0415x; 1.0415x over previous
import contextlib
import numpy as np
import ml_dtypes
import concourse.bass as bass
import concourse.mybir as mybir
from concourse.bass_utils import run_bass_kernel_spmd

F32 = mybir.dt.float32
BF16 = mybir.dt.bfloat16
AF = mybir.ActivationFunctionType
ALU = mybir.AluOpType
BF = ml_dtypes.bfloat16

KC = 16
TP = 2560
NCHUNK = 5
NJ = 12
NI = 64
NRI = 8
EPS = 1e-5
INV_D = 1.0 / 2048.0

_DEBUG_TAPS = []

ENG_SEM = {"tensor": "s_pe", "scalar": "s_act", "vector": "s_dve",
           "gpsimd": "s_pool", "sync": "s_sy"}
NDS = 4
DMA_SEMS = {q: [f"d_{q[:2]}{i}" for i in range(NDS)] for q in ("sync", "gpsimd", "scalar")}
ALL_SEMS = sorted(set(list(ENG_SEM.values())
                      + [s for v in DMA_SEMS.values() for s in v]))


class Res:
    __slots__ = ("w", "r")

    def __init__(self):
        self.w = None
        self.r = []


class Sched:
    def __init__(self):
        self.ops = []
        self.cnt = {}
        self.pending = {}
        self.dma_idx = {q: 0 for q in DMA_SEMS}
        self.phase = "init"
        self.dbg_map = {e: [] for e in ENG_SEM}

    def _mk(self, eng, emit, waits, inc_sem, inc_by):
        if eng in self.pending:
            waits = list(waits) + self.pending.pop(eng)
        self.cnt[inc_sem] = self.cnt.get(inc_sem, 0) + inc_by
        tok = (inc_sem, self.cnt[inc_sem], eng)
        best = {}
        for wt in waits:
            if wt is None:
                continue
            s, v, peng = wt
            if best.get(s, -1) < v:
                best[s] = v
        wl = sorted(best.items())

        ph = self.phase

        def run(e, emit=emit, wl=wl, inc=(inc_sem, inc_by), ph=ph, eng=eng):
            n0 = e.bass.n_instructions() if hasattr(e, "bass") else None
            for s, v in wl:
                e.wait_ge(_SEMS[s], v)
            inst = emit(e)
            inst.then_inc(_SEMS[inc[0]], inc[1])
            n1 = e.bass.n_instructions() if n0 is not None else None
            self.dbg_map[eng].append((n1 - n0 if n0 is not None else 1 + len(wl), ph))
        self.ops.append((eng, run))
        return tok

    def _deps(self, reads, writes, waits):
        ws = list(waits)
        for r in reads:
            if r.w is not None:
                ws.append(r.w)
        for r in writes:
            if r.w is not None:
                ws.append(r.w)
            ws.extend(r.r)
        return ws

    def _note(self, tok, reads, writes):
        for r in reads:
            r.r.append(tok)
        for r in writes:
            r.w = tok
            r.r = []

    def op(self, eng, emit, reads=(), writes=(), waits=()):
        tok = self._mk(eng, emit, self._deps(reads, writes, waits), ENG_SEM[eng], 1)
        self._note(tok, reads, writes)
        return tok

    def dma(self, eng, emit, reads=(), writes=(), waits=()):
        sem = DMA_SEMS[eng][self.dma_idx[eng] % NDS]
        self.dma_idx[eng] += 1
        ws = self._deps(reads, writes, waits)
        prior = self.cnt.get(sem, 0)
        if prior > 0:
            ws = list(ws) + [(sem, prior, "_dma")]
        tok = self._mk(eng, emit, ws, sem, 16)
        self._note(tok, reads, writes)
        return tok

    def snapshot(self):
        return [(s, self.cnt.get(s, 0), "_f") for s in ALL_SEMS if self.cnt.get(s, 0) > 0]

    def fence(self):
        snap = self.snapshot()
        for e in ENG_SEM:
            self.pending[e] = list(snap)

    def fence_from(self, snap):
        for e in ENG_SEM:
            self.pending[e] = list(snap)


_SEMS = None


def _host_shared(inp):
    g = {}
    in_proj_W = inp["L0_attn_in_proj_W"]
    A_log = inp["L0_attn_A_log"]
    pj = np.arange(128)
    g["lnw_cols"] = np.ascontiguousarray(inp["L0_attn_ln_w"].reshape(16, 128).T)
    g["lnb_cols"] = np.ascontiguousarray(inp["L0_attn_ln_b"].reshape(16, 128).T)
    g["inpT"] = np.ascontiguousarray(
        in_proj_W.reshape(96, 16, 128).transpose(2, 1, 0).reshape(128, 16 * 96))
    g["outT"] = np.ascontiguousarray(inp["L0_attn_out_proj_W"].T)
    g["xprojT"] = np.ascontiguousarray(inp["L0_attn_x_proj_W"].T)
    g["dtprojT"] = np.ascontiguousarray(inp["L0_attn_dt_proj_W"].T)
    g["dtb_col"] = np.ascontiguousarray(inp["L0_attn_dt_proj_b"].reshape(96, 1))
    g["D_col"] = np.ascontiguousarray(inp["L0_attn_D_param"].reshape(96, 1))
    g["convw"] = np.ascontiguousarray(inp["L0_attn_conv_w"].reshape(96, 3))
    g["convb_col"] = np.ascontiguousarray(inp["L0_attn_conv_b"].reshape(96, 1))
    acols = np.empty((128, 12), np.float32)
    for j in range(12):
        acols[:, j] = A_log[8 * j + pj // 16, pj % 16]
    g["Acols"] = acols
    selD = np.zeros((96, 12 * 128), np.float32)
    for j in range(12):
        selD[8 * j + pj // 16, 128 * j + pj] = 1.0
    g["selD"] = selD
    selB = np.zeros((38, 128), np.float32)
    selC = np.zeros((38, 128), np.float32)
    selB[6 + pj % 16, pj] = 1.0
    selC[22 + pj % 16, pj] = 1.0
    g["selB"] = selB
    g["selC"] = selC
    selR = np.zeros((128, 12 * 96), np.float32)
    for j in range(12):
        selR[pj, 96 * j + (8 * j + pj // 16)] = 1.0
    g["selR"] = selR
    sel8 = np.zeros((8, 8 * 128), np.float32)
    for tt in range(8):
        sel8[tt, 128 * tt:128 * (tt + 1)] = 1.0
    g["sel8"] = sel8
    g["ident"] = np.eye(128, dtype=np.float32)
    g["cconsts"] = np.repeat(np.array([[1e-5, 1.0 / 2048.0, -1.0 / 2048.0, -0.5,
                                        np.float32(1.0) + np.float32(1e-5),
                                        np.float32(1e-5) * np.float32(1e-5)]],
                                      np.float32), 128, axis=0)
    g["ones1x128"] = np.ones((1, 128), np.float32)
    g["ones1x128bf"] = np.ones((1, 128), BF)
    g["ones128bf"] = np.ones((128, 1), BF)
    rw = inp["L0_ffn_router_weight"]
    g["routerT"] = np.ascontiguousarray(
        rw.reshape(8, 16, 128).transpose(2, 1, 0).reshape(128, 128))
    g["Wg"] = np.ascontiguousarray(
        inp["L0_ffn_sh_gate"].reshape(64, 128, 16, 128).transpose(0, 3, 2, 1).reshape(64, 128, 2048))
    g["Wu"] = np.ascontiguousarray(
        inp["L0_ffn_sh_up"].reshape(64, 128, 16, 128).transpose(0, 3, 2, 1).reshape(64, 128, 2048))
    g["Wd"] = np.ascontiguousarray(
        inp["L0_ffn_sh_down"].reshape(16, 128, 64, 128).transpose(0, 3, 2, 1).reshape(16, 128, 8192))
    g["Wrg"] = np.ascontiguousarray(
        inp["L0_ffn_r_gate"].reshape(8, 128, 16, 128).transpose(0, 3, 2, 1).reshape(8, 128, 2048))
    g["Wru"] = np.ascontiguousarray(
        inp["L0_ffn_r_up"].reshape(8, 128, 16, 128).transpose(0, 3, 2, 1).reshape(8, 128, 2048))
    g["Wrd"] = np.ascontiguousarray(
        inp["L0_ffn_r_down"].reshape(16, 128, 8, 128).transpose(0, 3, 2, 1).reshape(16, 128, 1024))
    return g


def _host_percore(x, c):
    b, h = c // 2, c % 2
    xt = x[b].T
    ctx = np.zeros((2048, TP), np.float32)
    mask = np.zeros((1, TP), np.float32)
    if h == 0:
        ctx[:, 1024:2048] = xt[:, 0:1024]
        ctx[:, 2048] = xt[:, 1024]
        mask[0, 1024:2049] = 1.0
    else:
        ctx[:, 0:2048] = xt[:, 0:2048]
        mask[0, 0:2048] = 1.0
    return {"xT": np.ascontiguousarray(ctx),
            "xmask": np.ascontiguousarray(mask.astype(BF))}


def build_program(debug_taps=()):
    global _SEMS
    nc = bass.Bass("TRN2", target_bir_lowering=False, debug=False)

    def din(name, shape, dt=F32):
        return nc.dram_tensor(name, list(shape), dt, kind="ExternalInput").ap()

    xT = din("xT", (2048, TP))
    xmask = din("xmask", (1, TP), BF16)
    lnw_h = din("lnw_cols", (128, 16))
    lnb_h = din("lnb_cols", (128, 16))
    inpT_h = din("inpT", (128, 16 * 96))
    outT_h = din("outT", (96, 2048))
    xprojT_h = din("xprojT", (96, 38))
    dtprojT_h = din("dtprojT", (6, 96))
    dtb_h = din("dtb_col", (96, 1))
    Dcol_h = din("D_col", (96, 1))
    convw_h = din("convw", (96, 3))
    convb_h = din("convb_col", (96, 1))
    Acols_h = din("Acols", (128, 12))
    selD_h = din("selD", (96, 12 * 128))
    selB_h = din("selB", (38, 128))
    selC_h = din("selC", (38, 128))
    selR_h = din("selR", (128, 12 * 96))
    sel8_h = din("sel8", (8, 8 * 128))
    ident_h = din("ident", (128, 128))
    cconsts_h = din("cconsts", (128, 6))
    ones1x128_h = din("ones1x128", (1, 128))
    ones1x128bf_h = din("ones1x128bf", (1, 128), BF16)
    ones128bf_h = din("ones128bf", (128, 1), BF16)
    routerT_h = din("routerT", (128, 128))
    Wg_h = din("Wg", (64, 128, 2048))
    Wu_h = din("Wu", (64, 128, 2048))
    Wd_h = din("Wd", (16, 128, 8192))
    Wrg_h = din("Wrg", (8, 128, 2048))
    Wru_h = din("Wru", (8, 128, 2048))
    Wrd_h = din("Wrd", (16, 128, 1024))

    out_dev = nc.dram_tensor("out", [2048, 1024], F32, kind="ExternalOutput").ap()
    x2_st = nc.dram_tensor("x2_stash", [16, 128, 1024], F32).ap()
    fr_st = nc.dram_tensor("fr_stash", [16, 128, 1024], F32).ap()
    f_st = nc.dram_tensor("f_stash", [16, 128, 1024], F32).ap()

    taps = {}
    for name, shape in debug_taps:
        taps[name] = nc.dram_tensor("dbg_" + name, list(shape), F32,
                                    kind="ExternalOutput").ap()

    BASEA = 16512
    LIMIT = 229344

    def at(name, shape, dt, off):
        per = int(np.prod(shape[1:])) * mybir.dt.size(dt)
        assert BASEA <= off and off + per <= LIMIT, f"sbuf {name} [{off},{off + per})"
        return nc.alloc_sbuf_tensor_at(name, list(shape), dt, offset=off).ap()

    ones1x128 = at("s_o1x", (1, 128), F32, BASEA)
    ones128bf = at("s_o128b", (128, 1), BF16, BASEA + 512)
    cconsts_sb = at("s_cconsts", (128, 6), F32, BASEA + 544)
    for _i, _v in enumerate([EPS, INV_D, -INV_D, -0.5,
                             float(np.float32(1.0) + np.float32(1e-5)),
                             float(np.float32(1e-5) * np.float32(1e-5))]):
        nc.const_aps.register(F32, _v, cconsts_sb[:, _i:_i + 1])
    P_END = BASEA + 1024

    RW0 = P_END
    mask_sb = at("s_mask", (1, TP), BF16, RW0)
    r1own = at("s_r1own", (1, 1024), F32, RW0 + 5120)
    nm1r1bf = at("s_nm1r1b", (1, 1024), BF16, RW0 + 9216)
    comb_bc = at("s_combbc", (128, 1024), F32, RW0)
    gs_t = [at(f"s_gs{i}", (128, 512), BF16, RW0 + 4096 + 1024 * i) for i in range(2)]
    us_t = [at(f"s_us{i}", (128, 512), BF16, RW0 + 6144 + 1024 * i) for i in range(2)]
    RW_END = RW0 + 12 * 1024

    WB0 = RW_END
    gbuf = [at(f"s_gbuf{i}", (128, 2048), BF16, WB0 + 4096 * i) for i in range(2)]
    ubuf = [at(f"s_ubuf{i}", (128, 2048), BF16, WB0 + 8192 + 4096 * i) for i in range(2)]
    xsq_bf = [at(f"s_xsq{i}", (128, 512), BF16, WB0 + 1024 * i) for i in range(2)]
    nmu_bf = [at(f"s_nmub{i}", (1, 512), BF16, WB0 + 2048 + 1024 * i) for i in range(2)]
    du_bf = at("s_dub", (96, 512), BF16, WB0 + 4096)
    m96k = [at(f"s_m96k{i}", (96, 512), F32, WB0 + 5120 + 2048 * i) for i in range(2)]
    rdbuf = [at(f"s_rdbuf{i}", (128, 1024), BF16, WB0 + 2048 * i) for i in range(2)]
    frbuf = [at(f"s_frbuf{i}", (128, 1024), F32, WB0 + 4096 * (1 + i)) for i in range(2)]
    eprows = at("s_eprows", (1, 8 * 512), F32, WB0 + 12288)
    WB_END = WB0 + 16 * 1024

    B1 = WB_END
    o1 = [B1]

    def atb1(name, shape, dt):
        per = int(np.prod(shape[1:])) * mybir.dt.size(dt)
        off = (o1[0] + 31) & ~31
        o1[0] = off + per
        assert o1[0] <= B1 + 64 * 1024, f"B1 overflow {name}"
        return at(name, shape, dt, off)

    xp_all = atb1("s_xp", (96, 2624), F32)
    u_ch = [atb1(f"s_u{i}", (96, 512), F32) for i in range(2)]
    u_bf = atb1("s_ubf", (96, 512), BF16)
    dus_bf = atb1("s_dusb", (96, 512), BF16)
    dbc_bf = atb1("s_dbcb", (38, 512), BF16)
    dA_b = [atb1(f"s_dA{i}", (128, 512), F32) for i in range(2)]
    dBu_b = [atb1(f"s_dBu{i}", (128, 512), F32) for i in range(2)]
    xs_b = [atb1(f"s_xs{i}", (128, 512), F32) for i in range(2)]
    Bp_bc = [atb1(f"s_Bp{i}", (128, 512), F32) for i in range(2)]
    C_bc = [atb1(f"s_C{i}", (128, 512), F32) for i in range(2)]
    rr96_bc = atb1("s_rr96", (96, 512), F32)
    rowsA = atb1("s_rows", (1, 10 * 512), F32)
    bc_a = atb1("s_bca", (128, 512), F32)
    bc_b = atb1("s_bcb", (128, 512), F32)
    prod_b = dA_b
    inpT_f32 = at("s_inpTf", (128, 16 * 96), F32,
                  nc.lookup_mloc(dA_b[0].tensor).addr)
    x_half = at("s_xhalf", (128, 16 * 512), F32,
                nc.lookup_mloc(xp_all.tensor).addr)
    assert (nc.lookup_mloc(rowsA.tensor).addr
            >= nc.lookup_mloc(x_half.tensor).addr + 32 * 1024), "x_half overlaps rows"
    rh_tmp = at("s_rhtmp", (128, 1024), BF16, B1)
    r_h_sb = at("s_rh", (128, 8 * 1024), BF16, B1 + 2048)
    frstage = [at(f"s_frstg{i}", (128, 1024), F32, B1 + 18432 + 4096 * i) for i in range(2)]

    B2 = B1 + 64 * 1024
    h_all = at("s_h", (128, 64 * 1024), BF16, B1)
    o2 = [B2 + 4096]

    def atb2(name, shape, dt):
        per = int(np.prod(shape[1:])) * mybir.dt.size(dt)
        off = (o2[0] + 31) & ~31
        o2[0] = off + per
        assert o2[0] <= B2 + 64 * 1024, f"B2 overflow {name}"
        return at(name, shape, dt, off)

    ident = atb2("s_ident", (128, 128), F32)
    ones1x128bf = atb2("s_o1xb", (1, 128), BF16)
    routerTbf = atb2("s_routerTb", (128, 128), BF16)
    xprojTbf = atb2("s_xprojTb", (96, 38), BF16)
    dtprojTbf = atb2("s_dtprojTb", (6, 96), BF16)
    lnw_sb = atb2("s_lnw", (128, 16), F32)
    lnb_sb = atb2("s_lnb", (128, 16), F32)
    dtb_sb = atb2("s_dtb", (96, 1), F32)
    D_sb = atb2("s_D", (96, 1), F32)
    convw_sb = atb2("s_convw", (96, 3), F32)
    convb_sb = atb2("s_convb", (96, 1), F32)
    nconvb_sb = atb2("s_nconvb", (96, 1), F32)
    c0_sb = atb2("s_c0", (96, 1), F32)
    Acols_sb = atb2("s_Acols", (128, 12), F32)
    A2 = atb2("s_A2", (128, 12), F32)
    state_cols = atb2("s_state", (128, 12), F32)
    wpsum_bf = atb2("s_wpsum", (1, 96), BF16)
    sel8_sb = atb2("s_sel8", (8, 8 * 128), F32)
    cmb8 = atb2("s_cmb8", (8, 128), F32)
    outTbf = atb2("s_outTb", (96, 2048), BF16)
    inpTbf = atb2("s_inpTb", (128, 16 * 96), BF16)
    selDbf = atb2("s_selDb", (96, 12 * 128), BF16)
    selBbf = atb2("s_selBb", (38, 128), BF16)
    selCbf = atb2("s_selCb", (38, 128), BF16)
    selR_sb = atb2("s_selR", (128, 12 * 96), F32)
    y_f = atb2("s_yf", (96, 1024), F32)
    y_bf = atb2("s_ybf", (96, 1024), BF16)
    lgt_sb = atb2("s_lgt", (8, 1024), F32)
    lgt_tm = atb2("s_lgttm", (128, 64), F32)
    rt_m1 = atb2("s_rtm1", (128, 8), F32)
    rt_t = atb2("s_rtt", (128, 8), F32)
    rt_t2 = atb2("s_rtt2", (128, 8), F32)
    rows4 = atb2("s_rows4", (1, 4 * 512), F32)
    cast_bf = [atb2(f"s_castb{i}", (128, 512), BF16) for i in range(2)]
    cast_bf2 = [atb2(f"s_castb2{i}", (128, 512), BF16) for i in range(2)]
    x2t = [atb2(f"s_x2t{i}", (128, 512), F32) for i in range(3)]
    bc_b2 = atb2("s_bcb2", (128, 512), F32)
    comb_tm = atb2("s_combtm", (128, 8), F32)

    B3 = B2 + 64 * 1024
    assert B3 + 32 * 1024 <= LIMIT, f"arena end {B3 + 32 * 1024} > {LIMIT}"
    xcb = [at(f"s_xcb{i}", (128, 16 * 512), BF16, B3 + 16384 * i) for i in range(2)]
    mr_sb = at("s_mr", (128, 16 * 512), F32, B3)
    z4_sb = at("s_z4", (128, 16 * 1024), BF16, B3)
    dbuf = [at(f"s_dbuf{i}", (128, 4096), BF16, B3 + 8192 * i) for i in range(2)]
    f_stage = [at(f"s_fstg{i}", (128, 1024), F32, B3 + 16384 + 4096 * i) for i in range(2)]
    f_bf = at("s_fbf", (128, 1024), BF16, B3 + 24576)
    f2_bf = at("s_f2bf", (128, 1024), BF16, B3 + 26624)
    ep_x2 = [at(f"s_epx2{i}", (128, 1024), F32, B3 + 4096 * i) for i in range(3)]
    ep_f = [at(f"s_epf{i}", (128, 1024), F32, B3 + 12288 + 4096 * i) for i in range(3)]
    ep_o = [at(f"s_epo{i}", (128, 1024), F32, B3 + 24576 + 2048 * 0 + 4096 * i) for i in range(2)]

    pb = [nc.alloc_psum_tensor(f"pb{i}", [128, 512], F32).ap() for i in range(8)]

    es = contextlib.ExitStack()
    _SEMS = {n: es.enter_context(nc.semaphore(n)) for n in ALL_SEMS}
    S = Sched()

    def row(i):
        return rowsA[:, 512 * i:512 * (i + 1)]

    def eprow(i):
        return eprows[:, 512 * i:512 * (i + 1)]

    r_pb = [Res() for _ in range(8)]
    r_xcb = [Res(), Res()]
    r_xsq = [Res(), Res()]
    r_nmu = [Res(), Res()]
    r_rows = [Res() for _ in range(10)]
    r_rows4 = [Res() for _ in range(4)]
    r_eprows = [Res() for _ in range(8)]
    r_r1own = Res()
    r_nm1r1 = Res()
    r_xp = [Res() for _ in range(NCHUNK)]
    r_u = [Res(), Res()]
    r_ubf = Res()
    r_dus = Res()
    r_dbc = Res()
    r_dub = Res()
    r_m96 = [Res(), Res()]
    r_rr96 = Res()
    r_bca = Res()
    r_bcb = Res()
    r_bcb2 = Res()
    r_dA = [Res(), Res()]
    r_dBu = [Res(), Res()]
    r_xs = [Res(), Res()]
    r_Bp = [Res(), Res()]
    r_C = [Res(), Res()]
    r_state = [Res() for _ in range(NJ)]
    r_yf = Res()
    r_ybf = Res()
    r_xhalf = Res()
    r_mr = Res()
    r_x2t = [Res(), Res(), Res()]
    r_x2st = [Res() for _ in range(KC)]
    r_cast = [Res(), Res()]
    r_cast2 = [Res(), Res()]
    r_zs = Res()
    r_z4 = Res()
    r_lgt = Res()
    r_lgttm = Res()
    r_rt = Res()
    r_cmb8 = Res()
    r_combtm = Res()
    r_combbc = Res()
    r_gbuf = [Res(), Res()]
    r_ubuf = [Res(), Res()]
    r_gs = [Res(), Res()]
    r_us = [Res(), Res()]
    r_rhtmp = Res()
    r_rh = [Res() for _ in range(NRI)]
    r_frstage = [Res(), Res()]
    r_frst = [Res() for _ in range(KC)]
    r_h = [Res() for _ in range(NI)]
    r_dbuf = [Res(), Res()]
    r_rdbuf = [Res(), Res()]
    r_frbuf = [Res(), Res()]
    r_fstage = [Res(), Res()]
    r_fst = [Res() for _ in range(KC)]
    r_fbf = Res()
    r_f2bf = Res()
    r_epx2 = [Res(), Res(), Res()]
    r_epf = [Res(), Res(), Res()]
    r_epo = [Res(), Res()]
    r_misc = Res()

    out_toks = []

    def tap(name, ap_src, waits, reads=(), q="sync"):
        if name in taps:
            out_toks.append(S.dma(q,
                                  lambda e, a=ap_src, d=taps[name]: e.dma_start(out=d, in_=a),
                                  reads=reads, waits=waits))

    S.phase = "prolog"
    const_toks = []
    for dst, src in [(ident, ident_h), (ones1x128, ones1x128_h),
                     (ones1x128bf, ones1x128bf_h), (ones128bf, ones128bf_h),
                     (lnw_sb, lnw_h), (lnb_sb, lnb_h), (dtb_sb, dtb_h),
                     (D_sb, Dcol_h), (convw_sb, convw_h), (convb_sb, convb_h),
                     (Acols_sb, Acols_h), (selR_sb, selR_h), (inpT_f32, inpT_h),
                     (mask_sb, xmask), (sel8_sb, sel8_h),
                     (cconsts_sb, cconsts_h)]:
        const_toks.append(S.dma("sync", lambda e, d=dst, s=src: e.dma_start(out=d, in_=s)))
    for dst, src in [(outTbf, outT_h), (xprojTbf, xprojT_h), (dtprojTbf, dtprojT_h),
                     (selDbf, selD_h), (selBbf, selB_h), (selCbf, selC_h),
                     (routerTbf, routerT_h)]:
        const_toks.append(S.dma("gpsimd", lambda e, d=dst, s=src: e.dma_start(out=d, in_=s)))
    CW = const_toks

    t_xpad = S.op("vector", lambda e: e.memset(xp_all[0:96, 0:1], 0.0))
    tk = S.op("scalar", lambda e: e.activation(A2[:], Acols_sb[:], AF.Exp), waits=CW)
    t_A2 = S.op("vector", lambda e: e.tensor_scalar_mul(A2[:], A2[:], -0.001), waits=[tk])
    t_nconvb = S.op("scalar", lambda e: e.activation(nconvb_sb[:], convb_sb[:], AF.Copy,
                                                     scale=-1.0), waits=CW)

    def mm_c0(e):
        for k in range(KC):
            i = e.matmul(pb[5][0:96, 0:1], inpT_f32[:, 96 * k:96 * (k + 1)],
                         lnb_sb[:, k:k + 1], start=(k == 0), stop=(k == KC - 1))
        return i
    tk = S.op("tensor", mm_c0, writes=[r_pb[5]], waits=CW)
    t_c0 = S.op("scalar", lambda e: e.activation(c0_sb[:], pb[5][0:96, 0:1], AF.Copy),
                reads=[r_pb[5]], waits=[tk])

    def mk_inpbf(e):
        for k in range(KC):
            i = e.activation(inpTbf[:, 96 * k:96 * (k + 1)], inpT_f32[:, 96 * k:96 * (k + 1)],
                             AF.Copy, scale=lnw_sb[:, k:k + 1])
        return i
    t_inpbf = S.op("scalar", mk_inpbf, waits=CW)
    for rr_ in (r_dA[0], r_dA[1], r_dBu[0]):
        rr_.w = t_inpbf

    def mm_wpsum(e):
        for k in range(KC):
            i = e.matmul(pb[6][0:1, 0:96], ones128bf[:], inpTbf[:, 96 * k:96 * (k + 1)],
                         start=(k == 0), stop=(k == KC - 1))
        return i
    tk = S.op("tensor", mm_wpsum, writes=[r_pb[6]], waits=[t_inpbf] + CW)
    t_wpsum = S.op("scalar", lambda e: e.activation(wpsum_bf[:], pb[6][0:1, 0:96], AF.Copy),
                   reads=[r_pb[6]], waits=[tk])

    S.phase = "mixer"
    x3 = xT.rearrange("(k p) t -> p k t", p=128)
    t_xbf = [None] * NCHUNK

    def load_chunk(c):
        cc = c % 2
        dst = xcb[cc].rearrange("p (k t) -> p k t", t=512)
        t_xbf[c] = S.dma("gpsimd",
                         lambda e, d=dst, c=c: e.dma_start(out=d,
                                                           in_=x3[:, :, 512 * c:512 * (c + 1)]),
                         writes=[r_xcb[cc]])

    load_chunk(0)
    load_chunk(1)
    R_MU, R_M2, R_V, R_T1, R_R1, R_RR, R_NMU, R_T2 = range(8)
    tok_xpc = [None] * NCHUNK

    for c in range(NCHUNK):
        cc = c % 2
        S.phase = "mstat"
        def mm_sx(e, cc=cc):
            for k in range(KC):
                i = e.matmul(pb[0][0:1, :], ones128bf[:], xcb[cc][:, 512 * k:512 * (k + 1)],
                             start=(k == 0), stop=(k == KC - 1))
            return i
        t_sx = S.op("tensor", mm_sx, reads=[r_xcb[cc]], writes=[r_pb[0]],
                    waits=[t_xbf[c]] + CW)
        for k in range(KC):
            kk = k % 2
            tsq = S.op("scalar", lambda e, k=k, kk=kk, cc=cc: e.activation(
                           xsq_bf[kk][:], xcb[cc][:, 512 * k:512 * (k + 1)], AF.Square),
                       reads=[r_xcb[cc]], writes=[r_xsq[kk]], waits=[t_xbf[c]])
            tmm = S.op("tensor", lambda e, k=k, kk=kk: e.matmul(
                           pb[1][0:1, :], ones128bf[:], xsq_bf[kk][:],
                           start=(k == 0), stop=(k == KC - 1)),
                       reads=[r_xsq[kk]], writes=[r_pb[1]] if k == 0 else [], waits=[tsq])
            if k > 0:
                r_pb[1].w = tmm
        t_sx2 = tmm

        tA = S.op("scalar", lambda e: e.activation(row(R_MU), pb[0][0:1, :], AF.Copy,
                                                   scale=INV_D),
                  reads=[r_pb[0]], writes=[r_rows[R_MU]], waits=[t_sx])
        tB = S.op("scalar", lambda e: e.activation(row(R_M2), pb[1][0:1, :], AF.Copy,
                                                   scale=INV_D),
                  reads=[r_pb[1]], writes=[r_rows[R_M2]], waits=[t_sx2])
        tL = S.op("scalar", lambda e: e.activation(row(R_NMU), pb[0][0:1, :], AF.Copy,
                                                   scale=-INV_D),
                  reads=[r_pb[0]], writes=[r_rows[R_NMU]], waits=[t_sx])
        tC = S.op("vector", lambda e: e.tensor_mul(row(R_T1), row(R_MU), row(R_MU)),
                  reads=[r_rows[R_MU]], writes=[r_rows[R_T1]], waits=[tA])
        tD = S.op("vector", lambda e: e.tensor_sub(row(R_V), row(R_M2), row(R_T1)),
                  reads=[r_rows[R_M2], r_rows[R_T1]], writes=[r_rows[R_V]], waits=[tB])
        tE = S.op("scalar", lambda e: e.activation(row(R_T1), row(R_V), AF.Ln,
                                                   scale=float(np.float32(1.0) + np.float32(1e-5)),
                                                   bias=float(np.float32(1e-5) * np.float32(1e-5))),
                  reads=[r_rows[R_V]], writes=[r_rows[R_T1]], waits=[tD])
        tK = S.op("scalar", lambda e: e.activation(row(R_RR), row(R_T1), AF.Exp, scale=-0.5),
                  reads=[r_rows[R_T1]], writes=[r_rows[R_RR]], waits=[tE])
        tM = S.op("scalar", lambda e, cc=cc: e.activation(nmu_bf[cc][:], row(R_NMU), AF.Copy),
                  reads=[r_rows[R_NMU]], writes=[r_nmu[cc]], waits=[tL])
        if c in (2, 3):
            off = 512 * (c - 2)
            tE2 = S.op("scalar", lambda e: e.activation(row(R_T2), row(R_V), AF.Ln, bias=EPS),
                       reads=[r_rows[R_V]], writes=[r_rows[R_T2]], waits=[tD])
            tF = S.op("scalar", lambda e, off=off: e.activation(
                          r1own[:, off:off + 512], row(R_T2), AF.Exp, scale=-0.5),
                      reads=[r_rows[R_T2]], writes=[r_r1own] if c == 3 else [], waits=[tE2])
            S.op("vector", lambda e, off=off: e.scalar_tensor_tensor(
                     nm1r1bf[:, off:off + 512], row(R_MU), -1.0, r1own[:, off:off + 512],
                     ALU.mult, ALU.mult),
                 reads=[r_rows[R_MU], r_r1own], writes=[r_nm1r1] if c == 3 else [],
                 waits=[tF])

        def mm_ip(e, cc=cc):
            for k in range(KC):
                e.matmul(pb[2][0:96, :], inpTbf[:, 96 * k:96 * (k + 1)],
                         xcb[cc][:, 512 * k:512 * (k + 1)], start=(k == 0), stop=False)
            return e.matmul(pb[2][0:96, :], wpsum_bf[:], nmu_bf[cc][:], start=False, stop=True)
        t_ip = S.op("tensor", mm_ip, reads=[r_xcb[cc], r_nmu[cc]], writes=[r_pb[2]],
                    waits=[t_xbf[c], t_inpbf, t_wpsum, tM])
        if c + 2 < NCHUNK:
            load_chunk(c + 2)

        t_rr = S.op("tensor", lambda e: e.matmul(pb[3][0:96, :], ones1x128[0:1, 0:96],
                                                 row(R_RR), start=True, stop=True),
                    reads=[r_rows[R_RR]], writes=[r_pb[3]], waits=[tK] + CW)
        t_rre = S.op("scalar", lambda e: e.activation(rr96_bc[:], pb[3][0:96, :], AF.Copy),
                     reads=[r_pb[3]], writes=[r_rr96], waits=[t_rr])
        t_mk = S.op("tensor", lambda e, c=c: e.matmul(pb[4][0:96, :], ones1x128bf[0:1, 0:96],
                                                      mask_sb[:, 512 * c:512 * (c + 1)],
                                                      start=True, stop=True),
                    writes=[r_pb[4]], waits=CW)
        t_mke = S.op("scalar", lambda e, cc=cc: e.activation(m96k[cc][:], pb[4][0:96, :],
                                                             AF.Copy),
                     reads=[r_pb[4]], writes=[r_m96[cc]], waits=[t_mk])
        if c == 2:
            tap("mu2", rowsA[:, 512 * R_MU:512 * (R_MU + 1)], [tA],
                reads=[r_rows[R_MU]])
            tap("rr2", rr96_bc[0:96, :], [t_rre], reads=[r_rr96])

        xpc = xp_all[0:96, 1 + 512 * c:1 + 512 * (c + 1)]
        t1_ = S.op("vector", lambda e, xpc=xpc: e.tensor_mul(xpc, pb[2][0:96, :], rr96_bc[:]),
                   reads=[r_pb[2], r_rr96], writes=[r_xp[c]], waits=[t_ip, t_rre])
        if c == 2:
            tap("xpraw2", xp_all[0:96, 1 + 512 * c:1 + 512 * (c + 1)], [t1_],
                reads=[r_xp[c]])
        t2_ = S.op("vector", lambda e, xpc=xpc, cc=cc: e.tensor_mul(xpc, xpc, m96k[cc][:]),
                   reads=[r_m96[cc]], writes=[r_xp[c]], waits=[t1_, t_mke])
        t3_ = S.op("vector", lambda e, xpc=xpc, cc=cc: e.scalar_tensor_tensor(
                       xpc, m96k[cc][:], c0_sb[:], xpc, ALU.mult, ALU.add),
                   reads=[r_m96[cc]], writes=[r_xp[c]], waits=[t2_, t_c0])
        tok_xpc[c] = t3_
        tap(f"xp{c}", xp_all[0:96, 1 + 512 * c:1 + 512 * (c + 1)], [t3_],
            reads=[r_xp[c]])

        S.phase = "postxp"
        pc = c - 1
        if pc < 0:
            continue
        pcc = pc % 2
        upre = u_ch[pcc][:, :]
        x_l = xp_all[0:96, 512 * pc + 0:512 * pc + 512]
        x_m = xp_all[0:96, 512 * pc + 1:512 * pc + 513]
        x_r = xp_all[0:96, 512 * pc + 2:512 * pc + 514]
        ta = S.op("vector", lambda e, x_l=x_l, upre=upre: e.tensor_scalar(
                      upre, x_l, convw_sb[:, 0:1], None, ALU.mult),
                  reads=[r_xp[pc]], writes=[r_u[pcc]], waits=[tok_xpc[pc], t_xpad] + CW)
        tb = S.op("vector", lambda e, x_m=x_m, upre=upre: e.scalar_tensor_tensor(
                      upre, x_m, convw_sb[:, 1:2], upre, ALU.mult, ALU.add),
                  reads=[r_xp[pc]], writes=[r_u[pcc]], waits=[ta])
        tc_ = S.op("vector", lambda e, x_r=x_r, upre=upre: e.scalar_tensor_tensor(
                       upre, x_r, convw_sb[:, 2:3], upre, ALU.mult, ALU.add),
                   reads=[r_xp[pc], r_xp[c]], writes=[r_u[pcc]], waits=[tb, tok_xpc[c]])
        td = S.op("scalar", lambda e, upre=upre: e.activation(
                      bc_a[0:96, :], upre, AF.Exp, scale=-1.0, bias=nconvb_sb[:]),
                  reads=[r_u[pcc]], writes=[r_bca], waits=[tc_, t_nconvb])
        te = S.op("vector", lambda e: e.tensor_scalar_add(bc_a[0:96, :], bc_a[0:96, :], 1.0),
                  writes=[r_bca], waits=[td])
        tf_ = S.op("vector", lambda e: e.reciprocal(bc_b[0:96, :], bc_a[0:96, :]),
                   reads=[r_bca], writes=[r_bcb], waits=[te])
        th = S.op("vector", lambda e, upre=upre: e.scalar_tensor_tensor(
                      upre, upre, convb_sb[:], bc_b[0:96, :], ALU.add, ALU.mult),
                  reads=[r_bcb], writes=[r_u[pcc]], waits=[tf_, td])
        ti = S.op("scalar", lambda e, upre=upre: e.activation(u_bf[:], upre, AF.Copy),
                  reads=[r_u[pcc]], writes=[r_ubf], waits=[th])
        tap(f"u{pc}", u_ch[pcc][0:96, :], [th], reads=[r_u[pcc]])

        tj = S.op("tensor", lambda e: e.matmul(pb[5][0:38, :], xprojTbf[:], u_bf[:],
                                               start=True, stop=True),
                  reads=[r_ubf], writes=[r_pb[5]], waits=[ti])
        tk_ = S.op("scalar", lambda e: e.activation(dbc_bf[:], pb[5][0:38, :], AF.Copy),
                   reads=[r_pb[5]], writes=[r_dbc], waits=[tj])
        tl = S.op("tensor", lambda e: e.matmul(pb[5][0:96, :], dtprojTbf[:], dbc_bf[0:6, :],
                                               start=True, stop=True),
                  reads=[r_dbc], writes=[r_pb[5]], waits=[tk_])
        tm_ = S.op("scalar", lambda e: e.activation(bc_a[0:96, :], pb[5][0:96, :], AF.Abs,
                                                    bias=dtb_sb[:]),
                   reads=[r_pb[5]], writes=[r_bca], waits=[tl])
        tn = S.op("scalar", lambda e: e.activation(bc_a[0:96, :], bc_a[0:96, :], AF.Exp,
                                                   scale=-1.0),
                  writes=[r_bca], waits=[tm_])
        to_ = S.op("scalar", lambda e: e.activation(bc_a[0:96, :], bc_a[0:96, :], AF.Ln,
                                                    bias=1.0),
                   writes=[r_bca], waits=[tn])
        tp_ = S.op("scalar", lambda e: e.activation(bc_b[0:96, :], pb[5][0:96, :], AF.Relu,
                                                    bias=dtb_sb[:]),
                   reads=[r_pb[5]], writes=[r_bcb], waits=[tm_])
        tq = S.op("vector", lambda e: e.tensor_add(dus_bf[:], bc_b[0:96, :], bc_a[0:96, :]),
                  reads=[r_bca, r_bcb], writes=[r_dus], waits=[to_, tp_])
        tr_ = S.op("vector", lambda e, pcc=pcc: e.scalar_tensor_tensor(
                       du_bf[:], dus_bf[:], 0.001, u_ch[pcc][:, :], ALU.mult, ALU.mult),
                   reads=[r_dus, r_u[pcc]], writes=[r_dub], waits=[tq])
        ts_ = S.op("vector", lambda e, pcc=pcc: e.tensor_mul(du_bf[:], du_bf[:], m96k[pcc][:]),
                   reads=[r_m96[pcc]], writes=[r_dub], waits=[tr_])

        tt = S.op("tensor", lambda e: e.matmul(pb[6][:, :], selBbf[:], dbc_bf[:],
                                               start=True, stop=True),
                  reads=[r_dbc], writes=[r_pb[6]], waits=[tk_] + CW)
        tu_ = S.op("scalar", lambda e, pcc=pcc: e.activation(Bp_bc[pcc][:], pb[6][:, :],
                                                             AF.Copy),
                   reads=[r_pb[6]], writes=[r_Bp[pcc]], waits=[tt])
        if pc >= 2:
            tv = S.op("tensor", lambda e: e.matmul(pb[6][:, :], selCbf[:], dbc_bf[:],
                                                   start=True, stop=True),
                      reads=[r_dbc], writes=[r_pb[6]], waits=[tu_])
            tw_ = S.op("scalar", lambda e, pcc=pcc: e.activation(C_bc[pcc][:], pb[6][:, :],
                                                                 AF.Copy),
                       reads=[r_pb[6]], writes=[r_C[pcc]], waits=[tv])

        y_last = None
        S.phase = "jloop"
        for j in range(NJ):
            jb = j % 2
            bs = 3 if jb == 0 else 6
            bs2 = 4 if jb == 0 else 7
            tj1 = S.op("tensor", lambda e, j=j, bs=bs: e.matmul(
                           pb[bs][:, :], selDbf[:, 128 * j:128 * (j + 1)], dus_bf[:],
                           start=True, stop=True),
                       reads=[r_dus], writes=[r_pb[bs]], waits=[tq])
            tj2 = S.op("scalar", lambda e, j=j, jb=jb, bs=bs: e.activation(
                           dA_b[jb][:], pb[bs][:, :], AF.Exp, scale=A2[:, j:j + 1]),
                       reads=[r_pb[bs]], writes=[r_dA[jb]], waits=[tj1, t_A2])
            tj3 = S.op("tensor", lambda e, j=j, bs2=bs2: e.matmul(
                           pb[bs2][:, :], selDbf[:, 128 * j:128 * (j + 1)], du_bf[:],
                           start=True, stop=True),
                       reads=[r_dub], writes=[r_pb[bs2]], waits=[ts_])
            tj4 = S.op("vector", lambda e, jb=jb, pcc=pcc, bs2=bs2: e.tensor_mul(
                           dBu_b[jb][:], pb[bs2][:, :], Bp_bc[pcc][:]),
                       reads=[r_pb[bs2], r_Bp[pcc]], writes=[r_dBu[jb]], waits=[tj3, tu_])
            init = 0.0 if pc == 0 else state_cols[:, j:j + 1]
            tj5 = S.op("vector", lambda e, jb=jb, init=init: e.tensor_tensor_scan(
                           xs_b[jb][:], dA_b[jb][:], dBu_b[jb][:], init, ALU.mult, ALU.add),
                       reads=[r_dA[jb], r_dBu[jb]] + ([r_state[j]] if pc > 0 else []),
                       writes=[r_xs[jb]], waits=[tj2, tj4])
            tj6 = S.op("vector", lambda e, j=j, jb=jb: e.tensor_copy(
                           state_cols[:, j:j + 1], xs_b[jb][:, 511:512]),
                       reads=[r_xs[jb]], writes=[r_state[j]], waits=[tj5])
            if pc >= 2:
                tj7 = S.op("vector", lambda e, jb=jb, pcc=pcc: e.tensor_mul(
                               prod_b[jb][:], xs_b[jb][:], C_bc[pcc][:]),
                           reads=[r_xs[jb], r_C[pcc]], writes=[r_dA[jb]], waits=[tw_, tj6])
                tj8 = S.op("tensor", lambda e, j=j, jb=jb: e.matmul(
                               pb[2][0:96, :], selR_sb[:, 96 * j:96 * (j + 1)], prod_b[jb][:],
                               start=(j == 0), stop=(j == NJ - 1)),
                           reads=[r_dA[jb]], writes=[r_pb[2]] if j == 0 else [],
                           waits=[tj7] + CW)
                if j > 0:
                    r_pb[2].w = tj8
                y_last = tj8

        if pc >= 2:
            off = 512 * (pc - 2)
            ty1 = S.op("vector", lambda e, off=off, pcc=pcc: e.scalar_tensor_tensor(
                           y_f[:, off:off + 512], u_ch[pcc][:, :], D_sb[:],
                           pb[2][0:96, :], ALU.mult, ALU.add),
                       reads=[r_pb[2], r_u[pcc]], writes=[r_yf], waits=[y_last] + CW)
            S.op("scalar", lambda e, off=off: e.activation(
                     y_bf[:, off:off + 512], y_f[:, off:off + 512], AF.Copy),
                 reads=[r_yf], writes=[r_ybf], waits=[ty1])
    tap("yf", y_f[0:96, :], [r_ybf.w], reads=[r_yf])

    S.phase = "O"
    S.fence()

    def stats_tile(src_f32, first, last, wtok, par=0):
        cb = cast_bf if par == 0 else cast_bf2
        rc = r_cast if par == 0 else r_cast2
        tca = S.op("scalar", lambda e, s=src_f32, cb=cb: e.activation(cb[0][:], s, AF.Copy),
                   writes=[rc[0]], waits=wtok)
        tcb = S.op("scalar", lambda e, s=src_f32, cb=cb: e.activation(cb[1][:], s, AF.Square),
                   writes=[rc[1]], waits=wtok)
        ts1 = S.op("tensor", lambda e, first=first, last=last, cb=cb: e.matmul(
                       pb[0][0:1, :], ones128bf[:], cb[0][:], start=first, stop=last),
                   reads=[rc[0]], writes=[r_pb[0]] if first else [], waits=[tca] + CW)
        if not first:
            r_pb[0].w = ts1
        ts2 = S.op("tensor", lambda e, first=first, last=last, cb=cb: e.matmul(
                       pb[1][0:1, :], ones128bf[:], cb[1][:], start=first, stop=last),
                   reads=[rc[1]], writes=[r_pb[1]] if first else [], waits=[tcb] + CW)
        if not first:
            r_pb[1].w = ts2
        return ts1, ts2

    def rstd_rows(dst_r_res, dst_r, dst_nm_res, dst_nm, rowset):
        mu, m2, v, t1 = rowset
        tA = S.op("scalar", lambda e: e.activation(mu, pb[0][0:1, :], AF.Copy, scale=INV_D),
                  reads=[r_pb[0]], writes=[r_rows[R_MU]])
        tB = S.op("scalar", lambda e: e.activation(m2, pb[1][0:1, :], AF.Copy, scale=INV_D),
                  reads=[r_pb[1]], writes=[r_rows[R_M2]])
        tC = S.op("vector", lambda e: e.tensor_mul(t1, mu, mu),
                  reads=[r_rows[R_MU]], writes=[r_rows[R_T1]], waits=[tA])
        tD = S.op("vector", lambda e: e.tensor_sub(v, m2, t1),
                  reads=[r_rows[R_M2], r_rows[R_T1]], writes=[r_rows[R_V]], waits=[tB, tC])
        tE = S.op("scalar", lambda e: e.activation(t1, v, AF.Ln, bias=EPS),
                  reads=[r_rows[R_V]], writes=[r_rows[R_T1]], waits=[tD])
        tF = S.op("scalar", lambda e: e.activation(dst_r, t1, AF.Exp, scale=-0.5),
                  reads=[r_rows[R_T1]], writes=[dst_r_res], waits=[tE])
        tG = S.op("vector", lambda e: e.scalar_tensor_tensor(dst_nm, mu, -1.0, dst_r,
                                                             ALU.mult, ALU.mult),
                  reads=[r_rows[R_MU], dst_r_res], writes=[dst_nm_res], waits=[tF])
        return tF, tG

    def bcast(dstbuf, dst_res, src_row, src_res, wtoks, bank, npart=128):
        tb1 = S.op("tensor", lambda e: e.matmul(pb[bank][0:npart, :], ones1x128[0:1, 0:npart],
                                                src_row, start=True, stop=True),
                   reads=[src_res], writes=[r_pb[bank]], waits=wtoks + CW)
        tb2 = S.op("scalar", lambda e: e.activation(dstbuf[0:npart, :], pb[bank][0:npart, :],
                                                    AF.Copy),
                   reads=[r_pb[bank]], writes=[dst_res], waits=[tb1])
        return tb2

    xh3 = x_half.rearrange("p (k t) -> p k t", t=512)
    ln4_toks = []

    for hc in range(2):
        off = 512 * hc
        t_xh = S.dma("sync", lambda e, off=off: e.dma_start(
                         out=xh3[:, :, :], in_=x3[:, :, 1024 + off:1024 + off + 512]),
                     writes=[r_xhalf])
        t_r1b = bcast(bc_a, r_bca, r1own[:, off:off + 512], r_r1own, [], 3)
        for mc in range(KC):
            bank = 2 if mc % 2 == 0 else 5

            def mm_op1(e, mc=mc, off=off, bank=bank):
                e.matmul(pb[bank][:, :], outTbf[:, 128 * mc:128 * (mc + 1)],
                         y_bf[:, off:off + 512], start=True, stop=False)
                return e.matmul(pb[bank][:, :], ones1x128bf[:], nm1r1bf[:, off:off + 512],
                                start=False, stop=True)
            t_op = S.op("tensor", mm_op1, reads=[r_ybf, r_nm1r1], writes=[r_pb[bank]],
                        waits=CW)
            bcb_i = bc_b if mc % 2 == 0 else bc_b2
            r_bcb_i = r_bcb if mc % 2 == 0 else r_bcb2
            t_s = S.op("vector", lambda e, mc=mc, bcb_i=bcb_i: e.tensor_mul(
                           bcb_i[:, :], x_half[:, 512 * mc:512 * (mc + 1)], bc_a[:, :]),
                       reads=[r_xhalf, r_bca], writes=[r_bcb_i], waits=[t_xh, t_r1b])
            t_mr = S.op("vector", lambda e, mc=mc, bank=bank, bcb_i=bcb_i: e.tensor_add(
                            mr_sb[:, 512 * mc:512 * (mc + 1)], pb[bank][:, :], bcb_i[:, :]),
                        reads=[r_pb[bank], r_bcb_i], writes=[r_mr] if mc == 0 else [],
                        waits=[t_op, t_s])
            if mc > 0:
                r_mr.w = t_mr
            stats_tile(mr_sb[:, 512 * mc:512 * (mc + 1)], mc == 0, mc == KC - 1, [t_mr],
                       par=mc % 2)
        tap(f"mr{hc}", mr_sb[:, :], [r_mr.w], reads=[r_mr])
        t_r3, t_nm3 = rstd_rows(r_rows[R_R1], row(R_R1), r_rows[R_T2], row(R_T2),
                                (row(R_MU), row(R_M2), row(R_V), row(R_T1)))
        t_r3b = bcast(bc_a, r_bca, row(R_R1), r_rows[R_R1], [t_r3], 3)
        t_nm3b = bcast(bc_b, r_bcb, row(R_T2), r_rows[R_T2], [t_nm3], 4)
        for k in range(KC):
            kk = k % 2
            tx1 = S.op("vector", lambda e, k=k: e.tensor_mul(
                           mr_sb[:, 512 * k:512 * (k + 1)], mr_sb[:, 512 * k:512 * (k + 1)],
                           bc_a[:, :]),
                       reads=[r_bca], writes=[r_mr], waits=[t_r3b])
            tx2 = S.op("gpsimd", lambda e, k=k: e.tensor_add(
                           mr_sb[:, 512 * k:512 * (k + 1)], mr_sb[:, 512 * k:512 * (k + 1)],
                           bc_b[:, :]),
                       reads=[r_bcb], writes=[r_mr], waits=[tx1, t_nm3b])
            tx3 = S.op("vector", lambda e, k=k, kk=kk: e.tensor_add(
                           x2t[kk][:], mr_sb[:, 512 * k:512 * (k + 1)],
                           x_half[:, 512 * k:512 * (k + 1)]),
                       reads=[r_mr, r_xhalf], writes=[r_x2t[kk]], waits=[tx2, t_xh])
            tst = S.dma("scalar", lambda e, k=k, kk=kk, off=off: e.dma_start(
                            out=x2_st[k, :, off:off + 512], in_=x2t[kk][:]),
                        reads=[r_x2t[kk]], waits=[tx3])
            r_x2st[k] = tst if hc == 0 else tst
            cb4 = cast_bf if k % 2 == 0 else cast_bf2
            rc4 = r_cast if k % 2 == 0 else r_cast2
            tca = S.op("scalar", lambda e, kk=kk, cb4=cb4: e.activation(
                           cb4[0][:], x2t[kk][:], AF.Copy),
                       reads=[r_x2t[kk]], writes=[rc4[0]], waits=[tx3])
            tcb = S.op("scalar", lambda e, kk=kk, cb4=cb4: e.activation(
                           cb4[1][:], x2t[kk][:], AF.Square),
                       reads=[r_x2t[kk]], writes=[rc4[1]], waits=[tx3])
            ts1 = S.op("tensor", lambda e, k=k, cb4=cb4: e.matmul(
                           pb[6][0:1, :], ones128bf[:], cb4[0][:],
                           start=(k == 0), stop=(k == KC - 1)),
                       reads=[rc4[0]], writes=[r_pb[6]] if k == 0 else [], waits=[tca] + CW)
            if k > 0:
                r_pb[6].w = ts1
            ts2 = S.op("tensor", lambda e, k=k, cb4=cb4: e.matmul(
                           pb[7][0:1, :], ones128bf[:], cb4[1][:],
                           start=(k == 0), stop=(k == KC - 1)),
                       reads=[rc4[1]], writes=[r_pb[7]] if k == 0 else [], waits=[tcb] + CW)
            if k > 0:
                r_pb[7].w = ts2
        def row4(i):
            return rows4[:, 512 * i:512 * (i + 1)]
        s_r4, s_nm4 = 2 * hc, 2 * hc + 1
        tA = S.op("scalar", lambda e: e.activation(row(R_MU), pb[6][0:1, :], AF.Copy,
                                                   scale=INV_D),
                  reads=[r_pb[6]], writes=[r_rows[R_MU]], waits=[ts1])
        tB = S.op("scalar", lambda e: e.activation(row(R_M2), pb[7][0:1, :], AF.Copy,
                                                   scale=INV_D),
                  reads=[r_pb[7]], writes=[r_rows[R_M2]], waits=[ts2])
        tC = S.op("vector", lambda e: e.tensor_mul(row(R_T1), row(R_MU), row(R_MU)),
                  reads=[r_rows[R_MU]], writes=[r_rows[R_T1]], waits=[tA])
        tD = S.op("vector", lambda e: e.tensor_sub(row(R_V), row(R_M2), row(R_T1)),
                  reads=[r_rows[R_M2], r_rows[R_T1]], writes=[r_rows[R_V]], waits=[tB, tC])
        tE = S.op("scalar", lambda e: e.activation(row(R_T1), row(R_V), AF.Ln, bias=EPS),
                  reads=[r_rows[R_V]], writes=[r_rows[R_T1]], waits=[tD])
        tF = S.op("scalar", lambda e, s_r4=s_r4, row4=row4: e.activation(
                      row4(s_r4), row(R_T1), AF.Exp, scale=-0.5),
                  reads=[r_rows[R_T1]], writes=[r_rows4[s_r4]], waits=[tE])
        tG = S.op("vector", lambda e, s_r4=s_r4, s_nm4=s_nm4, row4=row4:
                  e.scalar_tensor_tensor(
                      row4(s_nm4), row(R_MU), -1.0, row4(s_r4), ALU.mult, ALU.mult),
                  reads=[r_rows[R_MU], r_rows4[s_r4]], writes=[r_rows4[s_nm4]], waits=[tF])
        ln4_toks.append((tF, tG))

    S.fence()
    for hc in range(2):
        off = 512 * hc
        s_r4, s_nm4 = 2 * hc, 2 * hc + 1
        t_r4b = bcast(bc_a, r_bca, rows4[:, 512 * s_r4:512 * (s_r4 + 1)],
                      r_rows4[s_r4], [ln4_toks[hc][0]], 3)
        t_nm4b = bcast(bc_b, r_bcb, rows4[:, 512 * s_nm4:512 * (s_nm4 + 1)],
                       r_rows4[s_nm4], [ln4_toks[hc][1]], 4)
        for k in range(KC):
            kk = k % 3
            trl = S.dma("scalar" if k % 2 == 0 else "sync",
                        lambda e, k=k, kk=kk, off=off: e.dma_start(
                            out=x2t[kk][:], in_=x2_st[k, :, off:off + 512]),
                        writes=[r_x2t[kk]], waits=[r_x2st[k]])
            tz1 = S.op("vector", lambda e, kk=kk: e.tensor_mul(
                           x2t[kk][:], x2t[kk][:], bc_a[:, :]),
                       reads=[r_bca], writes=[r_x2t[kk]], waits=[trl, t_r4b])
            tz2 = S.op("vector", lambda e, k=k, kk=kk, off=off: e.tensor_add(
                           z4_sb[:, 1024 * k + off:1024 * k + off + 512], x2t[kk][:],
                           bc_b[:, :]),
                       reads=[r_x2t[kk], r_bcb], writes=[r_z4] if (hc == 0 and k == 0) else [],
                       waits=[tz1, t_nm4b])
            if not (hc == 0 and k == 0):
                r_z4.w = tz2
    tap("x2st", x2_st[:, :, :].rearrange("a b c -> (a b) c"), list(r_x2st))
    tap("z4t", z4_sb[:, :], [r_z4.w], reads=[r_z4], q="gpsimd")

    S.phase = "router"
    snapO = S.snapshot()
    S.fence_from(snapO)
    t_lge = [None, None]
    for hc in range(2):
        off = 512 * hc

        def mm_lg(e, off=off):
            for k in range(KC):
                i = e.matmul(pb[5][0:8, :], routerTbf[:, 8 * k:8 * (k + 1)],
                             z4_sb[:, 1024 * k + off:1024 * k + off + 512],
                             start=(k == 0), stop=(k == KC - 1))
            return i
        t_lg = S.op("tensor", mm_lg, reads=[r_z4], writes=[r_pb[5]], waits=CW)
        t_lge[hc] = S.op("scalar", lambda e, off=off: e.activation(
                             lgt_sb[:, off:off + 512], pb[5][0:8, :], AF.Copy),
                         reads=[r_pb[5]], writes=[r_lgt], waits=[t_lg])
    for tt in range(8):
        t_tr = S.op("tensor", lambda e, tt=tt: e.transpose(
                        pb[7][:, 0:8], lgt_sb[:, 128 * tt:128 * (tt + 1)], ident[0:8, 0:8]),
                    reads=[r_lgt], writes=[r_pb[7]], waits=t_lge + CW)
        S.op("scalar", lambda e, tt=tt: e.activation(lgt_tm[:, 8 * tt:8 * (tt + 1)],
                                                     pb[7][:, 0:8], AF.Copy),
             reads=[r_pb[7]], writes=[r_lgttm])
    for tt in range(8):
        lg8 = lgt_tm[:, 8 * tt:8 * (tt + 1)]
        tm1 = S.op("vector", lambda e, lg8=lg8, tt=tt: e.reduce_max(
                       rt_m1[:, tt:tt + 1], lg8, axis=mybir.AxisListType.X),
                   reads=[r_lgttm], writes=[r_rt], waits=[])
        tmsk = S.op("vector", lambda e, lg8=lg8, tt=tt: e.tensor_scalar(
                        rt_t[:], lg8, rt_m1[:, tt:tt + 1], None, ALU.is_equal),
                    reads=[r_lgttm], writes=[r_rt], waits=[tm1])
        tmv = S.op("vector", lambda e, lg8=lg8: e.scalar_tensor_tensor(
                       rt_t[:], rt_t[:], -1e30, lg8, ALU.mult, ALU.add),
                   reads=[r_lgttm], writes=[r_rt], waits=[tmsk])
        tm2 = S.op("vector", lambda e: e.reduce_max(rt_t2[:, 0:1], rt_t[:],
                                                    axis=mybir.AxisListType.X),
                   writes=[r_rt], waits=[tmv])
        tdl = S.op("vector", lambda e, tt=tt: e.tensor_sub(rt_t2[:, 1:2], rt_t2[:, 0:1],
                                                           rt_m1[:, tt:tt + 1]),
                   writes=[r_rt], waits=[tm2])
        tsg = S.op("scalar", lambda e: e.activation(rt_t2[:, 2:3], rt_t2[:, 1:2], AF.Exp),
                   reads=[r_rt], writes=[r_rt], waits=[tdl])
        tT_ = S.op("vector", lambda e: e.tensor_scalar_add(rt_t2[:, 3:4], rt_t2[:, 2:3], 1.0),
                   writes=[r_rt], waits=[tsg])
        tin = S.op("vector", lambda e: e.reciprocal(rt_t2[:, 4:5], rt_t2[:, 3:4]),
                   writes=[r_rt], waits=[tT_])
        tcm = S.op("vector", lambda e, tt=tt: e.scalar_tensor_tensor(
                       comb_tm[:, tt:tt + 1], rt_t2[:, 2:3], rt_t2[:, 4:5], rt_t2[:, 4:5],
                       ALU.mult, ALU.add),
                   reads=[r_rt], writes=[r_combtm], waits=[tin])
    t_ct = S.op("tensor", lambda e: e.transpose(pb[7][0:8, 0:128], comb_tm[:], ident[:]),
                reads=[r_combtm], writes=[r_pb[7]], waits=CW)
    t_ce = S.op("scalar", lambda e: e.activation(cmb8[:], pb[7][0:8, 0:128], AF.Copy),
                reads=[r_pb[7]], writes=[r_cmb8], waits=[t_ct])
    for half in range(2):
        def mm_cb(e, half=half):
            for q in range(4):
                tt = 4 * half + q
                i = e.matmul(pb[3][:, 128 * q:128 * (q + 1)],
                             sel8_sb[:, 128 * tt:128 * (tt + 1)], cmb8[:],
                             start=True, stop=True)
            return i
        t_cb = S.op("tensor", mm_cb, reads=[r_cmb8], writes=[r_pb[3]], waits=[t_ce] + CW)
        S.op("scalar", lambda e, half=half: e.activation(
                 comb_bc[:, 512 * half:512 * (half + 1)], pb[3][:, :], AF.Copy),
             reads=[r_pb[3]], writes=[r_combbc])

    tap("combt", comb_bc[:, :], [r_combbc.w], reads=[r_combbc])
    tap("lgtt", lgt_tm[:, :], [r_lgttm.w], reads=[r_lgttm])

    S.phase = "routed"
    S.fence_from(snapO)
    t_wrg = [None] * NRI
    t_wru = [None] * NRI

    def sched_rdma(ir):
        ib = ir % 2
        t_wrg[ir] = S.dma("gpsimd", lambda e, ir=ir, ib=ib: e.dma_start(
                              out=gbuf[ib][:], in_=Wrg_h[ir]), writes=[r_gbuf[ib]])
        t_wru[ir] = S.dma("gpsimd", lambda e, ir=ir, ib=ib: e.dma_start(
                              out=ubuf[ib][:], in_=Wru_h[ir]), writes=[r_ubuf[ib]])

    sched_rdma(0)
    sched_rdma(1)
    for ir in range(NRI):
        ib = ir % 2
        for half in range(2):
            bg = (0 if ib == 0 else 4) + half
            bu = (2 if ib == 0 else 6) + half
            hof = 512 * half

            def mm_rg(e, ib=ib, hof=hof, bg=bg):
                for k in range(KC):
                    i = e.matmul(pb[bg][:, :], gbuf[ib][:, 128 * k:128 * (k + 1)],
                                 z4_sb[:, 1024 * k + hof:1024 * k + hof + 512],
                                 start=(k == 0), stop=(k == KC - 1))
                return i
            t_g = S.op("tensor", mm_rg, reads=[r_gbuf[ib], r_z4], writes=[r_pb[bg]],
                       waits=[t_wrg[ir]])

            def mm_ru(e, ib=ib, hof=hof, bu=bu):
                for k in range(KC):
                    i = e.matmul(pb[bu][:, :], ubuf[ib][:, 128 * k:128 * (k + 1)],
                                 z4_sb[:, 1024 * k + hof:1024 * k + hof + 512],
                                 start=(k == 0), stop=(k == KC - 1))
                return i
            t_u = S.op("tensor", mm_ru, reads=[r_ubuf[ib], r_z4], writes=[r_pb[bu]],
                       waits=[t_wru[ir]])
            tsl = S.op("scalar", lambda e, half=half, bg=bg: e.activation(
                           gs_t[half][:], pb[bg][:, :], AF.Silu),
                       reads=[r_pb[bg]], writes=[r_gs[half]], waits=[t_g])
            tuc = S.op("scalar", lambda e, half=half, bu=bu: e.activation(
                           us_t[half][:], pb[bu][:, :], AF.Copy),
                       reads=[r_pb[bu]], writes=[r_us[half]], waits=[t_u])
            tmul = S.op("gpsimd", lambda e, half=half, hof=hof: e.tensor_mul(
                            rh_tmp[:, hof:hof + 512], gs_t[half][:], us_t[half][:]),
                        reads=[r_gs[half], r_us[half]], writes=[r_rhtmp], waits=[tsl, tuc])
            S.op("vector", lambda e, ir=ir, hof=hof: e.tensor_mul(
                     r_h_sb[:, 1024 * ir + hof:1024 * ir + hof + 512],
                     rh_tmp[:, hof:hof + 512], comb_bc[:, hof:hof + 512]),
                 reads=[r_rhtmp, r_combbc], writes=[r_rh[ir]], waits=[tmul])
        if ir + 2 < NRI:
            sched_rdma(ir + 2)
    tap("rht", r_h_sb[:, :], [r_rh[NRI - 1].w], reads=list(r_rh), q="gpsimd")
    S.fence_from(snapO)
    for dt in range(KC):
        db = dt % 2
        t_wrd = S.dma("gpsimd", lambda e, dt=dt, db=db: e.dma_start(
                          out=rdbuf[db][:], in_=Wrd_h[dt]), writes=[r_rdbuf[db]])
        for half in range(2):
            bank = 2 * db + half
            hof = 512 * half

            def mm_rd(e, db=db, hof=hof, bank=bank):
                for ikr in range(NRI):
                    i = e.matmul(pb[bank][:, :], rdbuf[db][:, 128 * ikr:128 * (ikr + 1)],
                                 r_h_sb[:, 1024 * ikr + hof:1024 * ikr + hof + 512],
                                 start=(ikr == 0), stop=(ikr == NRI - 1))
                return i
            t_rd = S.op("tensor", mm_rd,
                        reads=[r_rdbuf[db]] + [r_rh[i_] for i_ in range(NRI)],
                        writes=[r_pb[bank]], waits=[t_wrd])
            t_ev = S.op("scalar", lambda e, db=db, hof=hof, bank=bank: e.activation(
                            frstage[db][:, hof:hof + 512], pb[bank][:, :], AF.Copy),
                        reads=[r_pb[bank]], writes=[r_frstage[db]] if half == 0 else [],
                        waits=[t_rd])
            if half == 1:
                r_frstage[db].w = t_ev
        r_frst[dt] = S.dma("scalar", lambda e, dt=dt, db=db: e.dma_start(
                               out=fr_st[dt], in_=frstage[db][:]),
                           reads=[r_frstage[db]], waits=[])

    tap("frst", fr_st[:, :, :].rearrange("a b c -> (a b) c"), list(r_frst))

    S.phase = "gu"
    S.fence()
    t_stash_done = r_x2st[KC - 1]
    t_wg = [None] * NI
    t_wu = [None] * NI

    def sched_wdma(i):
        ib = i % 2
        t_wg[i] = S.dma("gpsimd", lambda e, i=i, ib=ib: e.dma_start(
                            out=gbuf[ib][:], in_=Wg_h[i]), writes=[r_gbuf[ib]])
        t_wu[i] = S.dma("gpsimd", lambda e, i=i, ib=ib: e.dma_start(
                            out=ubuf[ib][:], in_=Wu_h[i]), writes=[r_ubuf[ib]])

    sched_wdma(0)
    sched_wdma(1)
    for i in range(NI):
        ib = i % 2
        for half in range(2):
            bg = (0 if ib == 0 else 4) + half
            bu = (2 if ib == 0 else 6) + half
            hof = 512 * half

            def mm_g(e, ib=ib, hof=hof, bg=bg):
                for k in range(KC):
                    i2 = e.matmul(pb[bg][:, :], gbuf[ib][:, 128 * k:128 * (k + 1)],
                                  z4_sb[:, 1024 * k + hof:1024 * k + hof + 512],
                                  start=(k == 0), stop=(k == KC - 1))
                return i2
            t_g = S.op("tensor", mm_g, reads=[r_gbuf[ib], r_z4], writes=[r_pb[bg]],
                       waits=[t_wg[i]])

            def mm_u(e, ib=ib, hof=hof, bu=bu):
                for k in range(KC):
                    i2 = e.matmul(pb[bu][:, :], ubuf[ib][:, 128 * k:128 * (k + 1)],
                                  z4_sb[:, 1024 * k + hof:1024 * k + hof + 512],
                                  start=(k == 0), stop=(k == KC - 1))
                return i2
            t_u = S.op("tensor", mm_u, reads=[r_ubuf[ib], r_z4], writes=[r_pb[bu]],
                       waits=[t_wu[i]])
            tsl = S.op("scalar", lambda e, half=half, bg=bg: e.activation(
                           gs_t[half][:], pb[bg][:, :], AF.Silu),
                       reads=[r_pb[bg]], writes=[r_gs[half]], waits=[t_g])
            tuc = S.op("scalar", lambda e, half=half, bu=bu: e.activation(
                           us_t[half][:], pb[bu][:, :], AF.Copy),
                       reads=[r_pb[bu]], writes=[r_us[half]], waits=[t_u])
            S.op("gpsimd", lambda e, i=i, hof=hof, half=half: e.tensor_mul(
                     h_all[:, 1024 * i + hof:1024 * i + hof + 512],
                     gs_t[half][:], us_t[half][:]),
                 reads=[r_gs[half], r_us[half]], writes=[r_h[i]], waits=[tsl, tuc])
        if i + 2 < NI:
            sched_wdma(i + 2)

    S.phase = "down"
    S.fence()
    t_wd = [[None, None] for _ in range(KC)]
    t_fr = [None] * KC

    def sched_ddma(dt, hh):
        t_wd[dt][hh] = S.dma("gpsimd", lambda e, dt=dt, hh=hh: e.dma_start(
                                 out=dbuf[hh][:], in_=Wd_h[dt, :, 4096 * hh:4096 * (hh + 1)]),
                             writes=[r_dbuf[hh]])

    def sched_fdma(dt):
        db = dt % 2
        t_fr[dt] = S.dma("scalar", lambda e, dt=dt, db=db: e.dma_start(
                             out=frbuf[db][:], in_=fr_st[dt]),
                         writes=[r_frbuf[db]], waits=[r_frst[dt]])

    sched_ddma(0, 0)
    sched_ddma(0, 1)
    sched_fdma(0)
    sched_fdma(1)
    for dt in range(KC):
        db = dt % 2
        b0, b1 = 2 * db, 2 * db + 1

        def mm_dA(e, b0=b0, b1=b1):
            for ik in range(32):
                e.matmul(pb[b0][:, :], dbuf[0][:, 128 * ik:128 * (ik + 1)],
                         h_all[:, 1024 * ik:1024 * ik + 512],
                         start=(ik == 0), stop=False)
                i2 = e.matmul(pb[b1][:, :], dbuf[0][:, 128 * ik:128 * (ik + 1)],
                              h_all[:, 1024 * ik + 512:1024 * (ik + 1)],
                              start=(ik == 0), stop=False)
            return i2
        t_dA = S.op("tensor", mm_dA,
                    reads=[r_dbuf[0]] + [r_h[i_] for i_ in range(32)],
                    writes=[r_pb[b0], r_pb[b1]], waits=[t_wd[dt][0]])
        if dt + 1 < KC:
            sched_ddma(dt + 1, 0)

        def mm_dB(e, b0=b0, b1=b1):
            for ik in range(32, NI):
                e.matmul(pb[b0][:, :], dbuf[1][:, 128 * (ik - 32):128 * (ik - 31)],
                         h_all[:, 1024 * ik:1024 * ik + 512],
                         start=False, stop=(ik == NI - 1))
                i2 = e.matmul(pb[b1][:, :], dbuf[1][:, 128 * (ik - 32):128 * (ik - 31)],
                              h_all[:, 1024 * ik + 512:1024 * (ik + 1)],
                              start=False, stop=(ik == NI - 1))
            return i2
        t_dB = S.op("tensor", mm_dB,
                    reads=[r_dbuf[1]] + [r_h[i_] for i_ in range(32, NI)],
                    writes=[], waits=[t_wd[dt][1], t_dA])
        r_pb[b0].w = t_dB
        r_pb[b1].w = t_dB
        if dt + 1 < KC:
            sched_ddma(dt + 1, 1)
        for half in range(2):
            bank = 2 * db + half
            hof = 512 * half
            t_ev = S.op("vector", lambda e, db=db, hof=hof, bank=bank: e.tensor_add(
                            f_stage[db][:, hof:hof + 512], pb[bank][:, :],
                            frbuf[db][:, hof:hof + 512]),
                        reads=[r_pb[bank], r_frbuf[db]],
                        writes=[r_fstage[db]] if half == 0 else [],
                        waits=[t_dB, t_fr[dt]])
            if half == 1:
                r_fstage[db].w = t_ev
            tca = S.op("scalar", lambda e, db=db, hof=hof: e.activation(
                           f_bf[:, hof:hof + 512], f_stage[db][:, hof:hof + 512], AF.Copy),
                       reads=[r_fstage[db]], writes=[r_fbf], waits=[t_ev])
            tcb = S.op("scalar", lambda e, db=db, hof=hof: e.activation(
                           f2_bf[:, hof:hof + 512], f_stage[db][:, hof:hof + 512], AF.Square),
                       reads=[r_fstage[db]], writes=[r_f2bf], waits=[t_ev])
            ts1 = S.op("tensor", lambda e, dt=dt, hof=hof, half=half: e.matmul(
                           pb[4 + half][0:1, :], ones128bf[:], f_bf[:, hof:hof + 512],
                           start=(dt == 0), stop=(dt == KC - 1)),
                       reads=[r_fbf], writes=[r_pb[4 + half]] if dt == 0 else [],
                       waits=[tca] + CW)
            if dt > 0:
                r_pb[4 + half].w = ts1
            ts2 = S.op("tensor", lambda e, dt=dt, hof=hof, half=half: e.matmul(
                           pb[6 + half][0:1, :], ones128bf[:], f2_bf[:, hof:hof + 512],
                           start=(dt == 0), stop=(dt == KC - 1)),
                       reads=[r_f2bf], writes=[r_pb[6 + half]] if dt == 0 else [],
                       waits=[tcb] + CW)
            if dt > 0:
                r_pb[6 + half].w = ts2
        r_fst[dt] = S.dma("scalar", lambda e, dt=dt, db=db: e.dma_start(
                              out=f_st[dt], in_=f_stage[db][:]),
                          reads=[r_fstage[db]], waits=[])
        if dt + 2 < KC:
            sched_fdma(dt + 2)

    S.phase = "ep"
    S.fence()
    ep_r5 = [eprow(0), eprow(1)]
    ep_nm5 = [eprow(2), eprow(3)]
    for half in range(2):
        tA = S.op("scalar", lambda e, half=half: e.activation(
                      eprow(4), pb[4 + half][0:1, :], AF.Copy, scale=INV_D),
                  reads=[r_pb[4 + half]], writes=[r_eprows[4]])
        tB = S.op("scalar", lambda e, half=half: e.activation(
                      eprow(5), pb[6 + half][0:1, :], AF.Copy, scale=INV_D),
                  reads=[r_pb[6 + half]], writes=[r_eprows[5]])
        tC = S.op("vector", lambda e: e.tensor_mul(eprow(6), eprow(4), eprow(4)),
                  reads=[r_eprows[4]], writes=[r_eprows[6]], waits=[tA])
        tD = S.op("vector", lambda e: e.tensor_sub(eprow(7), eprow(5), eprow(6)),
                  reads=[r_eprows[5], r_eprows[6]], writes=[r_eprows[7]], waits=[tB, tC])
        tE = S.op("scalar", lambda e: e.activation(eprow(6), eprow(7), AF.Ln, bias=EPS),
                  reads=[r_eprows[7]], writes=[r_eprows[6]], waits=[tD])
        tF = S.op("scalar", lambda e, half=half: e.activation(
                      ep_r5[half], eprow(6), AF.Exp, scale=-0.5),
                  reads=[r_eprows[6]], writes=[r_eprows[half]], waits=[tE])
        S.op("vector", lambda e, half=half: e.scalar_tensor_tensor(
                 ep_nm5[half], eprow(4), -1.0, ep_r5[half], ALU.mult, ALU.mult),
             reads=[r_eprows[4], r_eprows[half]], writes=[r_eprows[2 + half]], waits=[tF])
    r5_bc = comb_bc
    nm5_bc = frbuf[0]
    for half in range(2):
        hof = 512 * half
        tb1 = S.op("tensor", lambda e, half=half: e.matmul(
                       pb[0][:, :], ones1x128[:], ep_r5[half], start=True, stop=True),
                   reads=[r_eprows[half]], writes=[r_pb[0]], waits=CW)
        S.op("scalar", lambda e, hof=hof: e.activation(r5_bc[:, hof:hof + 512],
                                                       pb[0][:, :], AF.Copy),
             reads=[r_pb[0]], writes=[r_combbc], waits=[tb1])
        tb2 = S.op("tensor", lambda e, half=half: e.matmul(
                       pb[1][:, :], ones1x128[:], ep_nm5[half], start=True, stop=True),
                   reads=[r_eprows[2 + half]], writes=[r_pb[1]], waits=CW)
        S.op("scalar", lambda e, hof=hof: e.activation(nm5_bc[:, hof:hof + 512],
                                                       pb[1][:, :], AF.Copy),
             reads=[r_pb[1]], writes=[r_frbuf[0]], waits=[tb2])
    for k in range(KC):
        kk = k % 3
        ko = k % 2
        t_lx = S.dma("scalar", lambda e, k=k, kk=kk: e.dma_start(
                         out=ep_x2[kk][:], in_=x2_st[k]), writes=[r_epx2[kk]],
                     waits=[r_x2st[k]])
        fsrc, rfs = ep_f[kk], r_epf[kk]
        t_lf = S.dma("gpsimd", lambda e, k=k, kk=kk: e.dma_start(
                         out=ep_f[kk][:], in_=f_st[k]), writes=[r_epf[kk]],
                     waits=[r_fst[k]])
        t_e1 = S.op("vector", lambda e, kk=kk, fsrc=fsrc: e.tensor_mul(
                        ep_f[kk][:], fsrc[:], r5_bc[:, :]),
                    reads=[r_combbc, rfs], writes=[r_epf[kk]], waits=[t_lf])
        t_e2 = S.op("gpsimd", lambda e, kk=kk, ko=ko: e.tensor_add(
                        ep_o[ko][:], ep_x2[kk][:], nm5_bc[:, :]),
                    reads=[r_frbuf[0], r_epx2[kk]], writes=[r_epo[ko]], waits=[t_lx])
        t_e3 = S.op("vector", lambda e, kk=kk, ko=ko: e.tensor_add(
                        ep_o[ko][:], ep_f[kk][:], ep_o[ko][:]),
                    reads=[r_epf[kk]], writes=[r_epo[ko]], waits=[t_e1, t_e2])
        out_toks.append(S.dma("sync", lambda e, k=k, ko=ko: e.dma_start(
                                  out=out_dev[128 * k:128 * (k + 1), :], in_=ep_o[ko][:]),
                              reads=[r_epo[ko]], waits=[t_e3]))

    final_waits = {}
    for (s, v, _) in out_toks:
        final_waits[s] = max(final_waits.get(s, 0), v)
    fw = list(final_waits.items())

    def final_op(e, fw=fw):
        for s, v in fw:
            e.wait_ge(_SEMS[s], v)
        return e.nop()
    S.cnt["s_sy"] = S.cnt.get("s_sy", 0) + 1
    S.ops.append(("sync", lambda e: final_op(e).then_inc(_SEMS["s_sy"], 1)))

    with nc.Block() as block:
        @block.sync
        def _(e):
            for eng, fn in S.ops:
                if eng == "sync":
                    fn(e)

        @block.gpsimd
        def _(e):
            for eng, fn in S.ops:
                if eng == "gpsimd":
                    fn(e)

        @block.tensor
        def _(e):
            for eng, fn in S.ops:
                if eng == "tensor":
                    fn(e)

        @block.vector
        def _(e):
            for eng, fn in S.ops:
                if eng == "vector":
                    fn(e)

        @block.scalar
        def _(e):
            for eng, fn in S.ops:
                if eng == "scalar":
                    fn(e)

    es.close()
    nc._dbg_sched = S
    return nc


_PROG_CACHE = {}


def kernel(**inputs):
    key = tuple(sorted(n for n, _ in _DEBUG_TAPS))
    if key not in _PROG_CACHE:
        _PROG_CACHE[key] = build_program(list(_DEBUG_TAPS))
    nc = _PROG_CACHE[key]
    g = _host_shared(inputs)
    x = inputs["L0_attn_feat_patch"]
    in_maps = []
    for c in range(8):
        m = dict(g)
        m.update(_host_percore(x, c))
        in_maps.append(m)
    res = run_bass_kernel_spmd(nc, in_maps, core_ids=list(range(8)))
    out = np.empty((4, 2048, 2048), np.float32)
    for c in range(8):
        b, h = c // 2, c % 2
        out[b, 1024 * h:1024 * (h + 1), :] = res.results[c]["out"].T
    if _DEBUG_TAPS:
        kernel._dbg = [{k: v for k, v in r.items() if k.startswith("dbg_")}
                       for r in res.results]
    return out


# version 17 (final)
# speedup vs baseline: 1.0463x; 1.0046x over previous
import contextlib
import numpy as np
import ml_dtypes
import concourse.bass as bass
import concourse.mybir as mybir
from concourse.bass_utils import run_bass_kernel_spmd

F32 = mybir.dt.float32
BF16 = mybir.dt.bfloat16
AF = mybir.ActivationFunctionType
ALU = mybir.AluOpType
BF = ml_dtypes.bfloat16

KC = 16
TP = 2560
NCHUNK = 5
NJ = 12
NI = 64
NRI = 8
EPS = 1e-5
INV_D = 1.0 / 2048.0

_DEBUG_TAPS = []

ENG_SEM = {"tensor": "s_pe", "scalar": "s_act", "vector": "s_dve",
           "gpsimd": "s_pool", "sync": "s_sy"}
NDS = 4
DMA_SEMS = {q: [f"d_{q[:2]}{i}" for i in range(NDS)] for q in ("sync", "gpsimd", "scalar")}
ALL_SEMS = sorted(set(list(ENG_SEM.values())
                      + [s for v in DMA_SEMS.values() for s in v]))


class Res:
    __slots__ = ("w", "r")

    def __init__(self):
        self.w = None
        self.r = []


class Sched:
    def __init__(self):
        self.ops = []
        self.cnt = {}
        self.pending = {}
        self.dma_idx = {q: 0 for q in DMA_SEMS}
        self.phase = "init"
        self.dbg_map = {e: [] for e in ENG_SEM}

    def _mk(self, eng, emit, waits, inc_sem, inc_by):
        if eng in self.pending:
            waits = list(waits) + self.pending.pop(eng)
        self.cnt[inc_sem] = self.cnt.get(inc_sem, 0) + inc_by
        tok = (inc_sem, self.cnt[inc_sem], eng)
        best = {}
        for wt in waits:
            if wt is None:
                continue
            s, v, peng = wt
            if best.get(s, -1) < v:
                best[s] = v
        wl = sorted(best.items())

        ph = self.phase

        def run(e, emit=emit, wl=wl, inc=(inc_sem, inc_by), ph=ph, eng=eng):
            n0 = e.bass.n_instructions() if hasattr(e, "bass") else None
            for s, v in wl:
                e.wait_ge(_SEMS[s], v)
            inst = emit(e)
            inst.then_inc(_SEMS[inc[0]], inc[1])
            n1 = e.bass.n_instructions() if n0 is not None else None
            self.dbg_map[eng].append((n1 - n0 if n0 is not None else 1 + len(wl), ph))
        self.ops.append((eng, run))
        return tok

    def _deps(self, reads, writes, waits):
        ws = list(waits)
        for r in reads:
            if r.w is not None:
                ws.append(r.w)
        for r in writes:
            if r.w is not None:
                ws.append(r.w)
            ws.extend(r.r)
        return ws

    def _note(self, tok, reads, writes):
        for r in reads:
            r.r.append(tok)
        for r in writes:
            r.w = tok
            r.r = []

    def op(self, eng, emit, reads=(), writes=(), waits=()):
        tok = self._mk(eng, emit, self._deps(reads, writes, waits), ENG_SEM[eng], 1)
        self._note(tok, reads, writes)
        return tok

    def dma(self, eng, emit, reads=(), writes=(), waits=()):
        sem = DMA_SEMS[eng][self.dma_idx[eng] % NDS]
        self.dma_idx[eng] += 1
        ws = self._deps(reads, writes, waits)
        prior = self.cnt.get(sem, 0)
        if prior > 0:
            ws = list(ws) + [(sem, prior, "_dma")]
        tok = self._mk(eng, emit, ws, sem, 16)
        self._note(tok, reads, writes)
        return tok

    def snapshot(self):
        return [(s, self.cnt.get(s, 0), "_f") for s in ALL_SEMS if self.cnt.get(s, 0) > 0]

    def fence(self):
        snap = self.snapshot()
        for e in ENG_SEM:
            self.pending[e] = list(snap)

    def fence_from(self, snap):
        for e in ENG_SEM:
            self.pending[e] = list(snap)


_SEMS = None


def _host_shared(inp):
    g = {}
    in_proj_W = inp["L0_attn_in_proj_W"]
    A_log = inp["L0_attn_A_log"]
    pj = np.arange(128)
    g["lnw_cols"] = np.ascontiguousarray(inp["L0_attn_ln_w"].reshape(16, 128).T)
    g["lnb_cols"] = np.ascontiguousarray(inp["L0_attn_ln_b"].reshape(16, 128).T)
    g["inpT"] = np.ascontiguousarray(
        in_proj_W.reshape(96, 16, 128).transpose(2, 1, 0).reshape(128, 16 * 96))
    g["outT"] = np.ascontiguousarray(inp["L0_attn_out_proj_W"].T)
    g["xprojT"] = np.ascontiguousarray(inp["L0_attn_x_proj_W"].T)
    g["dtprojT"] = np.ascontiguousarray(inp["L0_attn_dt_proj_W"].T)
    g["dtb_col"] = np.ascontiguousarray(inp["L0_attn_dt_proj_b"].reshape(96, 1))
    g["D_col"] = np.ascontiguousarray(inp["L0_attn_D_param"].reshape(96, 1))
    g["convw"] = np.ascontiguousarray(inp["L0_attn_conv_w"].reshape(96, 3))
    g["convb_col"] = np.ascontiguousarray(inp["L0_attn_conv_b"].reshape(96, 1))
    acols = np.empty((128, 12), np.float32)
    for j in range(12):
        acols[:, j] = A_log[8 * j + pj // 16, pj % 16]
    g["Acols"] = acols
    selD = np.zeros((96, 12 * 128), np.float32)
    for j in range(12):
        selD[8 * j + pj // 16, 128 * j + pj] = 1.0
    g["selD"] = selD
    selB = np.zeros((38, 128), np.float32)
    selC = np.zeros((38, 128), np.float32)
    selB[6 + pj % 16, pj] = 1.0
    selC[22 + pj % 16, pj] = 1.0
    g["selB"] = selB
    g["selC"] = selC
    selR = np.zeros((128, 12 * 96), np.float32)
    for j in range(12):
        selR[pj, 96 * j + (8 * j + pj // 16)] = 1.0
    g["selR"] = selR
    sel8 = np.zeros((8, 8 * 128), np.float32)
    for tt in range(8):
        sel8[tt, 128 * tt:128 * (tt + 1)] = 1.0
    g["sel8"] = sel8
    g["ident"] = np.eye(128, dtype=np.float32)
    g["cconsts"] = np.repeat(np.array([[1e-5, 1.0 / 2048.0, -1.0 / 2048.0, -0.5,
                                        np.float32(1.0) + np.float32(1e-5),
                                        np.float32(1e-5) * np.float32(1e-5)]],
                                      np.float32), 128, axis=0)
    g["ones1x128"] = np.ones((1, 128), np.float32)
    g["ones1x128bf"] = np.ones((1, 128), BF)
    g["ones128bf"] = np.ones((128, 1), BF)
    rw = inp["L0_ffn_router_weight"]
    g["routerT"] = np.ascontiguousarray(
        rw.reshape(8, 16, 128).transpose(2, 1, 0).reshape(128, 128))
    g["Wg"] = np.ascontiguousarray(
        inp["L0_ffn_sh_gate"].reshape(64, 128, 16, 128).transpose(0, 3, 2, 1).reshape(64, 128, 2048))
    g["Wu"] = np.ascontiguousarray(
        inp["L0_ffn_sh_up"].reshape(64, 128, 16, 128).transpose(0, 3, 2, 1).reshape(64, 128, 2048))
    g["Wd"] = np.ascontiguousarray(
        inp["L0_ffn_sh_down"].reshape(16, 128, 64, 128).transpose(0, 3, 2, 1).reshape(16, 128, 8192))
    g["Wrg"] = np.ascontiguousarray(
        inp["L0_ffn_r_gate"].reshape(8, 128, 16, 128).transpose(0, 3, 2, 1).reshape(8, 128, 2048))
    g["Wru"] = np.ascontiguousarray(
        inp["L0_ffn_r_up"].reshape(8, 128, 16, 128).transpose(0, 3, 2, 1).reshape(8, 128, 2048))
    g["Wrd"] = np.ascontiguousarray(
        inp["L0_ffn_r_down"].reshape(16, 128, 8, 128).transpose(0, 3, 2, 1).reshape(16, 128, 1024))
    return g


def _host_percore(x, c):
    b, h = c // 2, c % 2
    xt = x[b].T
    ctx = np.zeros((2048, TP), np.float32)
    mask = np.zeros((1, TP), np.float32)
    if h == 0:
        ctx[:, 1024:2048] = xt[:, 0:1024]
        ctx[:, 2048] = xt[:, 1024]
        mask[0, 1024:2049] = 1.0
    else:
        ctx[:, 0:2048] = xt[:, 0:2048]
        mask[0, 0:2048] = 1.0
    return {"xT": np.ascontiguousarray(ctx),
            "xmask": np.ascontiguousarray(mask.astype(BF))}


def build_program(debug_taps=()):
    global _SEMS
    nc = bass.Bass("TRN2", target_bir_lowering=False, debug=False)

    def din(name, shape, dt=F32):
        return nc.dram_tensor(name, list(shape), dt, kind="ExternalInput").ap()

    xT = din("xT", (2048, TP))
    xmask = din("xmask", (1, TP), BF16)
    lnw_h = din("lnw_cols", (128, 16))
    lnb_h = din("lnb_cols", (128, 16))
    inpT_h = din("inpT", (128, 16 * 96))
    outT_h = din("outT", (96, 2048))
    xprojT_h = din("xprojT", (96, 38))
    dtprojT_h = din("dtprojT", (6, 96))
    dtb_h = din("dtb_col", (96, 1))
    Dcol_h = din("D_col", (96, 1))
    convw_h = din("convw", (96, 3))
    convb_h = din("convb_col", (96, 1))
    Acols_h = din("Acols", (128, 12))
    selD_h = din("selD", (96, 12 * 128))
    selB_h = din("selB", (38, 128))
    selC_h = din("selC", (38, 128))
    selR_h = din("selR", (128, 12 * 96))
    sel8_h = din("sel8", (8, 8 * 128))
    ident_h = din("ident", (128, 128))
    cconsts_h = din("cconsts", (128, 6))
    ones1x128_h = din("ones1x128", (1, 128))
    ones1x128bf_h = din("ones1x128bf", (1, 128), BF16)
    ones128bf_h = din("ones128bf", (128, 1), BF16)
    routerT_h = din("routerT", (128, 128))
    Wg_h = din("Wg", (64, 128, 2048))
    Wu_h = din("Wu", (64, 128, 2048))
    Wd_h = din("Wd", (16, 128, 8192))
    Wrg_h = din("Wrg", (8, 128, 2048))
    Wru_h = din("Wru", (8, 128, 2048))
    Wrd_h = din("Wrd", (16, 128, 1024))

    out_dev = nc.dram_tensor("out", [2048, 1024], F32, kind="ExternalOutput").ap()
    x2_st = nc.dram_tensor("x2_stash", [16, 128, 1024], F32).ap()
    fr_st = nc.dram_tensor("fr_stash", [16, 128, 1024], F32).ap()
    f_st = nc.dram_tensor("f_stash", [16, 128, 1024], F32).ap()

    taps = {}
    for name, shape in debug_taps:
        taps[name] = nc.dram_tensor("dbg_" + name, list(shape), F32,
                                    kind="ExternalOutput").ap()

    BASEA = 16512
    LIMIT = 229344

    def at(name, shape, dt, off):
        per = int(np.prod(shape[1:])) * mybir.dt.size(dt)
        assert BASEA <= off and off + per <= LIMIT, f"sbuf {name} [{off},{off + per})"
        return nc.alloc_sbuf_tensor_at(name, list(shape), dt, offset=off).ap()

    ones1x128 = at("s_o1x", (1, 128), F32, BASEA)
    ones128bf = at("s_o128b", (128, 1), BF16, BASEA + 512)
    cconsts_sb = at("s_cconsts", (128, 6), F32, BASEA + 544)
    for _i, _v in enumerate([EPS, INV_D, -INV_D, -0.5,
                             float(np.float32(1.0) + np.float32(1e-5)),
                             float(np.float32(1e-5) * np.float32(1e-5))]):
        nc.const_aps.register(F32, _v, cconsts_sb[:, _i:_i + 1])
    P_END = BASEA + 1024

    RW0 = P_END
    mask_sb = at("s_mask", (1, TP), BF16, RW0)
    r1own = at("s_r1own", (1, 1024), F32, RW0 + 5120)
    nm1r1bf = at("s_nm1r1b", (1, 1024), BF16, RW0 + 9216)
    comb_bc = at("s_combbc", (128, 1024), F32, RW0)
    gs_t = [at(f"s_gs{i}", (128, 512), BF16, RW0 + 4096 + 1024 * i) for i in range(2)]
    us_t = [at(f"s_us{i}", (128, 512), BF16, RW0 + 6144 + 1024 * i) for i in range(2)]
    RW_END = RW0 + 12 * 1024

    WB0 = RW_END
    gbuf = [at(f"s_gbuf{i}", (128, 2048), BF16, WB0 + 4096 * i) for i in range(2)]
    ubuf = [at(f"s_ubuf{i}", (128, 2048), BF16, WB0 + 8192 + 4096 * i) for i in range(2)]
    xsq_bf = [at(f"s_xsq{i}", (128, 512), BF16, WB0 + 1024 * i) for i in range(2)]
    nmu_bf = [at(f"s_nmub{i}", (1, 512), BF16, WB0 + 2048 + 1024 * i) for i in range(2)]
    du_bf = at("s_dub", (96, 512), BF16, WB0 + 4096)
    m96k = [at(f"s_m96k{i}", (96, 512), F32, WB0 + 5120 + 2048 * i) for i in range(2)]
    rdbuf = [at(f"s_rdbuf{i}", (128, 1024), BF16, WB0 + 2048 * i) for i in range(2)]
    frbuf = [at(f"s_frbuf{i}", (128, 1024), F32, WB0 + 4096 * (1 + i)) for i in range(2)]
    eprows = at("s_eprows", (1, 8 * 512), F32, WB0 + 12288)
    WB_END = WB0 + 16 * 1024

    B1 = WB_END
    o1 = [B1]

    def atb1(name, shape, dt):
        per = int(np.prod(shape[1:])) * mybir.dt.size(dt)
        off = (o1[0] + 31) & ~31
        o1[0] = off + per
        assert o1[0] <= B1 + 64 * 1024, f"B1 overflow {name}"
        return at(name, shape, dt, off)

    xp_all = atb1("s_xp", (96, 2624), F32)
    u_ch = [atb1(f"s_u{i}", (96, 512), F32) for i in range(2)]
    u_bf = atb1("s_ubf", (96, 512), BF16)
    dus_bf = atb1("s_dusb", (96, 512), BF16)
    dbc_bf = atb1("s_dbcb", (38, 512), BF16)
    dA_b = [atb1(f"s_dA{i}", (128, 512), F32) for i in range(2)]
    dBu_b = [atb1(f"s_dBu{i}", (128, 512), F32) for i in range(2)]
    xs_b = [atb1(f"s_xs{i}", (128, 512), F32) for i in range(2)]
    Bp_bc = [atb1(f"s_Bp{i}", (128, 512), F32) for i in range(2)]
    C_bc = [atb1(f"s_C{i}", (128, 512), F32) for i in range(2)]
    rr96_bc = atb1("s_rr96", (96, 512), F32)
    rowsA = atb1("s_rows", (1, 10 * 512), F32)
    bc_a = atb1("s_bca", (128, 512), F32)
    bc_b = atb1("s_bcb", (128, 512), F32)
    prod_b = dA_b
    inpT_f32 = at("s_inpTf", (128, 16 * 96), F32,
                  nc.lookup_mloc(dA_b[0].tensor).addr)
    x_half = at("s_xhalf", (128, 16 * 512), F32,
                nc.lookup_mloc(xp_all.tensor).addr)
    assert (nc.lookup_mloc(rowsA.tensor).addr
            >= nc.lookup_mloc(x_half.tensor).addr + 32 * 1024), "x_half overlaps rows"
    rh_tmp = at("s_rhtmp", (128, 1024), BF16, B1)
    r_h_sb = at("s_rh", (128, 8 * 1024), BF16, B1 + 2048)
    frstage = [at(f"s_frstg{i}", (128, 1024), F32, B1 + 18432 + 4096 * i) for i in range(2)]

    B2 = B1 + 64 * 1024
    h_all = at("s_h", (128, 64 * 1024), BF16, B1)
    o2 = [B2 + 4096]

    def atb2(name, shape, dt):
        per = int(np.prod(shape[1:])) * mybir.dt.size(dt)
        off = (o2[0] + 31) & ~31
        o2[0] = off + per
        assert o2[0] <= B2 + 64 * 1024, f"B2 overflow {name}"
        return at(name, shape, dt, off)

    ident = atb2("s_ident", (128, 128), F32)
    ones1x128bf = atb2("s_o1xb", (1, 128), BF16)
    routerTbf = atb2("s_routerTb", (128, 128), BF16)
    xprojTbf = atb2("s_xprojTb", (96, 38), BF16)
    dtprojTbf = atb2("s_dtprojTb", (6, 96), BF16)
    lnw_sb = atb2("s_lnw", (128, 16), F32)
    lnb_sb = atb2("s_lnb", (128, 16), F32)
    dtb_sb = atb2("s_dtb", (96, 1), F32)
    D_sb = atb2("s_D", (96, 1), F32)
    convw_sb = atb2("s_convw", (96, 3), F32)
    convb_sb = atb2("s_convb", (96, 1), F32)
    nconvb_sb = atb2("s_nconvb", (96, 1), F32)
    c0_sb = atb2("s_c0", (96, 1), F32)
    Acols_sb = atb2("s_Acols", (128, 12), F32)
    A2 = atb2("s_A2", (128, 12), F32)
    state_cols = atb2("s_state", (128, 12), F32)
    wpsum_bf = atb2("s_wpsum", (1, 96), BF16)
    sel8_sb = atb2("s_sel8", (8, 8 * 128), F32)
    cmb8 = atb2("s_cmb8", (8, 128), F32)
    outTbf = atb2("s_outTb", (96, 2048), BF16)
    inpTbf = atb2("s_inpTb", (128, 16 * 96), BF16)
    selDbf = atb2("s_selDb", (96, 12 * 128), BF16)
    selBbf = atb2("s_selBb", (38, 128), BF16)
    selCbf = atb2("s_selCb", (38, 128), BF16)
    selR_sb = atb2("s_selR", (128, 12 * 96), F32)
    y_f = atb2("s_yf", (96, 1024), F32)
    y_bf = atb2("s_ybf", (96, 1024), BF16)
    lgt_sb = atb2("s_lgt", (8, 1024), F32)
    lgt_tm = atb2("s_lgttm", (128, 64), F32)
    rt_m1 = atb2("s_rtm1", (128, 8), F32)
    rt_t = atb2("s_rtt", (128, 8), F32)
    rt_t2 = atb2("s_rtt2", (128, 8), F32)
    rows4 = atb2("s_rows4", (1, 4 * 512), F32)
    cast_bf = [atb2(f"s_castb{i}", (128, 512), BF16) for i in range(2)]
    cast_bf2 = [atb2(f"s_castb2{i}", (128, 512), BF16) for i in range(2)]
    x2t = [atb2(f"s_x2t{i}", (128, 512), F32) for i in range(3)]
    bc_b2 = atb2("s_bcb2", (128, 512), F32)
    comb_tm = atb2("s_combtm", (128, 8), F32)

    B3 = B2 + 64 * 1024
    assert B3 + 32 * 1024 <= LIMIT, f"arena end {B3 + 32 * 1024} > {LIMIT}"
    xcb = [at(f"s_xcb{i}", (128, 16 * 512), BF16, B3 + 16384 * i) for i in range(2)]
    mr_sb = at("s_mr", (128, 16 * 512), F32, B3)
    z4_sb = at("s_z4", (128, 16 * 1024), BF16, B3)
    dbuf = [at(f"s_dbuf{i}", (128, 4096), BF16, B3 + 8192 * i) for i in range(2)]
    f_stage = [at(f"s_fstg{i}", (128, 1024), F32, B3 + 16384 + 4096 * i) for i in range(2)]
    f_bf = at("s_fbf", (128, 1024), BF16, B3 + 24576)
    f2_bf = at("s_f2bf", (128, 1024), BF16, B3 + 26624)
    ep_x2 = [at(f"s_epx2{i}", (128, 1024), F32, B3 + 4096 * i) for i in range(3)]
    ep_f = [at(f"s_epf{i}", (128, 1024), F32, B3 + 12288 + 4096 * i) for i in range(3)]
    ep_o = [at(f"s_epo{i}", (128, 1024), F32, B3 + 24576 + 2048 * 0 + 4096 * i) for i in range(2)]

    pb = [nc.alloc_psum_tensor(f"pb{i}", [128, 512], F32).ap() for i in range(8)]

    es = contextlib.ExitStack()
    _SEMS = {n: es.enter_context(nc.semaphore(n)) for n in ALL_SEMS}
    S = Sched()

    def row(i):
        return rowsA[:, 512 * i:512 * (i + 1)]

    def eprow(i):
        return eprows[:, 512 * i:512 * (i + 1)]

    r_pb = [Res() for _ in range(8)]
    r_xcb = [Res(), Res()]
    r_xsq = [Res(), Res()]
    r_nmu = [Res(), Res()]
    r_rows = [Res() for _ in range(10)]
    r_rows4 = [Res() for _ in range(4)]
    r_eprows = [Res() for _ in range(8)]
    r_r1own = Res()
    r_nm1r1 = Res()
    r_xp = [Res() for _ in range(NCHUNK)]
    r_u = [Res(), Res()]
    r_ubf = Res()
    r_dus = Res()
    r_dbc = Res()
    r_dub = Res()
    r_m96 = [Res(), Res()]
    r_rr96 = Res()
    r_bca = Res()
    r_bcb = Res()
    r_bcb2 = Res()
    r_dA = [Res(), Res()]
    r_dBu = [Res(), Res()]
    r_xs = [Res(), Res()]
    r_Bp = [Res(), Res()]
    r_C = [Res(), Res()]
    r_state = [Res() for _ in range(NJ)]
    r_yf = Res()
    r_ybf = Res()
    r_xhalf = Res()
    r_mr = Res()
    r_x2t = [Res(), Res(), Res()]
    r_x2st = [Res() for _ in range(KC)]
    r_cast = [Res(), Res()]
    r_cast2 = [Res(), Res()]
    r_zs = Res()
    r_z4 = Res()
    r_lgt = Res()
    r_lgttm = Res()
    r_rt = Res()
    r_cmb8 = Res()
    r_combtm = Res()
    r_combbc = Res()
    r_gbuf = [Res(), Res()]
    r_ubuf = [Res(), Res()]
    r_gs = [Res(), Res()]
    r_us = [Res(), Res()]
    r_rhtmp = Res()
    r_rh = [Res() for _ in range(NRI)]
    r_frstage = [Res(), Res()]
    r_frst = [Res() for _ in range(KC)]
    r_h = [Res() for _ in range(NI)]
    r_dbuf = [Res(), Res()]
    r_rdbuf = [Res(), Res()]
    r_frbuf = [Res(), Res()]
    r_fstage = [Res(), Res()]
    r_fst = [Res() for _ in range(KC)]
    r_fbf = Res()
    r_f2bf = Res()
    r_epx2 = [Res(), Res(), Res()]
    r_epf = [Res(), Res(), Res()]
    r_epo = [Res(), Res()]
    r_misc = Res()

    out_toks = []

    def tap(name, ap_src, waits, reads=(), q="sync"):
        if name in taps:
            out_toks.append(S.dma(q,
                                  lambda e, a=ap_src, d=taps[name]: e.dma_start(out=d, in_=a),
                                  reads=reads, waits=waits))

    S.phase = "prolog"
    const_toks = []
    for dst, src in [(ident, ident_h), (ones1x128, ones1x128_h),
                     (ones1x128bf, ones1x128bf_h), (ones128bf, ones128bf_h),
                     (lnw_sb, lnw_h), (lnb_sb, lnb_h), (dtb_sb, dtb_h),
                     (D_sb, Dcol_h), (convw_sb, convw_h), (convb_sb, convb_h),
                     (Acols_sb, Acols_h), (selR_sb, selR_h), (inpT_f32, inpT_h),
                     (mask_sb, xmask), (sel8_sb, sel8_h),
                     (cconsts_sb, cconsts_h)]:
        const_toks.append(S.dma("sync", lambda e, d=dst, s=src: e.dma_start(out=d, in_=s)))
    for dst, src in [(outTbf, outT_h), (xprojTbf, xprojT_h), (dtprojTbf, dtprojT_h),
                     (selDbf, selD_h), (selBbf, selB_h), (selCbf, selC_h),
                     (routerTbf, routerT_h)]:
        const_toks.append(S.dma("gpsimd", lambda e, d=dst, s=src: e.dma_start(out=d, in_=s)))
    CW = const_toks

    t_xpad = S.op("vector", lambda e: e.memset(xp_all[0:96, 0:1], 0.0))
    tk = S.op("scalar", lambda e: e.activation(A2[:], Acols_sb[:], AF.Exp), waits=CW)
    t_A2 = S.op("vector", lambda e: e.tensor_scalar_mul(A2[:], A2[:], -0.001), waits=[tk])
    t_nconvb = S.op("scalar", lambda e: e.activation(nconvb_sb[:], convb_sb[:], AF.Copy,
                                                     scale=-1.0), waits=CW)

    def mm_c0(e):
        for k in range(KC):
            i = e.matmul(pb[5][0:96, 0:1], inpT_f32[:, 96 * k:96 * (k + 1)],
                         lnb_sb[:, k:k + 1], start=(k == 0), stop=(k == KC - 1))
        return i
    tk = S.op("tensor", mm_c0, writes=[r_pb[5]], waits=CW)
    t_c0 = S.op("scalar", lambda e: e.activation(c0_sb[:], pb[5][0:96, 0:1], AF.Copy),
                reads=[r_pb[5]], waits=[tk])

    def mk_inpbf(e):
        for k in range(KC):
            i = e.activation(inpTbf[:, 96 * k:96 * (k + 1)], inpT_f32[:, 96 * k:96 * (k + 1)],
                             AF.Copy, scale=lnw_sb[:, k:k + 1])
        return i
    t_inpbf = S.op("scalar", mk_inpbf, waits=CW)
    for rr_ in (r_dA[0], r_dA[1], r_dBu[0]):
        rr_.w = t_inpbf

    def mm_wpsum(e):
        for k in range(KC):
            i = e.matmul(pb[6][0:1, 0:96], ones128bf[:], inpTbf[:, 96 * k:96 * (k + 1)],
                         start=(k == 0), stop=(k == KC - 1))
        return i
    tk = S.op("tensor", mm_wpsum, writes=[r_pb[6]], waits=[t_inpbf] + CW)
    t_wpsum = S.op("scalar", lambda e: e.activation(wpsum_bf[:], pb[6][0:1, 0:96], AF.Copy),
                   reads=[r_pb[6]], waits=[tk])

    S.phase = "mixer"
    x3 = xT.rearrange("(k p) t -> p k t", p=128)
    t_xbf = [None] * NCHUNK

    def load_chunk(c):
        cc = c % 2
        dst = xcb[cc].rearrange("p (k t) -> p k t", t=512)
        t_xbf[c] = S.dma("gpsimd",
                         lambda e, d=dst, c=c: e.dma_start(out=d,
                                                           in_=x3[:, :, 512 * c:512 * (c + 1)]),
                         writes=[r_xcb[cc]])

    load_chunk(0)
    load_chunk(1)
    R_MU, R_M2, R_V, R_T1, R_R1, R_RR, R_NMU, R_T2 = range(8)
    tok_xpc = [None] * NCHUNK

    for c in range(NCHUNK):
        cc = c % 2
        S.phase = "mstat"
        def mm_sx(e, cc=cc):
            for k in range(KC):
                i = e.matmul(pb[0][0:1, :], ones128bf[:], xcb[cc][:, 512 * k:512 * (k + 1)],
                             start=(k == 0), stop=(k == KC - 1))
            return i
        t_sx = S.op("tensor", mm_sx, reads=[r_xcb[cc]], writes=[r_pb[0]],
                    waits=[t_xbf[c]] + CW)
        for k in range(KC):
            kk = k % 2
            tsq = S.op("scalar", lambda e, k=k, kk=kk, cc=cc: e.activation(
                           xsq_bf[kk][:], xcb[cc][:, 512 * k:512 * (k + 1)], AF.Square),
                       reads=[r_xcb[cc]], writes=[r_xsq[kk]], waits=[t_xbf[c]])
            tmm = S.op("tensor", lambda e, k=k, kk=kk: e.matmul(
                           pb[1][0:1, :], ones128bf[:], xsq_bf[kk][:],
                           start=(k == 0), stop=(k == KC - 1)),
                       reads=[r_xsq[kk]], writes=[r_pb[1]] if k == 0 else [], waits=[tsq])
            if k > 0:
                r_pb[1].w = tmm
        t_sx2 = tmm

        tA = S.op("scalar", lambda e: e.activation(row(R_MU), pb[0][0:1, :], AF.Copy,
                                                   scale=INV_D),
                  reads=[r_pb[0]], writes=[r_rows[R_MU]], waits=[t_sx])
        tB = S.op("scalar", lambda e: e.activation(row(R_M2), pb[1][0:1, :], AF.Copy,
                                                   scale=INV_D),
                  reads=[r_pb[1]], writes=[r_rows[R_M2]], waits=[t_sx2])
        tL = S.op("scalar", lambda e: e.activation(row(R_NMU), pb[0][0:1, :], AF.Copy,
                                                   scale=-INV_D),
                  reads=[r_pb[0]], writes=[r_rows[R_NMU]], waits=[t_sx])
        tC = S.op("vector", lambda e: e.tensor_mul(row(R_T1), row(R_MU), row(R_MU)),
                  reads=[r_rows[R_MU]], writes=[r_rows[R_T1]], waits=[tA])
        tD = S.op("vector", lambda e: e.tensor_sub(row(R_V), row(R_M2), row(R_T1)),
                  reads=[r_rows[R_M2], r_rows[R_T1]], writes=[r_rows[R_V]], waits=[tB])
        tE = S.op("scalar", lambda e: e.activation(row(R_T1), row(R_V), AF.Ln,
                                                   scale=float(np.float32(1.0) + np.float32(1e-5)),
                                                   bias=float(np.float32(1e-5) * np.float32(1e-5))),
                  reads=[r_rows[R_V]], writes=[r_rows[R_T1]], waits=[tD])
        tK = S.op("scalar", lambda e: e.activation(row(R_RR), row(R_T1), AF.Exp, scale=-0.5),
                  reads=[r_rows[R_T1]], writes=[r_rows[R_RR]], waits=[tE])
        tM = S.op("scalar", lambda e, cc=cc: e.activation(nmu_bf[cc][:], row(R_NMU), AF.Copy),
                  reads=[r_rows[R_NMU]], writes=[r_nmu[cc]], waits=[tL])
        if c in (2, 3):
            off = 512 * (c - 2)
            tE2 = S.op("scalar", lambda e: e.activation(row(R_T2), row(R_V), AF.Ln, bias=EPS),
                       reads=[r_rows[R_V]], writes=[r_rows[R_T2]], waits=[tD])
            tF = S.op("scalar", lambda e, off=off: e.activation(
                          r1own[:, off:off + 512], row(R_T2), AF.Exp, scale=-0.5),
                      reads=[r_rows[R_T2]], writes=[r_r1own] if c == 3 else [], waits=[tE2])
            S.op("vector", lambda e, off=off: e.scalar_tensor_tensor(
                     nm1r1bf[:, off:off + 512], row(R_MU), -1.0, r1own[:, off:off + 512],
                     ALU.mult, ALU.mult),
                 reads=[r_rows[R_MU], r_r1own], writes=[r_nm1r1] if c == 3 else [],
                 waits=[tF])

        def mm_ip(e, cc=cc):
            for k in range(KC):
                e.matmul(pb[2][0:96, :], inpTbf[:, 96 * k:96 * (k + 1)],
                         xcb[cc][:, 512 * k:512 * (k + 1)], start=(k == 0), stop=False)
            return e.matmul(pb[2][0:96, :], wpsum_bf[:], nmu_bf[cc][:], start=False, stop=True)
        t_ip = S.op("tensor", mm_ip, reads=[r_xcb[cc], r_nmu[cc]], writes=[r_pb[2]],
                    waits=[t_xbf[c], t_inpbf, t_wpsum, tM])
        if c + 2 < NCHUNK:
            load_chunk(c + 2)

        t_rr = S.op("tensor", lambda e: e.matmul(pb[3][0:96, :], ones1x128[0:1, 0:96],
                                                 row(R_RR), start=True, stop=True),
                    reads=[r_rows[R_RR]], writes=[r_pb[3]], waits=[tK] + CW)
        t_rre = S.op("scalar", lambda e: e.activation(rr96_bc[:], pb[3][0:96, :], AF.Copy),
                     reads=[r_pb[3]], writes=[r_rr96], waits=[t_rr])
        t_mk = S.op("tensor", lambda e, c=c: e.matmul(pb[4][0:96, :], ones1x128bf[0:1, 0:96],
                                                      mask_sb[:, 512 * c:512 * (c + 1)],
                                                      start=True, stop=True),
                    writes=[r_pb[4]], waits=CW)
        t_mke = S.op("scalar", lambda e, cc=cc: e.activation(m96k[cc][:], pb[4][0:96, :],
                                                             AF.Copy),
                     reads=[r_pb[4]], writes=[r_m96[cc]], waits=[t_mk])
        if c == 2:
            tap("mu2", rowsA[:, 512 * R_MU:512 * (R_MU + 1)], [tA],
                reads=[r_rows[R_MU]])
            tap("rr2", rr96_bc[0:96, :], [t_rre], reads=[r_rr96])

        xpc = xp_all[0:96, 1 + 512 * c:1 + 512 * (c + 1)]
        t1_ = S.op("vector", lambda e, xpc=xpc: e.tensor_mul(xpc, pb[2][0:96, :], rr96_bc[:]),
                   reads=[r_pb[2], r_rr96], writes=[r_xp[c]], waits=[t_ip, t_rre])
        if c == 2:
            tap("xpraw2", xp_all[0:96, 1 + 512 * c:1 + 512 * (c + 1)], [t1_],
                reads=[r_xp[c]])
        t2_ = S.op("vector", lambda e, xpc=xpc, cc=cc: e.tensor_mul(xpc, xpc, m96k[cc][:]),
                   reads=[r_m96[cc]], writes=[r_xp[c]], waits=[t1_, t_mke])
        t3_ = S.op("vector", lambda e, xpc=xpc, cc=cc: e.scalar_tensor_tensor(
                       xpc, m96k[cc][:], c0_sb[:], xpc, ALU.mult, ALU.add),
                   reads=[r_m96[cc]], writes=[r_xp[c]], waits=[t2_, t_c0])
        tok_xpc[c] = t3_
        tap(f"xp{c}", xp_all[0:96, 1 + 512 * c:1 + 512 * (c + 1)], [t3_],
            reads=[r_xp[c]])

        S.phase = "postxp"
        pc = c - 1
        if pc < 0:
            continue
        pcc = pc % 2
        upre = u_ch[pcc][:, :]
        x_l = xp_all[0:96, 512 * pc + 0:512 * pc + 512]
        x_m = xp_all[0:96, 512 * pc + 1:512 * pc + 513]
        x_r = xp_all[0:96, 512 * pc + 2:512 * pc + 514]
        ta = S.op("vector", lambda e, x_l=x_l, upre=upre: e.tensor_scalar(
                      upre, x_l, convw_sb[:, 0:1], None, ALU.mult),
                  reads=[r_xp[pc]], writes=[r_u[pcc]], waits=[tok_xpc[pc], t_xpad] + CW)
        tb = S.op("vector", lambda e, x_m=x_m, upre=upre: e.scalar_tensor_tensor(
                      upre, x_m, convw_sb[:, 1:2], upre, ALU.mult, ALU.add),
                  reads=[r_xp[pc]], writes=[r_u[pcc]], waits=[ta])
        tc_ = S.op("vector", lambda e, x_r=x_r, upre=upre: e.scalar_tensor_tensor(
                       upre, x_r, convw_sb[:, 2:3], upre, ALU.mult, ALU.add),
                   reads=[r_xp[pc], r_xp[c]], writes=[r_u[pcc]], waits=[tb, tok_xpc[c]])
        td = S.op("scalar", lambda e, upre=upre: e.activation(
                      bc_a[0:96, :], upre, AF.Exp, scale=-1.0, bias=nconvb_sb[:]),
                  reads=[r_u[pcc]], writes=[r_bca], waits=[tc_, t_nconvb])
        te = S.op("vector", lambda e: e.tensor_scalar_add(bc_a[0:96, :], bc_a[0:96, :], 1.0),
                  writes=[r_bca], waits=[td])
        tf_ = S.op("vector", lambda e: e.reciprocal(bc_b[0:96, :], bc_a[0:96, :]),
                   reads=[r_bca], writes=[r_bcb], waits=[te])
        th = S.op("vector", lambda e, upre=upre: e.scalar_tensor_tensor(
                      upre, upre, convb_sb[:], bc_b[0:96, :], ALU.add, ALU.mult),
                  reads=[r_bcb], writes=[r_u[pcc]], waits=[tf_, td])
        ti = S.op("scalar", lambda e, upre=upre: e.activation(u_bf[:], upre, AF.Copy),
                  reads=[r_u[pcc]], writes=[r_ubf], waits=[th])
        tap(f"u{pc}", u_ch[pcc][0:96, :], [th], reads=[r_u[pcc]])

        tj = S.op("tensor", lambda e: e.matmul(pb[5][0:38, :], xprojTbf[:], u_bf[:],
                                               start=True, stop=True),
                  reads=[r_ubf], writes=[r_pb[5]], waits=[ti])
        tk_ = S.op("scalar", lambda e: e.activation(dbc_bf[:], pb[5][0:38, :], AF.Copy),
                   reads=[r_pb[5]], writes=[r_dbc], waits=[tj])
        tl = S.op("tensor", lambda e: e.matmul(pb[5][0:96, :], dtprojTbf[:], dbc_bf[0:6, :],
                                               start=True, stop=True),
                  reads=[r_dbc], writes=[r_pb[5]], waits=[tk_])
        tm_ = S.op("scalar", lambda e: e.activation(bc_a[0:96, :], pb[5][0:96, :], AF.Abs,
                                                    bias=dtb_sb[:]),
                   reads=[r_pb[5]], writes=[r_bca], waits=[tl])
        tn = S.op("scalar", lambda e: e.activation(bc_a[0:96, :], bc_a[0:96, :], AF.Exp,
                                                   scale=-1.0),
                  writes=[r_bca], waits=[tm_])
        to_ = S.op("scalar", lambda e: e.activation(bc_a[0:96, :], bc_a[0:96, :], AF.Ln,
                                                    bias=1.0),
                   writes=[r_bca], waits=[tn])
        tp_ = S.op("scalar", lambda e: e.activation(bc_b[0:96, :], pb[5][0:96, :], AF.Relu,
                                                    bias=dtb_sb[:]),
                   reads=[r_pb[5]], writes=[r_bcb], waits=[tm_])
        tq = S.op("vector", lambda e: e.tensor_add(dus_bf[:], bc_b[0:96, :], bc_a[0:96, :]),
                  reads=[r_bca, r_bcb], writes=[r_dus], waits=[to_, tp_])
        tr_ = S.op("vector", lambda e, pcc=pcc: e.scalar_tensor_tensor(
                       du_bf[:], dus_bf[:], 0.001, u_ch[pcc][:, :], ALU.mult, ALU.mult),
                   reads=[r_dus, r_u[pcc]], writes=[r_dub], waits=[tq])
        ts_ = S.op("vector", lambda e, pcc=pcc: e.tensor_mul(du_bf[:], du_bf[:], m96k[pcc][:]),
                   reads=[r_m96[pcc]], writes=[r_dub], waits=[tr_])

        tt = S.op("tensor", lambda e: e.matmul(pb[6][:, :], selBbf[:], dbc_bf[:],
                                               start=True, stop=True),
                  reads=[r_dbc], writes=[r_pb[6]], waits=[tk_] + CW)
        tu_ = S.op("scalar", lambda e, pcc=pcc: e.activation(Bp_bc[pcc][:], pb[6][:, :],
                                                             AF.Copy),
                   reads=[r_pb[6]], writes=[r_Bp[pcc]], waits=[tt])
        if pc >= 2:
            tv = S.op("tensor", lambda e: e.matmul(pb[6][:, :], selCbf[:], dbc_bf[:],
                                                   start=True, stop=True),
                      reads=[r_dbc], writes=[r_pb[6]], waits=[tu_])
            tw_ = S.op("scalar", lambda e, pcc=pcc: e.activation(C_bc[pcc][:], pb[6][:, :],
                                                                 AF.Copy),
                       reads=[r_pb[6]], writes=[r_C[pcc]], waits=[tv])

        y_last = None
        S.phase = "jloop"
        for j in range(NJ):
            jb = j % 2
            bs = 3 if jb == 0 else 6
            bs2 = 4 if jb == 0 else 7
            tj1 = S.op("tensor", lambda e, j=j, bs=bs: e.matmul(
                           pb[bs][:, :], selDbf[:, 128 * j:128 * (j + 1)], dus_bf[:],
                           start=True, stop=True),
                       reads=[r_dus], writes=[r_pb[bs]], waits=[tq])
            tj2 = S.op("scalar", lambda e, j=j, jb=jb, bs=bs: e.activation(
                           dA_b[jb][:], pb[bs][:, :], AF.Exp, scale=A2[:, j:j + 1]),
                       reads=[r_pb[bs]], writes=[r_dA[jb]], waits=[tj1, t_A2])
            tj3 = S.op("tensor", lambda e, j=j, bs2=bs2: e.matmul(
                           pb[bs2][:, :], selDbf[:, 128 * j:128 * (j + 1)], du_bf[:],
                           start=True, stop=True),
                       reads=[r_dub], writes=[r_pb[bs2]], waits=[ts_])
            tj4 = S.op("vector", lambda e, jb=jb, pcc=pcc, bs2=bs2: e.tensor_mul(
                           dBu_b[jb][:], pb[bs2][:, :], Bp_bc[pcc][:]),
                       reads=[r_pb[bs2], r_Bp[pcc]], writes=[r_dBu[jb]], waits=[tj3, tu_])
            init = 0.0 if pc == 0 else state_cols[:, j:j + 1]
            tj5 = S.op("vector", lambda e, jb=jb, init=init: e.tensor_tensor_scan(
                           xs_b[jb][:], dA_b[jb][:], dBu_b[jb][:], init, ALU.mult, ALU.add),
                       reads=[r_dA[jb], r_dBu[jb]] + ([r_state[j]] if pc > 0 else []),
                       writes=[r_xs[jb]], waits=[tj2, tj4])
            tj6 = S.op("vector", lambda e, j=j, jb=jb: e.tensor_copy(
                           state_cols[:, j:j + 1], xs_b[jb][:, 511:512]),
                       reads=[r_xs[jb]], writes=[r_state[j]], waits=[tj5])
            if pc >= 2:
                tj7 = S.op("vector", lambda e, jb=jb, pcc=pcc: e.tensor_mul(
                               prod_b[jb][:], xs_b[jb][:], C_bc[pcc][:]),
                           reads=[r_xs[jb], r_C[pcc]], writes=[r_dA[jb]], waits=[tw_, tj6])
                tj8 = S.op("tensor", lambda e, j=j, jb=jb: e.matmul(
                               pb[2][0:96, :], selR_sb[:, 96 * j:96 * (j + 1)], prod_b[jb][:],
                               start=(j == 0), stop=(j == NJ - 1)),
                           reads=[r_dA[jb]], writes=[r_pb[2]] if j == 0 else [],
                           waits=[tj7] + CW)
                if j > 0:
                    r_pb[2].w = tj8
                y_last = tj8

        if pc >= 2:
            off = 512 * (pc - 2)
            ty1 = S.op("vector", lambda e, off=off, pcc=pcc: e.scalar_tensor_tensor(
                           y_f[:, off:off + 512], u_ch[pcc][:, :], D_sb[:],
                           pb[2][0:96, :], ALU.mult, ALU.add),
                       reads=[r_pb[2], r_u[pcc]], writes=[r_yf], waits=[y_last] + CW)
            S.op("scalar", lambda e, off=off: e.activation(
                     y_bf[:, off:off + 512], y_f[:, off:off + 512], AF.Copy),
                 reads=[r_yf], writes=[r_ybf], waits=[ty1])
    tap("yf", y_f[0:96, :], [r_ybf.w], reads=[r_yf])

    S.phase = "O"
    S.fence()

    def stats_tile(src_f32, first, last, wtok, par=0):
        cb = cast_bf if par == 0 else cast_bf2
        rc = r_cast if par == 0 else r_cast2
        tca = S.op("scalar", lambda e, s=src_f32, cb=cb: e.activation(cb[0][:], s, AF.Copy),
                   writes=[rc[0]], waits=wtok)
        tcb = S.op("scalar", lambda e, s=src_f32, cb=cb: e.activation(cb[1][:], s, AF.Square),
                   writes=[rc[1]], waits=wtok)
        ts1 = S.op("tensor", lambda e, first=first, last=last, cb=cb: e.matmul(
                       pb[0][0:1, :], ones128bf[:], cb[0][:], start=first, stop=last),
                   reads=[rc[0]], writes=[r_pb[0]] if first else [], waits=[tca] + CW)
        if not first:
            r_pb[0].w = ts1
        ts2 = S.op("tensor", lambda e, first=first, last=last, cb=cb: e.matmul(
                       pb[1][0:1, :], ones128bf[:], cb[1][:], start=first, stop=last),
                   reads=[rc[1]], writes=[r_pb[1]] if first else [], waits=[tcb] + CW)
        if not first:
            r_pb[1].w = ts2
        return ts1, ts2

    def rstd_rows(dst_r_res, dst_r, dst_nm_res, dst_nm, rowset):
        mu, m2, v, t1 = rowset
        tA = S.op("scalar", lambda e: e.activation(mu, pb[0][0:1, :], AF.Copy, scale=INV_D),
                  reads=[r_pb[0]], writes=[r_rows[R_MU]])
        tB = S.op("scalar", lambda e: e.activation(m2, pb[1][0:1, :], AF.Copy, scale=INV_D),
                  reads=[r_pb[1]], writes=[r_rows[R_M2]])
        tC = S.op("vector", lambda e: e.tensor_mul(t1, mu, mu),
                  reads=[r_rows[R_MU]], writes=[r_rows[R_T1]], waits=[tA])
        tD = S.op("vector", lambda e: e.tensor_sub(v, m2, t1),
                  reads=[r_rows[R_M2], r_rows[R_T1]], writes=[r_rows[R_V]], waits=[tB, tC])
        tE = S.op("scalar", lambda e: e.activation(t1, v, AF.Ln, bias=EPS),
                  reads=[r_rows[R_V]], writes=[r_rows[R_T1]], waits=[tD])
        tF = S.op("scalar", lambda e: e.activation(dst_r, t1, AF.Exp, scale=-0.5),
                  reads=[r_rows[R_T1]], writes=[dst_r_res], waits=[tE])
        tG = S.op("vector", lambda e: e.scalar_tensor_tensor(dst_nm, mu, -1.0, dst_r,
                                                             ALU.mult, ALU.mult),
                  reads=[r_rows[R_MU], dst_r_res], writes=[dst_nm_res], waits=[tF])
        return tF, tG

    def bcast(dstbuf, dst_res, src_row, src_res, wtoks, bank, npart=128):
        tb1 = S.op("tensor", lambda e: e.matmul(pb[bank][0:npart, :], ones1x128[0:1, 0:npart],
                                                src_row, start=True, stop=True),
                   reads=[src_res], writes=[r_pb[bank]], waits=wtoks + CW)
        tb2 = S.op("scalar", lambda e: e.activation(dstbuf[0:npart, :], pb[bank][0:npart, :],
                                                    AF.Copy),
                   reads=[r_pb[bank]], writes=[dst_res], waits=[tb1])
        return tb2

    xh3 = x_half.rearrange("p (k t) -> p k t", t=512)
    ln4_toks = []

    for hc in range(2):
        off = 512 * hc
        t_xh = S.dma("sync", lambda e, off=off: e.dma_start(
                         out=xh3[:, :, :], in_=x3[:, :, 1024 + off:1024 + off + 512]),
                     writes=[r_xhalf])
        t_r1b = bcast(bc_a, r_bca, r1own[:, off:off + 512], r_r1own, [], 3)
        for mc in range(KC):
            bank = 2 if mc % 2 == 0 else 5

            def mm_op1(e, mc=mc, off=off, bank=bank):
                e.matmul(pb[bank][:, :], outTbf[:, 128 * mc:128 * (mc + 1)],
                         y_bf[:, off:off + 512], start=True, stop=False)
                return e.matmul(pb[bank][:, :], ones1x128bf[:], nm1r1bf[:, off:off + 512],
                                start=False, stop=True)
            t_op = S.op("tensor", mm_op1, reads=[r_ybf, r_nm1r1], writes=[r_pb[bank]],
                        waits=CW)
            bcb_i = bc_b if mc % 2 == 0 else bc_b2
            r_bcb_i = r_bcb if mc % 2 == 0 else r_bcb2
            t_s = S.op("vector", lambda e, mc=mc, bcb_i=bcb_i: e.tensor_mul(
                           bcb_i[:, :], x_half[:, 512 * mc:512 * (mc + 1)], bc_a[:, :]),
                       reads=[r_xhalf, r_bca], writes=[r_bcb_i], waits=[t_xh, t_r1b])
            t_mr = S.op("vector", lambda e, mc=mc, bank=bank, bcb_i=bcb_i: e.tensor_add(
                            mr_sb[:, 512 * mc:512 * (mc + 1)], pb[bank][:, :], bcb_i[:, :]),
                        reads=[r_pb[bank], r_bcb_i], writes=[r_mr] if mc == 0 else [],
                        waits=[t_op, t_s])
            if mc > 0:
                r_mr.w = t_mr
            stats_tile(mr_sb[:, 512 * mc:512 * (mc + 1)], mc == 0, mc == KC - 1, [t_mr],
                       par=mc % 2)
        tap(f"mr{hc}", mr_sb[:, :], [r_mr.w], reads=[r_mr])
        t_r3, t_nm3 = rstd_rows(r_rows[R_R1], row(R_R1), r_rows[R_T2], row(R_T2),
                                (row(R_MU), row(R_M2), row(R_V), row(R_T1)))
        t_r3b = bcast(bc_a, r_bca, row(R_R1), r_rows[R_R1], [t_r3], 3)
        t_nm3b = bcast(bc_b, r_bcb, row(R_T2), r_rows[R_T2], [t_nm3], 4)
        for k in range(KC):
            kk = k % 2
            tx1 = S.op("vector", lambda e, k=k: e.tensor_mul(
                           mr_sb[:, 512 * k:512 * (k + 1)], mr_sb[:, 512 * k:512 * (k + 1)],
                           bc_a[:, :]),
                       reads=[r_bca], writes=[r_mr], waits=[t_r3b])
            tx2 = S.op("gpsimd", lambda e, k=k, kk=kk: e.tensor_add(
                           x2t[kk][:], x_half[:, 512 * k:512 * (k + 1)], bc_b[:, :]),
                       reads=[r_bcb, r_xhalf], writes=[r_x2t[kk]], waits=[t_nm3b, t_xh])
            tx3 = S.op("vector", lambda e, k=k, kk=kk: e.tensor_add(
                           x2t[kk][:], mr_sb[:, 512 * k:512 * (k + 1)], x2t[kk][:]),
                       reads=[r_mr], writes=[r_x2t[kk]], waits=[tx1, tx2])
            tst = S.dma("scalar", lambda e, k=k, kk=kk, off=off: e.dma_start(
                            out=x2_st[k, :, off:off + 512], in_=x2t[kk][:]),
                        reads=[r_x2t[kk]], waits=[tx3])
            r_x2st[k] = tst if hc == 0 else tst
            cb4 = cast_bf if k % 2 == 0 else cast_bf2
            rc4 = r_cast if k % 2 == 0 else r_cast2
            tca = S.op("scalar", lambda e, kk=kk, cb4=cb4: e.activation(
                           cb4[0][:], x2t[kk][:], AF.Copy),
                       reads=[r_x2t[kk]], writes=[rc4[0]], waits=[tx3])
            tcb = S.op("scalar", lambda e, kk=kk, cb4=cb4: e.activation(
                           cb4[1][:], x2t[kk][:], AF.Square),
                       reads=[r_x2t[kk]], writes=[rc4[1]], waits=[tx3])
            ts1 = S.op("tensor", lambda e, k=k, cb4=cb4: e.matmul(
                           pb[6][0:1, :], ones128bf[:], cb4[0][:],
                           start=(k == 0), stop=(k == KC - 1)),
                       reads=[rc4[0]], writes=[r_pb[6]] if k == 0 else [], waits=[tca] + CW)
            if k > 0:
                r_pb[6].w = ts1
            ts2 = S.op("tensor", lambda e, k=k, cb4=cb4: e.matmul(
                           pb[7][0:1, :], ones128bf[:], cb4[1][:],
                           start=(k == 0), stop=(k == KC - 1)),
                       reads=[rc4[1]], writes=[r_pb[7]] if k == 0 else [], waits=[tcb] + CW)
            if k > 0:
                r_pb[7].w = ts2
        def row4(i):
            return rows4[:, 512 * i:512 * (i + 1)]
        s_r4, s_nm4 = 2 * hc, 2 * hc + 1
        tA = S.op("scalar", lambda e: e.activation(row(R_MU), pb[6][0:1, :], AF.Copy,
                                                   scale=INV_D),
                  reads=[r_pb[6]], writes=[r_rows[R_MU]], waits=[ts1])
        tB = S.op("scalar", lambda e: e.activation(row(R_M2), pb[7][0:1, :], AF.Copy,
                                                   scale=INV_D),
                  reads=[r_pb[7]], writes=[r_rows[R_M2]], waits=[ts2])
        tC = S.op("vector", lambda e: e.tensor_mul(row(R_T1), row(R_MU), row(R_MU)),
                  reads=[r_rows[R_MU]], writes=[r_rows[R_T1]], waits=[tA])
        tD = S.op("vector", lambda e: e.tensor_sub(row(R_V), row(R_M2), row(R_T1)),
                  reads=[r_rows[R_M2], r_rows[R_T1]], writes=[r_rows[R_V]], waits=[tB, tC])
        tE = S.op("scalar", lambda e: e.activation(row(R_T1), row(R_V), AF.Ln, bias=EPS),
                  reads=[r_rows[R_V]], writes=[r_rows[R_T1]], waits=[tD])
        tF = S.op("scalar", lambda e, s_r4=s_r4, row4=row4: e.activation(
                      row4(s_r4), row(R_T1), AF.Exp, scale=-0.5),
                  reads=[r_rows[R_T1]], writes=[r_rows4[s_r4]], waits=[tE])
        tG = S.op("vector", lambda e, s_r4=s_r4, s_nm4=s_nm4, row4=row4:
                  e.scalar_tensor_tensor(
                      row4(s_nm4), row(R_MU), -1.0, row4(s_r4), ALU.mult, ALU.mult),
                  reads=[r_rows[R_MU], r_rows4[s_r4]], writes=[r_rows4[s_nm4]], waits=[tF])
        ln4_toks.append((tF, tG))

    S.fence()
    for hc in range(2):
        off = 512 * hc
        s_r4, s_nm4 = 2 * hc, 2 * hc + 1
        t_r4b = bcast(bc_a, r_bca, rows4[:, 512 * s_r4:512 * (s_r4 + 1)],
                      r_rows4[s_r4], [ln4_toks[hc][0]], 3)
        t_nm4b = bcast(bc_b, r_bcb, rows4[:, 512 * s_nm4:512 * (s_nm4 + 1)],
                       r_rows4[s_nm4], [ln4_toks[hc][1]], 4)
        for k in range(KC):
            kk = k % 3
            trl = S.dma("scalar" if k % 2 == 0 else "sync",
                        lambda e, k=k, kk=kk, off=off: e.dma_start(
                            out=x2t[kk][:], in_=x2_st[k, :, off:off + 512]),
                        writes=[r_x2t[kk]], waits=[r_x2st[k]])
            tz1 = S.op("vector", lambda e, kk=kk: e.tensor_mul(
                           x2t[kk][:], x2t[kk][:], bc_a[:, :]),
                       reads=[r_bca], writes=[r_x2t[kk]], waits=[trl, t_r4b])
            tz2 = S.op("vector", lambda e, k=k, kk=kk, off=off: e.tensor_add(
                           z4_sb[:, 1024 * k + off:1024 * k + off + 512], x2t[kk][:],
                           bc_b[:, :]),
                       reads=[r_x2t[kk], r_bcb], writes=[r_z4] if (hc == 0 and k == 0) else [],
                       waits=[tz1, t_nm4b])
            if not (hc == 0 and k == 0):
                r_z4.w = tz2
    tap("x2st", x2_st[:, :, :].rearrange("a b c -> (a b) c"), list(r_x2st))
    tap("z4t", z4_sb[:, :], [r_z4.w], reads=[r_z4], q="gpsimd")

    S.phase = "router"
    snapO = S.snapshot()
    S.fence_from(snapO)
    t_lge = [None, None]
    for hc in range(2):
        off = 512 * hc

        def mm_lg(e, off=off):
            for k in range(KC):
                i = e.matmul(pb[5][0:8, :], routerTbf[:, 8 * k:8 * (k + 1)],
                             z4_sb[:, 1024 * k + off:1024 * k + off + 512],
                             start=(k == 0), stop=(k == KC - 1))
            return i
        t_lg = S.op("tensor", mm_lg, reads=[r_z4], writes=[r_pb[5]], waits=CW)
        t_lge[hc] = S.op("scalar", lambda e, off=off: e.activation(
                             lgt_sb[:, off:off + 512], pb[5][0:8, :], AF.Copy),
                         reads=[r_pb[5]], writes=[r_lgt], waits=[t_lg])
    for tt in range(8):
        t_tr = S.op("tensor", lambda e, tt=tt: e.transpose(
                        pb[7][:, 0:8], lgt_sb[:, 128 * tt:128 * (tt + 1)], ident[0:8, 0:8]),
                    reads=[r_lgt], writes=[r_pb[7]], waits=t_lge + CW)
        S.op("scalar", lambda e, tt=tt: e.activation(lgt_tm[:, 8 * tt:8 * (tt + 1)],
                                                     pb[7][:, 0:8], AF.Copy),
             reads=[r_pb[7]], writes=[r_lgttm])
    for tt in range(8):
        lg8 = lgt_tm[:, 8 * tt:8 * (tt + 1)]
        tm1 = S.op("vector", lambda e, lg8=lg8, tt=tt: e.reduce_max(
                       rt_m1[:, tt:tt + 1], lg8, axis=mybir.AxisListType.X),
                   reads=[r_lgttm], writes=[r_rt], waits=[])
        tmsk = S.op("vector", lambda e, lg8=lg8, tt=tt: e.tensor_scalar(
                        rt_t[:], lg8, rt_m1[:, tt:tt + 1], None, ALU.is_equal),
                    reads=[r_lgttm], writes=[r_rt], waits=[tm1])
        tmv = S.op("vector", lambda e, lg8=lg8: e.scalar_tensor_tensor(
                       rt_t[:], rt_t[:], -1e30, lg8, ALU.mult, ALU.add),
                   reads=[r_lgttm], writes=[r_rt], waits=[tmsk])
        tm2 = S.op("vector", lambda e: e.reduce_max(rt_t2[:, 0:1], rt_t[:],
                                                    axis=mybir.AxisListType.X),
                   writes=[r_rt], waits=[tmv])
        tdl = S.op("vector", lambda e, tt=tt: e.tensor_sub(rt_t2[:, 1:2], rt_t2[:, 0:1],
                                                           rt_m1[:, tt:tt + 1]),
                   writes=[r_rt], waits=[tm2])
        tsg = S.op("scalar", lambda e: e.activation(rt_t2[:, 2:3], rt_t2[:, 1:2], AF.Exp),
                   reads=[r_rt], writes=[r_rt], waits=[tdl])
        tT_ = S.op("vector", lambda e: e.tensor_scalar_add(rt_t2[:, 3:4], rt_t2[:, 2:3], 1.0),
                   writes=[r_rt], waits=[tsg])
        tin = S.op("vector", lambda e: e.reciprocal(rt_t2[:, 4:5], rt_t2[:, 3:4]),
                   writes=[r_rt], waits=[tT_])
        tcm = S.op("vector", lambda e, tt=tt: e.scalar_tensor_tensor(
                       comb_tm[:, tt:tt + 1], rt_t2[:, 2:3], rt_t2[:, 4:5], rt_t2[:, 4:5],
                       ALU.mult, ALU.add),
                   reads=[r_rt], writes=[r_combtm], waits=[tin])
    t_ct = S.op("tensor", lambda e: e.transpose(pb[7][0:8, 0:128], comb_tm[:], ident[:]),
                reads=[r_combtm], writes=[r_pb[7]], waits=CW)
    t_ce = S.op("scalar", lambda e: e.activation(cmb8[:], pb[7][0:8, 0:128], AF.Copy),
                reads=[r_pb[7]], writes=[r_cmb8], waits=[t_ct])
    for half in range(2):
        def mm_cb(e, half=half):
            for q in range(4):
                tt = 4 * half + q
                i = e.matmul(pb[3][:, 128 * q:128 * (q + 1)],
                             sel8_sb[:, 128 * tt:128 * (tt + 1)], cmb8[:],
                             start=True, stop=True)
            return i
        t_cb = S.op("tensor", mm_cb, reads=[r_cmb8], writes=[r_pb[3]], waits=[t_ce] + CW)
        S.op("scalar", lambda e, half=half: e.activation(
                 comb_bc[:, 512 * half:512 * (half + 1)], pb[3][:, :], AF.Copy),
             reads=[r_pb[3]], writes=[r_combbc])

    tap("combt", comb_bc[:, :], [r_combbc.w], reads=[r_combbc])
    tap("lgtt", lgt_tm[:, :], [r_lgttm.w], reads=[r_lgttm])

    S.phase = "routed"
    S.fence_from(snapO)
    t_wrg = [None] * NRI
    t_wru = [None] * NRI

    def sched_rdma(ir):
        ib = ir % 2
        t_wrg[ir] = S.dma("gpsimd", lambda e, ir=ir, ib=ib: e.dma_start(
                              out=gbuf[ib][:], in_=Wrg_h[ir]), writes=[r_gbuf[ib]])
        t_wru[ir] = S.dma("gpsimd", lambda e, ir=ir, ib=ib: e.dma_start(
                              out=ubuf[ib][:], in_=Wru_h[ir]), writes=[r_ubuf[ib]])

    sched_rdma(0)
    sched_rdma(1)
    for ir in range(NRI):
        ib = ir % 2
        for half in range(2):
            bg = (0 if ib == 0 else 4) + half
            bu = (2 if ib == 0 else 6) + half
            hof = 512 * half

            def mm_rg(e, ib=ib, hof=hof, bg=bg):
                for k in range(KC):
                    i = e.matmul(pb[bg][:, :], gbuf[ib][:, 128 * k:128 * (k + 1)],
                                 z4_sb[:, 1024 * k + hof:1024 * k + hof + 512],
                                 start=(k == 0), stop=(k == KC - 1))
                return i
            t_g = S.op("tensor", mm_rg, reads=[r_gbuf[ib], r_z4], writes=[r_pb[bg]],
                       waits=[t_wrg[ir]])

            def mm_ru(e, ib=ib, hof=hof, bu=bu):
                for k in range(KC):
                    i = e.matmul(pb[bu][:, :], ubuf[ib][:, 128 * k:128 * (k + 1)],
                                 z4_sb[:, 1024 * k + hof:1024 * k + hof + 512],
                                 start=(k == 0), stop=(k == KC - 1))
                return i
            t_u = S.op("tensor", mm_ru, reads=[r_ubuf[ib], r_z4], writes=[r_pb[bu]],
                       waits=[t_wru[ir]])
            tsl = S.op("scalar", lambda e, half=half, bg=bg: e.activation(
                           gs_t[half][:], pb[bg][:, :], AF.Silu),
                       reads=[r_pb[bg]], writes=[r_gs[half]], waits=[t_g])
            tuc = S.op("scalar", lambda e, half=half, bu=bu: e.activation(
                           us_t[half][:], pb[bu][:, :], AF.Copy),
                       reads=[r_pb[bu]], writes=[r_us[half]], waits=[t_u])
            tmul = S.op("gpsimd", lambda e, half=half, hof=hof: e.tensor_mul(
                            rh_tmp[:, hof:hof + 512], gs_t[half][:], us_t[half][:]),
                        reads=[r_gs[half], r_us[half]], writes=[r_rhtmp], waits=[tsl, tuc])
            S.op("vector", lambda e, ir=ir, hof=hof: e.tensor_mul(
                     r_h_sb[:, 1024 * ir + hof:1024 * ir + hof + 512],
                     rh_tmp[:, hof:hof + 512], comb_bc[:, hof:hof + 512]),
                 reads=[r_rhtmp, r_combbc], writes=[r_rh[ir]], waits=[tmul])
        if ir + 2 < NRI:
            sched_rdma(ir + 2)
    tap("rht", r_h_sb[:, :], [r_rh[NRI - 1].w], reads=list(r_rh), q="gpsimd")
    S.fence_from(snapO)
    for dt in range(KC):
        db = dt % 2
        t_wrd = S.dma("gpsimd", lambda e, dt=dt, db=db: e.dma_start(
                          out=rdbuf[db][:], in_=Wrd_h[dt]), writes=[r_rdbuf[db]])
        for half in range(2):
            bank = 2 * db + half
            hof = 512 * half

            def mm_rd(e, db=db, hof=hof, bank=bank):
                for ikr in range(NRI):
                    i = e.matmul(pb[bank][:, :], rdbuf[db][:, 128 * ikr:128 * (ikr + 1)],
                                 r_h_sb[:, 1024 * ikr + hof:1024 * ikr + hof + 512],
                                 start=(ikr == 0), stop=(ikr == NRI - 1))
                return i
            t_rd = S.op("tensor", mm_rd,
                        reads=[r_rdbuf[db]] + [r_rh[i_] for i_ in range(NRI)],
                        writes=[r_pb[bank]], waits=[t_wrd])
            t_ev = S.op("scalar", lambda e, db=db, hof=hof, bank=bank: e.activation(
                            frstage[db][:, hof:hof + 512], pb[bank][:, :], AF.Copy),
                        reads=[r_pb[bank]], writes=[r_frstage[db]] if half == 0 else [],
                        waits=[t_rd])
            if half == 1:
                r_frstage[db].w = t_ev
        r_frst[dt] = S.dma("scalar", lambda e, dt=dt, db=db: e.dma_start(
                               out=fr_st[dt], in_=frstage[db][:]),
                           reads=[r_frstage[db]], waits=[])

    tap("frst", fr_st[:, :, :].rearrange("a b c -> (a b) c"), list(r_frst))

    S.phase = "gu"
    S.fence()
    t_stash_done = r_x2st[KC - 1]
    t_wg = [None] * NI
    t_wu = [None] * NI

    def sched_wdma(i):
        ib = i % 2
        t_wg[i] = S.dma("gpsimd", lambda e, i=i, ib=ib: e.dma_start(
                            out=gbuf[ib][:], in_=Wg_h[i]), writes=[r_gbuf[ib]])
        t_wu[i] = S.dma("gpsimd", lambda e, i=i, ib=ib: e.dma_start(
                            out=ubuf[ib][:], in_=Wu_h[i]), writes=[r_ubuf[ib]])

    sched_wdma(0)
    sched_wdma(1)
    for i in range(NI):
        ib = i % 2
        for half in range(2):
            bg = (0 if ib == 0 else 4) + half
            bu = (2 if ib == 0 else 6) + half
            hof = 512 * half

            def mm_g(e, ib=ib, hof=hof, bg=bg):
                for k in range(KC):
                    i2 = e.matmul(pb[bg][:, :], gbuf[ib][:, 128 * k:128 * (k + 1)],
                                  z4_sb[:, 1024 * k + hof:1024 * k + hof + 512],
                                  start=(k == 0), stop=(k == KC - 1))
                return i2
            t_g = S.op("tensor", mm_g, reads=[r_gbuf[ib], r_z4], writes=[r_pb[bg]],
                       waits=[t_wg[i]])

            def mm_u(e, ib=ib, hof=hof, bu=bu):
                for k in range(KC):
                    i2 = e.matmul(pb[bu][:, :], ubuf[ib][:, 128 * k:128 * (k + 1)],
                                  z4_sb[:, 1024 * k + hof:1024 * k + hof + 512],
                                  start=(k == 0), stop=(k == KC - 1))
                return i2
            t_u = S.op("tensor", mm_u, reads=[r_ubuf[ib], r_z4], writes=[r_pb[bu]],
                       waits=[t_wu[i]])
            tsl = S.op("scalar", lambda e, half=half, bg=bg: e.activation(
                           gs_t[half][:], pb[bg][:, :], AF.Silu),
                       reads=[r_pb[bg]], writes=[r_gs[half]], waits=[t_g])
            tuc = S.op("scalar", lambda e, half=half, bu=bu: e.activation(
                           us_t[half][:], pb[bu][:, :], AF.Copy),
                       reads=[r_pb[bu]], writes=[r_us[half]], waits=[t_u])
            S.op("gpsimd", lambda e, i=i, hof=hof, half=half: e.tensor_mul(
                     h_all[:, 1024 * i + hof:1024 * i + hof + 512],
                     gs_t[half][:], us_t[half][:]),
                 reads=[r_gs[half], r_us[half]], writes=[r_h[i]], waits=[tsl, tuc])
        if i + 2 < NI:
            sched_wdma(i + 2)

    S.phase = "down"
    S.fence()
    t_wd = [[None, None] for _ in range(KC)]
    t_fr = [None] * KC

    def sched_ddma(dt, hh):
        t_wd[dt][hh] = S.dma("gpsimd", lambda e, dt=dt, hh=hh: e.dma_start(
                                 out=dbuf[hh][:], in_=Wd_h[dt, :, 4096 * hh:4096 * (hh + 1)]),
                             writes=[r_dbuf[hh]])

    def sched_fdma(dt):
        db = dt % 2
        t_fr[dt] = S.dma("scalar", lambda e, dt=dt, db=db: e.dma_start(
                             out=frbuf[db][:], in_=fr_st[dt]),
                         writes=[r_frbuf[db]], waits=[r_frst[dt]])

    sched_ddma(0, 0)
    sched_ddma(0, 1)
    sched_fdma(0)
    sched_fdma(1)
    for dt in range(KC):
        db = dt % 2
        b0, b1 = 2 * db, 2 * db + 1

        def mm_dA(e, b0=b0, b1=b1):
            for ik in range(32):
                e.matmul(pb[b0][:, :], dbuf[0][:, 128 * ik:128 * (ik + 1)],
                         h_all[:, 1024 * ik:1024 * ik + 512],
                         start=(ik == 0), stop=False)
                i2 = e.matmul(pb[b1][:, :], dbuf[0][:, 128 * ik:128 * (ik + 1)],
                              h_all[:, 1024 * ik + 512:1024 * (ik + 1)],
                              start=(ik == 0), stop=False)
            return i2
        t_dA = S.op("tensor", mm_dA,
                    reads=[r_dbuf[0]] + [r_h[i_] for i_ in range(32)],
                    writes=[r_pb[b0], r_pb[b1]], waits=[t_wd[dt][0]])
        if dt + 1 < KC:
            sched_ddma(dt + 1, 0)

        def mm_dB(e, b0=b0, b1=b1):
            for ik in range(32, NI):
                e.matmul(pb[b0][:, :], dbuf[1][:, 128 * (ik - 32):128 * (ik - 31)],
                         h_all[:, 1024 * ik:1024 * ik + 512],
                         start=False, stop=(ik == NI - 1))
                i2 = e.matmul(pb[b1][:, :], dbuf[1][:, 128 * (ik - 32):128 * (ik - 31)],
                              h_all[:, 1024 * ik + 512:1024 * (ik + 1)],
                              start=False, stop=(ik == NI - 1))
            return i2
        t_dB = S.op("tensor", mm_dB,
                    reads=[r_dbuf[1]] + [r_h[i_] for i_ in range(32, NI)],
                    writes=[], waits=[t_wd[dt][1], t_dA])
        r_pb[b0].w = t_dB
        r_pb[b1].w = t_dB
        if dt + 1 < KC:
            sched_ddma(dt + 1, 1)
        for half in range(2):
            bank = 2 * db + half
            hof = 512 * half
            t_ev = S.op("vector", lambda e, db=db, hof=hof, bank=bank: e.tensor_add(
                            f_stage[db][:, hof:hof + 512], pb[bank][:, :],
                            frbuf[db][:, hof:hof + 512]),
                        reads=[r_pb[bank], r_frbuf[db]],
                        writes=[r_fstage[db]] if half == 0 else [],
                        waits=[t_dB, t_fr[dt]])
            if half == 1:
                r_fstage[db].w = t_ev
            tca = S.op("scalar", lambda e, db=db, hof=hof: e.activation(
                           f_bf[:, hof:hof + 512], f_stage[db][:, hof:hof + 512], AF.Copy),
                       reads=[r_fstage[db]], writes=[r_fbf], waits=[t_ev])
            tcb = S.op("scalar", lambda e, db=db, hof=hof: e.activation(
                           f2_bf[:, hof:hof + 512], f_stage[db][:, hof:hof + 512], AF.Square),
                       reads=[r_fstage[db]], writes=[r_f2bf], waits=[t_ev])
            ts1 = S.op("tensor", lambda e, dt=dt, hof=hof, half=half: e.matmul(
                           pb[4 + half][0:1, :], ones128bf[:], f_bf[:, hof:hof + 512],
                           start=(dt == 0), stop=(dt == KC - 1)),
                       reads=[r_fbf], writes=[r_pb[4 + half]] if dt == 0 else [],
                       waits=[tca] + CW)
            if dt > 0:
                r_pb[4 + half].w = ts1
            ts2 = S.op("tensor", lambda e, dt=dt, hof=hof, half=half: e.matmul(
                           pb[6 + half][0:1, :], ones128bf[:], f2_bf[:, hof:hof + 512],
                           start=(dt == 0), stop=(dt == KC - 1)),
                       reads=[r_f2bf], writes=[r_pb[6 + half]] if dt == 0 else [],
                       waits=[tcb] + CW)
            if dt > 0:
                r_pb[6 + half].w = ts2
        r_fst[dt] = S.dma("scalar", lambda e, dt=dt, db=db: e.dma_start(
                              out=f_st[dt], in_=f_stage[db][:]),
                          reads=[r_fstage[db]], waits=[])
        if dt + 2 < KC:
            sched_fdma(dt + 2)

    S.phase = "ep"
    S.fence()
    ep_r5 = [eprow(0), eprow(1)]
    ep_nm5 = [eprow(2), eprow(3)]
    for half in range(2):
        tA = S.op("scalar", lambda e, half=half: e.activation(
                      eprow(4), pb[4 + half][0:1, :], AF.Copy, scale=INV_D),
                  reads=[r_pb[4 + half]], writes=[r_eprows[4]])
        tB = S.op("scalar", lambda e, half=half: e.activation(
                      eprow(5), pb[6 + half][0:1, :], AF.Copy, scale=INV_D),
                  reads=[r_pb[6 + half]], writes=[r_eprows[5]])
        tC = S.op("vector", lambda e: e.tensor_mul(eprow(6), eprow(4), eprow(4)),
                  reads=[r_eprows[4]], writes=[r_eprows[6]], waits=[tA])
        tD = S.op("vector", lambda e: e.tensor_sub(eprow(7), eprow(5), eprow(6)),
                  reads=[r_eprows[5], r_eprows[6]], writes=[r_eprows[7]], waits=[tB, tC])
        tE = S.op("scalar", lambda e: e.activation(eprow(6), eprow(7), AF.Ln, bias=EPS),
                  reads=[r_eprows[7]], writes=[r_eprows[6]], waits=[tD])
        tF = S.op("scalar", lambda e, half=half: e.activation(
                      ep_r5[half], eprow(6), AF.Exp, scale=-0.5),
                  reads=[r_eprows[6]], writes=[r_eprows[half]], waits=[tE])
        S.op("vector", lambda e, half=half: e.scalar_tensor_tensor(
                 ep_nm5[half], eprow(4), -1.0, ep_r5[half], ALU.mult, ALU.mult),
             reads=[r_eprows[4], r_eprows[half]], writes=[r_eprows[2 + half]], waits=[tF])
    r5_bc = comb_bc
    nm5_bc = frbuf[0]
    for half in range(2):
        hof = 512 * half
        tb1 = S.op("tensor", lambda e, half=half: e.matmul(
                       pb[0][:, :], ones1x128[:], ep_r5[half], start=True, stop=True),
                   reads=[r_eprows[half]], writes=[r_pb[0]], waits=CW)
        S.op("scalar", lambda e, hof=hof: e.activation(r5_bc[:, hof:hof + 512],
                                                       pb[0][:, :], AF.Copy),
             reads=[r_pb[0]], writes=[r_combbc], waits=[tb1])
        tb2 = S.op("tensor", lambda e, half=half: e.matmul(
                       pb[1][:, :], ones1x128[:], ep_nm5[half], start=True, stop=True),
                   reads=[r_eprows[2 + half]], writes=[r_pb[1]], waits=CW)
        S.op("scalar", lambda e, hof=hof: e.activation(nm5_bc[:, hof:hof + 512],
                                                       pb[1][:, :], AF.Copy),
             reads=[r_pb[1]], writes=[r_frbuf[0]], waits=[tb2])
    for k in range(KC):
        kk = k % 3
        ko = k % 2
        t_lx = S.dma("scalar", lambda e, k=k, kk=kk: e.dma_start(
                         out=ep_x2[kk][:], in_=x2_st[k]), writes=[r_epx2[kk]],
                     waits=[r_x2st[k]])
        fsrc, rfs = ep_f[kk], r_epf[kk]
        t_lf = S.dma("gpsimd", lambda e, k=k, kk=kk: e.dma_start(
                         out=ep_f[kk][:], in_=f_st[k]), writes=[r_epf[kk]],
                     waits=[r_fst[k]])
        t_e1 = S.op("vector", lambda e, kk=kk, fsrc=fsrc: e.tensor_mul(
                        ep_f[kk][:], fsrc[:], r5_bc[:, :]),
                    reads=[r_combbc, rfs], writes=[r_epf[kk]], waits=[t_lf])
        t_e2 = S.op("gpsimd", lambda e, kk=kk, ko=ko: e.tensor_add(
                        ep_o[ko][:], ep_x2[kk][:], nm5_bc[:, :]),
                    reads=[r_frbuf[0], r_epx2[kk]], writes=[r_epo[ko]], waits=[t_lx])
        t_e3 = S.op("vector", lambda e, kk=kk, ko=ko: e.tensor_add(
                        ep_o[ko][:], ep_f[kk][:], ep_o[ko][:]),
                    reads=[r_epf[kk]], writes=[r_epo[ko]], waits=[t_e1, t_e2])
        out_toks.append(S.dma("sync", lambda e, k=k, ko=ko: e.dma_start(
                                  out=out_dev[128 * k:128 * (k + 1), :], in_=ep_o[ko][:]),
                              reads=[r_epo[ko]], waits=[t_e3]))

    final_waits = {}
    for (s, v, _) in out_toks:
        final_waits[s] = max(final_waits.get(s, 0), v)
    fw = list(final_waits.items())

    def final_op(e, fw=fw):
        for s, v in fw:
            e.wait_ge(_SEMS[s], v)
        return e.nop()
    S.cnt["s_sy"] = S.cnt.get("s_sy", 0) + 1
    S.ops.append(("sync", lambda e: final_op(e).then_inc(_SEMS["s_sy"], 1)))

    with nc.Block() as block:
        @block.sync
        def _(e):
            for eng, fn in S.ops:
                if eng == "sync":
                    fn(e)

        @block.gpsimd
        def _(e):
            for eng, fn in S.ops:
                if eng == "gpsimd":
                    fn(e)

        @block.tensor
        def _(e):
            for eng, fn in S.ops:
                if eng == "tensor":
                    fn(e)

        @block.vector
        def _(e):
            for eng, fn in S.ops:
                if eng == "vector":
                    fn(e)

        @block.scalar
        def _(e):
            for eng, fn in S.ops:
                if eng == "scalar":
                    fn(e)

    es.close()
    nc._dbg_sched = S
    return nc


_PROG_CACHE = {}


def kernel(**inputs):
    key = tuple(sorted(n for n, _ in _DEBUG_TAPS))
    if key not in _PROG_CACHE:
        _PROG_CACHE[key] = build_program(list(_DEBUG_TAPS))
    nc = _PROG_CACHE[key]
    g = _host_shared(inputs)
    x = inputs["L0_attn_feat_patch"]
    in_maps = []
    for c in range(8):
        m = dict(g)
        m.update(_host_percore(x, c))
        in_maps.append(m)
    res = run_bass_kernel_spmd(nc, in_maps, core_ids=list(range(8)))
    out = np.empty((4, 2048, 2048), np.float32)
    for c in range(8):
        b, h = c // 2, c % 2
        out[b, 1024 * h:1024 * (h + 1), :] = res.results[c]["out"].T
    if _DEBUG_TAPS:
        kernel._dbg = [{k: v for k, v in r.items() if k.startswith("dbg_")}
                       for r in res.results]
    return out


# version 18
# speedup vs baseline: 1.0525x; 1.0059x over previous
import contextlib
import numpy as np
import ml_dtypes
import concourse.bass as bass
import concourse.mybir as mybir
from concourse.bass_utils import run_bass_kernel_spmd

F32 = mybir.dt.float32
BF16 = mybir.dt.bfloat16
AF = mybir.ActivationFunctionType
ALU = mybir.AluOpType
BF = ml_dtypes.bfloat16

KC = 16
TP = 2560
NCHUNK = 5
NJ = 12
NI = 64
NRI = 8
EPS = 1e-5
INV_D = 1.0 / 2048.0

_DEBUG_TAPS = []

ENG_SEM = {"tensor": "s_pe", "scalar": "s_act", "vector": "s_dve",
           "gpsimd": "s_pool", "sync": "s_sy"}
NDS = 4
DMA_SEMS = {q: [f"d_{q[:2]}{i}" for i in range(NDS)] for q in ("sync", "gpsimd", "scalar")}
ALL_SEMS = sorted(set(list(ENG_SEM.values())
                      + [s for v in DMA_SEMS.values() for s in v]))


class Res:
    __slots__ = ("w", "r")

    def __init__(self):
        self.w = None
        self.r = []


class Sched:
    def __init__(self):
        self.ops = []
        self.cnt = {}
        self.pending = {}
        self.dma_idx = {q: 0 for q in DMA_SEMS}
        self.phase = "init"
        self.dbg_map = {e: [] for e in ENG_SEM}

    def _mk(self, eng, emit, waits, inc_sem, inc_by):
        if eng in self.pending:
            waits = list(waits) + self.pending.pop(eng)
        self.cnt[inc_sem] = self.cnt.get(inc_sem, 0) + inc_by
        tok = (inc_sem, self.cnt[inc_sem], eng)
        best = {}
        for wt in waits:
            if wt is None:
                continue
            s, v, peng = wt
            if best.get(s, -1) < v:
                best[s] = v
        wl = sorted(best.items())

        ph = self.phase

        def run(e, emit=emit, wl=wl, inc=(inc_sem, inc_by), ph=ph, eng=eng):
            n0 = e.bass.n_instructions() if hasattr(e, "bass") else None
            for s, v in wl:
                e.wait_ge(_SEMS[s], v)
            inst = emit(e)
            inst.then_inc(_SEMS[inc[0]], inc[1])
            n1 = e.bass.n_instructions() if n0 is not None else None
            self.dbg_map[eng].append((n1 - n0 if n0 is not None else 1 + len(wl), ph))
        self.ops.append((eng, run))
        return tok

    def _deps(self, reads, writes, waits):
        ws = list(waits)
        for r in reads:
            if r.w is not None:
                ws.append(r.w)
        for r in writes:
            if r.w is not None:
                ws.append(r.w)
            ws.extend(r.r)
        return ws

    def _note(self, tok, reads, writes):
        for r in reads:
            r.r.append(tok)
        for r in writes:
            r.w = tok
            r.r = []

    def op(self, eng, emit, reads=(), writes=(), waits=()):
        tok = self._mk(eng, emit, self._deps(reads, writes, waits), ENG_SEM[eng], 1)
        self._note(tok, reads, writes)
        return tok

    def dma(self, eng, emit, reads=(), writes=(), waits=()):
        sem = DMA_SEMS[eng][self.dma_idx[eng] % NDS]
        self.dma_idx[eng] += 1
        ws = self._deps(reads, writes, waits)
        prior = self.cnt.get(sem, 0)
        if prior > 0:
            ws = list(ws) + [(sem, prior, "_dma")]
        tok = self._mk(eng, emit, ws, sem, 16)
        self._note(tok, reads, writes)
        return tok

    def snapshot(self):
        return [(s, self.cnt.get(s, 0), "_f") for s in ALL_SEMS if self.cnt.get(s, 0) > 0]

    def fence(self):
        snap = self.snapshot()
        for e in ENG_SEM:
            self.pending[e] = list(snap)

    def fence_from(self, snap):
        for e in ENG_SEM:
            self.pending[e] = list(snap)


_SEMS = None


def _host_shared(inp):
    g = {}
    in_proj_W = inp["L0_attn_in_proj_W"]
    A_log = inp["L0_attn_A_log"]
    pj = np.arange(128)
    g["lnw_cols"] = np.ascontiguousarray(inp["L0_attn_ln_w"].reshape(16, 128).T)
    g["lnb_cols"] = np.ascontiguousarray(inp["L0_attn_ln_b"].reshape(16, 128).T)
    g["inpT"] = np.ascontiguousarray(
        in_proj_W.reshape(96, 16, 128).transpose(2, 1, 0).reshape(128, 16 * 96))
    g["outT"] = np.ascontiguousarray(inp["L0_attn_out_proj_W"].T)
    g["xprojT"] = np.ascontiguousarray(inp["L0_attn_x_proj_W"].T)
    g["dtprojT"] = np.ascontiguousarray(inp["L0_attn_dt_proj_W"].T)
    g["dtb_col"] = np.ascontiguousarray(inp["L0_attn_dt_proj_b"].reshape(96, 1))
    g["D_col"] = np.ascontiguousarray(inp["L0_attn_D_param"].reshape(96, 1))
    g["convw"] = np.ascontiguousarray(inp["L0_attn_conv_w"].reshape(96, 3))
    g["convb_col"] = np.ascontiguousarray(inp["L0_attn_conv_b"].reshape(96, 1))
    acols = np.empty((128, 12), np.float32)
    for j in range(12):
        acols[:, j] = A_log[8 * j + pj // 16, pj % 16]
    g["Acols"] = acols
    selD = np.zeros((96, 12 * 128), np.float32)
    for j in range(12):
        selD[8 * j + pj // 16, 128 * j + pj] = 1.0
    g["selD"] = selD
    selB = np.zeros((38, 128), np.float32)
    selC = np.zeros((38, 128), np.float32)
    selB[6 + pj % 16, pj] = 1.0
    selC[22 + pj % 16, pj] = 1.0
    g["selB"] = selB
    g["selC"] = selC
    selR = np.zeros((128, 12 * 96), np.float32)
    for j in range(12):
        selR[pj, 96 * j + (8 * j + pj // 16)] = 1.0
    g["selR"] = selR
    sel8 = np.zeros((8, 8 * 128), np.float32)
    for tt in range(8):
        sel8[tt, 128 * tt:128 * (tt + 1)] = 1.0
    g["sel8"] = sel8
    g["ident"] = np.eye(128, dtype=np.float32)
    g["cconsts"] = np.repeat(np.array([[1e-5, 1.0 / 2048.0, -1.0 / 2048.0, -0.5,
                                        np.float32(1.0) + np.float32(1e-5),
                                        np.float32(1e-5) * np.float32(1e-5)]],
                                      np.float32), 128, axis=0)
    g["ones1x128"] = np.ones((1, 128), np.float32)
    g["ones1x128bf"] = np.ones((1, 128), BF)
    g["ones128bf"] = np.ones((128, 1), BF)
    rw = inp["L0_ffn_router_weight"]
    g["routerT"] = np.ascontiguousarray(
        rw.reshape(8, 16, 128).transpose(2, 1, 0).reshape(128, 128))
    g["Wg"] = np.ascontiguousarray(
        inp["L0_ffn_sh_gate"].reshape(64, 128, 16, 128).transpose(0, 3, 2, 1).reshape(64, 128, 2048))
    g["Wu"] = np.ascontiguousarray(
        inp["L0_ffn_sh_up"].reshape(64, 128, 16, 128).transpose(0, 3, 2, 1).reshape(64, 128, 2048))
    g["Wd"] = np.ascontiguousarray(
        inp["L0_ffn_sh_down"].reshape(16, 128, 64, 128).transpose(0, 3, 2, 1).reshape(16, 128, 8192))
    g["Wrg"] = np.ascontiguousarray(
        inp["L0_ffn_r_gate"].reshape(8, 128, 16, 128).transpose(0, 3, 2, 1).reshape(8, 128, 2048))
    g["Wru"] = np.ascontiguousarray(
        inp["L0_ffn_r_up"].reshape(8, 128, 16, 128).transpose(0, 3, 2, 1).reshape(8, 128, 2048))
    g["Wrd"] = np.ascontiguousarray(
        inp["L0_ffn_r_down"].reshape(16, 128, 8, 128).transpose(0, 3, 2, 1).reshape(16, 128, 1024))
    return g


def _host_percore(x, c):
    b, h = c // 2, c % 2
    xt = x[b].T
    ctx = np.zeros((2048, TP), np.float32)
    mask = np.zeros((1, TP), np.float32)
    if h == 0:
        ctx[:, 1024:2048] = xt[:, 0:1024]
        ctx[:, 2048] = xt[:, 1024]
        mask[0, 1024:2049] = 1.0
    else:
        ctx[:, 0:2048] = xt[:, 0:2048]
        mask[0, 0:2048] = 1.0
    return {"xT": np.ascontiguousarray(ctx),
            "xmask": np.ascontiguousarray(mask.astype(BF))}


def build_program(debug_taps=()):
    global _SEMS
    nc = bass.Bass("TRN2", target_bir_lowering=False, debug=False)

    def din(name, shape, dt=F32):
        return nc.dram_tensor(name, list(shape), dt, kind="ExternalInput").ap()

    xT = din("xT", (2048, TP))
    xmask = din("xmask", (1, TP), BF16)
    lnw_h = din("lnw_cols", (128, 16))
    lnb_h = din("lnb_cols", (128, 16))
    inpT_h = din("inpT", (128, 16 * 96))
    outT_h = din("outT", (96, 2048))
    xprojT_h = din("xprojT", (96, 38))
    dtprojT_h = din("dtprojT", (6, 96))
    dtb_h = din("dtb_col", (96, 1))
    Dcol_h = din("D_col", (96, 1))
    convw_h = din("convw", (96, 3))
    convb_h = din("convb_col", (96, 1))
    Acols_h = din("Acols", (128, 12))
    selD_h = din("selD", (96, 12 * 128))
    selB_h = din("selB", (38, 128))
    selC_h = din("selC", (38, 128))
    selR_h = din("selR", (128, 12 * 96))
    sel8_h = din("sel8", (8, 8 * 128))
    ident_h = din("ident", (128, 128))
    cconsts_h = din("cconsts", (128, 6))
    ones1x128_h = din("ones1x128", (1, 128))
    ones1x128bf_h = din("ones1x128bf", (1, 128), BF16)
    ones128bf_h = din("ones128bf", (128, 1), BF16)
    routerT_h = din("routerT", (128, 128))
    Wg_h = din("Wg", (64, 128, 2048))
    Wu_h = din("Wu", (64, 128, 2048))
    Wd_h = din("Wd", (16, 128, 8192))
    Wrg_h = din("Wrg", (8, 128, 2048))
    Wru_h = din("Wru", (8, 128, 2048))
    Wrd_h = din("Wrd", (16, 128, 1024))

    out_dev = nc.dram_tensor("out", [2048, 1024], F32, kind="ExternalOutput").ap()
    x2_st = nc.dram_tensor("x2_stash", [16, 128, 1024], F32).ap()
    fr_st = nc.dram_tensor("fr_stash", [16, 128, 1024], F32).ap()
    f_st = nc.dram_tensor("f_stash", [16, 128, 1024], F32).ap()

    taps = {}
    for name, shape in debug_taps:
        taps[name] = nc.dram_tensor("dbg_" + name, list(shape), F32,
                                    kind="ExternalOutput").ap()

    BASEA = 16512
    LIMIT = 229344

    def at(name, shape, dt, off):
        per = int(np.prod(shape[1:])) * mybir.dt.size(dt)
        assert BASEA <= off and off + per <= LIMIT, f"sbuf {name} [{off},{off + per})"
        return nc.alloc_sbuf_tensor_at(name, list(shape), dt, offset=off).ap()

    ones1x128 = at("s_o1x", (1, 128), F32, BASEA)
    ones128bf = at("s_o128b", (128, 1), BF16, BASEA + 512)
    cconsts_sb = at("s_cconsts", (128, 6), F32, BASEA + 544)
    for _i, _v in enumerate([EPS, INV_D, -INV_D, -0.5,
                             float(np.float32(1.0) + np.float32(1e-5)),
                             float(np.float32(1e-5) * np.float32(1e-5))]):
        nc.const_aps.register(F32, _v, cconsts_sb[:, _i:_i + 1])
    P_END = BASEA + 1024

    RW0 = P_END
    mask_sb = at("s_mask", (1, TP), BF16, RW0)
    r1own = at("s_r1own", (1, 1024), F32, RW0 + 5120)
    nm1r1bf = at("s_nm1r1b", (1, 1024), BF16, RW0 + 9216)
    comb_bc = at("s_combbc", (128, 1024), F32, RW0)
    gs_t = [at(f"s_gs{i}", (128, 512), BF16, RW0 + 4096 + 1024 * i) for i in range(2)]
    us_t = [at(f"s_us{i}", (128, 512), BF16, RW0 + 6144 + 1024 * i) for i in range(2)]
    RW_END = RW0 + 12 * 1024

    WB0 = RW_END
    gbuf = [at(f"s_gbuf{i}", (128, 2048), BF16, WB0 + 4096 * i) for i in range(2)]
    ubuf = [at(f"s_ubuf{i}", (128, 2048), BF16, WB0 + 8192 + 4096 * i) for i in range(2)]
    xsq_bf = [at(f"s_xsq{i}", (128, 512), BF16, WB0 + 1024 * i) for i in range(2)]
    nmu_bf = [at(f"s_nmub{i}", (1, 512), BF16, WB0 + 2048 + 1024 * i) for i in range(2)]
    du_bf = at("s_dub", (96, 512), BF16, WB0 + 4096)
    m96k = [at(f"s_m96k{i}", (96, 512), F32, WB0 + 5120 + 2048 * i) for i in range(2)]
    rdbuf = [at(f"s_rdbuf{i}", (128, 1024), BF16, WB0 + 2048 * i) for i in range(2)]
    frbuf = [at(f"s_frbuf{i}", (128, 1024), F32, WB0 + 4096 * (1 + i)) for i in range(2)]
    eprows = at("s_eprows", (1, 8 * 512), F32, WB0 + 12288)
    WB_END = WB0 + 16 * 1024

    B1 = WB_END
    o1 = [B1]

    def atb1(name, shape, dt):
        per = int(np.prod(shape[1:])) * mybir.dt.size(dt)
        off = (o1[0] + 31) & ~31
        o1[0] = off + per
        assert o1[0] <= B1 + 64 * 1024, f"B1 overflow {name}"
        return at(name, shape, dt, off)

    xp_all = atb1("s_xp", (96, 2624), F32)
    u_ch = [atb1(f"s_u{i}", (96, 512), F32) for i in range(2)]
    u_bf = atb1("s_ubf", (96, 512), BF16)
    dus_bf = atb1("s_dusb", (96, 512), BF16)
    dbc_bf = atb1("s_dbcb", (38, 512), BF16)
    dA_b = [atb1(f"s_dA{i}", (128, 512), F32) for i in range(2)]
    dBu_b = [atb1(f"s_dBu{i}", (128, 512), F32) for i in range(2)]
    xs_b = [atb1(f"s_xs{i}", (128, 512), F32) for i in range(2)]
    Bp_bc = [atb1(f"s_Bp{i}", (128, 512), F32) for i in range(2)]
    C_bc = [atb1(f"s_C{i}", (128, 512), F32) for i in range(2)]
    rr96_bc = atb1("s_rr96", (96, 512), F32)
    rowsA = atb1("s_rows", (1, 10 * 512), F32)
    bc_a = atb1("s_bca", (128, 512), F32)
    bc_b = atb1("s_bcb", (128, 512), F32)
    prod_b = dA_b
    inpT_f32 = at("s_inpTf", (128, 16 * 96), F32,
                  nc.lookup_mloc(dA_b[0].tensor).addr)
    x_half = at("s_xhalf", (128, 16 * 512), F32,
                nc.lookup_mloc(xp_all.tensor).addr)
    assert (nc.lookup_mloc(rowsA.tensor).addr
            >= nc.lookup_mloc(x_half.tensor).addr + 32 * 1024), "x_half overlaps rows"
    rh_tmp = at("s_rhtmp", (128, 1024), BF16, B1)
    r_h_sb = at("s_rh", (128, 8 * 1024), BF16, B1 + 2048)
    frstage = [at(f"s_frstg{i}", (128, 1024), F32, B1 + 18432 + 4096 * i) for i in range(2)]

    B2 = B1 + 64 * 1024
    h_all = at("s_h", (128, 64 * 1024), BF16, B1)
    o2 = [B2 + 4096]

    def atb2(name, shape, dt):
        per = int(np.prod(shape[1:])) * mybir.dt.size(dt)
        off = (o2[0] + 31) & ~31
        o2[0] = off + per
        assert o2[0] <= B2 + 64 * 1024, f"B2 overflow {name}"
        return at(name, shape, dt, off)

    ident = atb2("s_ident", (128, 128), F32)
    ones1x128bf = atb2("s_o1xb", (1, 128), BF16)
    routerTbf = atb2("s_routerTb", (128, 128), BF16)
    xprojTbf = atb2("s_xprojTb", (96, 38), BF16)
    dtprojTbf = atb2("s_dtprojTb", (6, 96), BF16)
    lnw_sb = atb2("s_lnw", (128, 16), F32)
    lnb_sb = atb2("s_lnb", (128, 16), F32)
    dtb_sb = atb2("s_dtb", (96, 1), F32)
    D_sb = atb2("s_D", (96, 1), F32)
    convw_sb = atb2("s_convw", (96, 3), F32)
    convb_sb = atb2("s_convb", (96, 1), F32)
    nconvb_sb = atb2("s_nconvb", (96, 1), F32)
    c0_sb = atb2("s_c0", (96, 1), F32)
    Acols_sb = atb2("s_Acols", (128, 12), F32)
    A2 = atb2("s_A2", (128, 12), F32)
    state_cols = atb2("s_state", (128, 12), F32)
    wpsum_bf = atb2("s_wpsum", (1, 96), BF16)
    sel8_sb = atb2("s_sel8", (8, 8 * 128), F32)
    cmb8 = atb2("s_cmb8", (8, 128), F32)
    outTbf = atb2("s_outTb", (96, 2048), BF16)
    inpTbf = atb2("s_inpTb", (128, 16 * 96), BF16)
    selDbf = atb2("s_selDb", (96, 12 * 128), BF16)
    selBbf = atb2("s_selBb", (38, 128), BF16)
    selCbf = atb2("s_selCb", (38, 128), BF16)
    selR_sb = atb2("s_selR", (128, 12 * 96), F32)
    y_f = atb2("s_yf", (96, 1024), F32)
    y_bf = atb2("s_ybf", (96, 1024), BF16)
    lgt_sb = atb2("s_lgt", (8, 1024), F32)
    lgt_tm = atb2("s_lgttm", (128, 64), F32)
    rt_m1 = atb2("s_rtm1", (128, 8), F32)
    rt_t = atb2("s_rtt", (128, 8), F32)
    rt_t2 = atb2("s_rtt2", (128, 8), F32)
    rows4 = atb2("s_rows4", (1, 4 * 512), F32)
    cast_bf = [atb2(f"s_castb{i}", (128, 512), BF16) for i in range(2)]
    cast_bf2 = [atb2(f"s_castb2{i}", (128, 512), BF16) for i in range(2)]
    x2t = [atb2(f"s_x2t{i}", (128, 512), F32) for i in range(3)]
    bc_b2 = atb2("s_bcb2", (128, 512), F32)
    comb_tm = atb2("s_combtm", (128, 8), F32)

    B3 = B2 + 64 * 1024
    assert B3 + 32 * 1024 <= LIMIT, f"arena end {B3 + 32 * 1024} > {LIMIT}"
    xcb = [at(f"s_xcb{i}", (128, 16 * 512), BF16, B3 + 16384 * i) for i in range(2)]
    mr_sb = at("s_mr", (128, 16 * 512), F32, B3)
    z4_sb = at("s_z4", (128, 16 * 1024), BF16, B3)
    dbuf = [at(f"s_dbuf{i}", (128, 4096), BF16, B3 + 8192 * i) for i in range(2)]
    f_stage = [at(f"s_fstg{i}", (128, 1024), F32, B3 + 16384 + 4096 * i) for i in range(2)]
    f_bf = at("s_fbf", (128, 1024), BF16, B3 + 24576)
    f2_bf = at("s_f2bf", (128, 1024), BF16, B3 + 26624)
    ep_x2 = [at(f"s_epx2{i}", (128, 1024), F32, B3 + 4096 * i) for i in range(3)]
    ep_f = [at(f"s_epf{i}", (128, 1024), F32, B3 + 12288 + 4096 * i) for i in range(3)]
    ep_o = [at(f"s_epo{i}", (128, 1024), F32, B3 + 24576 + 2048 * 0 + 4096 * i) for i in range(2)]

    pb = [nc.alloc_psum_tensor(f"pb{i}", [128, 512], F32).ap() for i in range(8)]

    es = contextlib.ExitStack()
    _SEMS = {n: es.enter_context(nc.semaphore(n)) for n in ALL_SEMS}
    S = Sched()

    def row(i):
        return rowsA[:, 512 * i:512 * (i + 1)]

    def eprow(i):
        return eprows[:, 512 * i:512 * (i + 1)]

    r_pb = [Res() for _ in range(8)]
    r_xcb = [Res(), Res()]
    r_xsq = [Res(), Res()]
    r_nmu = [Res(), Res()]
    r_rows = [Res() for _ in range(10)]
    r_rows4 = [Res() for _ in range(4)]
    r_eprows = [Res() for _ in range(8)]
    r_r1own = Res()
    r_nm1r1 = Res()
    r_xp = [Res() for _ in range(NCHUNK)]
    r_u = [Res(), Res()]
    r_ubf = Res()
    r_dus = Res()
    r_dbc = Res()
    r_dub = Res()
    r_m96 = [Res(), Res()]
    r_rr96 = Res()
    r_bca = Res()
    r_bcb = Res()
    r_bcb2 = Res()
    r_dA = [Res(), Res()]
    r_dBu = [Res(), Res()]
    r_xs = [Res(), Res()]
    r_Bp = [Res(), Res()]
    r_C = [Res(), Res()]
    r_state = [Res() for _ in range(NJ)]
    r_yf = Res()
    r_ybf = Res()
    r_xhalf = Res()
    r_mr = Res()
    r_x2t = [Res(), Res(), Res()]
    r_x2st = [Res() for _ in range(KC)]
    r_cast = [Res(), Res()]
    r_cast2 = [Res(), Res()]
    r_zs = Res()
    r_z4 = Res()
    r_lgt = Res()
    r_lgttm = Res()
    r_rt = Res()
    r_cmb8 = Res()
    r_combtm = Res()
    r_combbc = Res()
    r_gbuf = [Res(), Res()]
    r_ubuf = [Res(), Res()]
    r_gs = [Res(), Res()]
    r_us = [Res(), Res()]
    r_rhtmp = Res()
    r_rh = [Res() for _ in range(NRI)]
    r_frstage = [Res(), Res()]
    r_frst = [Res() for _ in range(KC)]
    r_h = [Res() for _ in range(NI)]
    r_dbuf = [Res(), Res()]
    r_rdbuf = [Res(), Res()]
    r_frbuf = [Res(), Res()]
    r_fstage = [Res(), Res()]
    r_fst = [Res() for _ in range(KC)]
    r_fbf = Res()
    r_f2bf = Res()
    r_epx2 = [Res(), Res(), Res()]
    r_epf = [Res(), Res(), Res()]
    r_epo = [Res(), Res()]
    r_misc = Res()

    out_toks = []

    def tap(name, ap_src, waits, reads=(), q="sync"):
        if name in taps:
            out_toks.append(S.dma(q,
                                  lambda e, a=ap_src, d=taps[name]: e.dma_start(out=d, in_=a),
                                  reads=reads, waits=waits))

    S.phase = "prolog"
    const_toks = []
    for dst, src in [(ident, ident_h), (ones1x128, ones1x128_h),
                     (ones1x128bf, ones1x128bf_h), (ones128bf, ones128bf_h),
                     (lnw_sb, lnw_h), (lnb_sb, lnb_h), (dtb_sb, dtb_h),
                     (D_sb, Dcol_h), (convw_sb, convw_h), (convb_sb, convb_h),
                     (Acols_sb, Acols_h), (selR_sb, selR_h), (inpT_f32, inpT_h),
                     (mask_sb, xmask), (sel8_sb, sel8_h),
                     (cconsts_sb, cconsts_h)]:
        const_toks.append(S.dma("sync", lambda e, d=dst, s=src: e.dma_start(out=d, in_=s)))
    for dst, src in [(outTbf, outT_h), (xprojTbf, xprojT_h), (dtprojTbf, dtprojT_h),
                     (selDbf, selD_h), (selBbf, selB_h), (selCbf, selC_h),
                     (routerTbf, routerT_h)]:
        const_toks.append(S.dma("gpsimd", lambda e, d=dst, s=src: e.dma_start(out=d, in_=s)))
    CW = const_toks

    t_xpad = S.op("vector", lambda e: e.memset(xp_all[0:96, 0:1], 0.0))
    tk = S.op("scalar", lambda e: e.activation(A2[:], Acols_sb[:], AF.Exp), waits=CW)
    t_A2 = S.op("vector", lambda e: e.tensor_scalar_mul(A2[:], A2[:], -0.001), waits=[tk])
    t_nconvb = S.op("scalar", lambda e: e.activation(nconvb_sb[:], convb_sb[:], AF.Copy,
                                                     scale=-1.0), waits=CW)

    def mm_c0(e):
        for k in range(KC):
            i = e.matmul(pb[5][0:96, 0:1], inpT_f32[:, 96 * k:96 * (k + 1)],
                         lnb_sb[:, k:k + 1], start=(k == 0), stop=(k == KC - 1))
        return i
    tk = S.op("tensor", mm_c0, writes=[r_pb[5]], waits=CW)
    t_c0 = S.op("scalar", lambda e: e.activation(c0_sb[:], pb[5][0:96, 0:1], AF.Copy),
                reads=[r_pb[5]], waits=[tk])

    def mk_inpbf(e):
        for k in range(KC):
            i = e.activation(inpTbf[:, 96 * k:96 * (k + 1)], inpT_f32[:, 96 * k:96 * (k + 1)],
                             AF.Copy, scale=lnw_sb[:, k:k + 1])
        return i
    t_inpbf = S.op("scalar", mk_inpbf, waits=CW)
    for rr_ in (r_dA[0], r_dA[1], r_dBu[0]):
        rr_.w = t_inpbf

    def mm_wpsum(e):
        for k in range(KC):
            i = e.matmul(pb[6][0:1, 0:96], ones128bf[:], inpTbf[:, 96 * k:96 * (k + 1)],
                         start=(k == 0), stop=(k == KC - 1))
        return i
    tk = S.op("tensor", mm_wpsum, writes=[r_pb[6]], waits=[t_inpbf] + CW)
    t_wpsum = S.op("scalar", lambda e: e.activation(wpsum_bf[:], pb[6][0:1, 0:96], AF.Copy),
                   reads=[r_pb[6]], waits=[tk])

    S.phase = "mixer"
    x3 = xT.rearrange("(k p) t -> p k t", p=128)
    t_xbf = [None] * NCHUNK

    def load_chunk(c):
        cc = c % 2
        dst = xcb[cc].rearrange("p (k t) -> p k t", t=512)
        t_xbf[c] = S.dma("gpsimd",
                         lambda e, d=dst, c=c: e.dma_start(out=d,
                                                           in_=x3[:, :, 512 * c:512 * (c + 1)]),
                         writes=[r_xcb[cc]])

    load_chunk(0)
    load_chunk(1)
    R_MU, R_M2, R_V, R_T1, R_R1, R_RR, R_NMU, R_T2 = range(8)
    tok_xpc = [None] * NCHUNK

    for c in range(NCHUNK):
        cc = c % 2
        S.phase = "mstat"
        def mm_sx(e, cc=cc):
            for k in range(KC):
                i = e.matmul(pb[0][0:1, :], ones128bf[:], xcb[cc][:, 512 * k:512 * (k + 1)],
                             start=(k == 0), stop=(k == KC - 1))
            return i
        t_sx = S.op("tensor", mm_sx, reads=[r_xcb[cc]], writes=[r_pb[0]],
                    waits=[t_xbf[c]] + CW)
        for k in range(KC):
            kk = k % 2
            tsq = S.op("scalar", lambda e, k=k, kk=kk, cc=cc: e.activation(
                           xsq_bf[kk][:], xcb[cc][:, 512 * k:512 * (k + 1)], AF.Square),
                       reads=[r_xcb[cc]], writes=[r_xsq[kk]], waits=[t_xbf[c]])
            tmm = S.op("tensor", lambda e, k=k, kk=kk: e.matmul(
                           pb[1][0:1, :], ones128bf[:], xsq_bf[kk][:],
                           start=(k == 0), stop=(k == KC - 1)),
                       reads=[r_xsq[kk]], writes=[r_pb[1]] if k == 0 else [], waits=[tsq])
            if k > 0:
                r_pb[1].w = tmm
        t_sx2 = tmm

        tA = S.op("scalar", lambda e: e.activation(row(R_MU), pb[0][0:1, :], AF.Copy,
                                                   scale=INV_D),
                  reads=[r_pb[0]], writes=[r_rows[R_MU]], waits=[t_sx])
        tB = S.op("scalar", lambda e: e.activation(row(R_M2), pb[1][0:1, :], AF.Copy,
                                                   scale=INV_D),
                  reads=[r_pb[1]], writes=[r_rows[R_M2]], waits=[t_sx2])
        tL = S.op("scalar", lambda e: e.activation(row(R_NMU), pb[0][0:1, :], AF.Copy,
                                                   scale=-INV_D),
                  reads=[r_pb[0]], writes=[r_rows[R_NMU]], waits=[t_sx])
        tC = S.op("vector", lambda e: e.tensor_mul(row(R_T1), row(R_MU), row(R_MU)),
                  reads=[r_rows[R_MU]], writes=[r_rows[R_T1]], waits=[tA])
        tD = S.op("vector", lambda e: e.tensor_sub(row(R_V), row(R_M2), row(R_T1)),
                  reads=[r_rows[R_M2], r_rows[R_T1]], writes=[r_rows[R_V]], waits=[tB])
        tE = S.op("scalar", lambda e: e.activation(row(R_T1), row(R_V), AF.Ln,
                                                   scale=float(np.float32(1.0) + np.float32(1e-5)),
                                                   bias=float(np.float32(1e-5) * np.float32(1e-5))),
                  reads=[r_rows[R_V]], writes=[r_rows[R_T1]], waits=[tD])
        tK = S.op("scalar", lambda e: e.activation(row(R_RR), row(R_T1), AF.Exp, scale=-0.5),
                  reads=[r_rows[R_T1]], writes=[r_rows[R_RR]], waits=[tE])
        tM = S.op("scalar", lambda e, cc=cc: e.activation(nmu_bf[cc][:], row(R_NMU), AF.Copy),
                  reads=[r_rows[R_NMU]], writes=[r_nmu[cc]], waits=[tL])
        if c in (2, 3):
            off = 512 * (c - 2)
            tE2 = S.op("scalar", lambda e: e.activation(row(R_T2), row(R_V), AF.Ln, bias=EPS),
                       reads=[r_rows[R_V]], writes=[r_rows[R_T2]], waits=[tD])
            tF = S.op("scalar", lambda e, off=off: e.activation(
                          r1own[:, off:off + 512], row(R_T2), AF.Exp, scale=-0.5),
                      reads=[r_rows[R_T2]], writes=[r_r1own] if c == 3 else [], waits=[tE2])
            S.op("vector", lambda e, off=off: e.scalar_tensor_tensor(
                     nm1r1bf[:, off:off + 512], row(R_MU), -1.0, r1own[:, off:off + 512],
                     ALU.mult, ALU.mult),
                 reads=[r_rows[R_MU], r_r1own], writes=[r_nm1r1] if c == 3 else [],
                 waits=[tF])

        def mm_ip(e, cc=cc):
            for k in range(KC):
                e.matmul(pb[2][0:96, :], inpTbf[:, 96 * k:96 * (k + 1)],
                         xcb[cc][:, 512 * k:512 * (k + 1)], start=(k == 0), stop=False)
            return e.matmul(pb[2][0:96, :], wpsum_bf[:], nmu_bf[cc][:], start=False, stop=True)
        t_ip = S.op("tensor", mm_ip, reads=[r_xcb[cc], r_nmu[cc]], writes=[r_pb[2]],
                    waits=[t_xbf[c], t_inpbf, t_wpsum, tM])
        if c + 2 < NCHUNK:
            load_chunk(c + 2)

        t_rr = S.op("tensor", lambda e: e.matmul(pb[3][0:96, :], ones1x128[0:1, 0:96],
                                                 row(R_RR), start=True, stop=True),
                    reads=[r_rows[R_RR]], writes=[r_pb[3]], waits=[tK] + CW)
        t_rre = S.op("scalar", lambda e: e.activation(rr96_bc[:], pb[3][0:96, :], AF.Copy),
                     reads=[r_pb[3]], writes=[r_rr96], waits=[t_rr])
        t_mk = S.op("tensor", lambda e, c=c: e.matmul(pb[4][0:96, :], ones1x128bf[0:1, 0:96],
                                                      mask_sb[:, 512 * c:512 * (c + 1)],
                                                      start=True, stop=True),
                    writes=[r_pb[4]], waits=CW)
        t_mke = S.op("scalar", lambda e, cc=cc: e.activation(m96k[cc][:], pb[4][0:96, :],
                                                             AF.Copy),
                     reads=[r_pb[4]], writes=[r_m96[cc]], waits=[t_mk])
        if c == 2:
            tap("mu2", rowsA[:, 512 * R_MU:512 * (R_MU + 1)], [tA],
                reads=[r_rows[R_MU]])
            tap("rr2", rr96_bc[0:96, :], [t_rre], reads=[r_rr96])

        xpc = xp_all[0:96, 1 + 512 * c:1 + 512 * (c + 1)]
        t1_ = S.op("vector", lambda e, xpc=xpc: e.tensor_mul(xpc, pb[2][0:96, :], rr96_bc[:]),
                   reads=[r_pb[2], r_rr96], writes=[r_xp[c]], waits=[t_ip, t_rre])
        if c == 2:
            tap("xpraw2", xp_all[0:96, 1 + 512 * c:1 + 512 * (c + 1)], [t1_],
                reads=[r_xp[c]])
        t3_ = S.op("vector", lambda e, xpc=xpc, cc=cc: e.scalar_tensor_tensor(
                       xpc, xpc, c0_sb[:], m96k[cc][:], ALU.add, ALU.mult),
                   reads=[r_m96[cc]], writes=[r_xp[c]], waits=[t1_, t_mke, t_c0])
        tok_xpc[c] = t3_
        tap(f"xp{c}", xp_all[0:96, 1 + 512 * c:1 + 512 * (c + 1)], [t3_],
            reads=[r_xp[c]])

        S.phase = "postxp"
        pc = c - 1
        if pc < 0:
            continue
        pcc = pc % 2
        upre = u_ch[pcc][:, :]
        x_l = xp_all[0:96, 512 * pc + 0:512 * pc + 512]
        x_m = xp_all[0:96, 512 * pc + 1:512 * pc + 513]
        x_r = xp_all[0:96, 512 * pc + 2:512 * pc + 514]
        ta = S.op("vector", lambda e, x_l=x_l, upre=upre: e.tensor_scalar(
                      upre, x_l, convw_sb[:, 0:1], None, ALU.mult),
                  reads=[r_xp[pc]], writes=[r_u[pcc]], waits=[tok_xpc[pc], t_xpad] + CW)
        tb = S.op("vector", lambda e, x_m=x_m, upre=upre: e.scalar_tensor_tensor(
                      upre, x_m, convw_sb[:, 1:2], upre, ALU.mult, ALU.add),
                  reads=[r_xp[pc]], writes=[r_u[pcc]], waits=[ta])
        tc_ = S.op("vector", lambda e, x_r=x_r, upre=upre: e.scalar_tensor_tensor(
                       upre, x_r, convw_sb[:, 2:3], upre, ALU.mult, ALU.add),
                   reads=[r_xp[pc], r_xp[c]], writes=[r_u[pcc]], waits=[tb, tok_xpc[c]])
        td = S.op("scalar", lambda e, upre=upre: e.activation(
                      bc_a[0:96, :], upre, AF.Exp, scale=-1.0, bias=nconvb_sb[:]),
                  reads=[r_u[pcc]], writes=[r_bca], waits=[tc_, t_nconvb])
        te = S.op("vector", lambda e: e.tensor_scalar_add(bc_a[0:96, :], bc_a[0:96, :], 1.0),
                  writes=[r_bca], waits=[td])
        tf_ = S.op("vector", lambda e: e.reciprocal(bc_b[0:96, :], bc_a[0:96, :]),
                   reads=[r_bca], writes=[r_bcb], waits=[te])
        th = S.op("vector", lambda e, upre=upre: e.scalar_tensor_tensor(
                      upre, upre, convb_sb[:], bc_b[0:96, :], ALU.add, ALU.mult),
                  reads=[r_bcb], writes=[r_u[pcc]], waits=[tf_, td])
        ti = S.op("scalar", lambda e, upre=upre: e.activation(u_bf[:], upre, AF.Copy),
                  reads=[r_u[pcc]], writes=[r_ubf], waits=[th])
        tap(f"u{pc}", u_ch[pcc][0:96, :], [th], reads=[r_u[pcc]])

        tj = S.op("tensor", lambda e: e.matmul(pb[5][0:38, :], xprojTbf[:], u_bf[:],
                                               start=True, stop=True),
                  reads=[r_ubf], writes=[r_pb[5]], waits=[ti])
        tk_ = S.op("scalar", lambda e: e.activation(dbc_bf[:], pb[5][0:38, :], AF.Copy),
                   reads=[r_pb[5]], writes=[r_dbc], waits=[tj])
        tl = S.op("tensor", lambda e: e.matmul(pb[5][0:96, :], dtprojTbf[:], dbc_bf[0:6, :],
                                               start=True, stop=True),
                  reads=[r_dbc], writes=[r_pb[5]], waits=[tk_])
        tm_ = S.op("scalar", lambda e: e.activation(bc_a[0:96, :], pb[5][0:96, :], AF.Abs,
                                                    bias=dtb_sb[:]),
                   reads=[r_pb[5]], writes=[r_bca], waits=[tl])
        tn = S.op("scalar", lambda e: e.activation(bc_a[0:96, :], bc_a[0:96, :], AF.Exp,
                                                   scale=-1.0),
                  writes=[r_bca], waits=[tm_])
        to_ = S.op("scalar", lambda e: e.activation(bc_a[0:96, :], bc_a[0:96, :], AF.Ln,
                                                    bias=1.0),
                   writes=[r_bca], waits=[tn])
        tp_ = S.op("scalar", lambda e: e.activation(bc_b[0:96, :], pb[5][0:96, :], AF.Relu,
                                                    bias=dtb_sb[:]),
                   reads=[r_pb[5]], writes=[r_bcb], waits=[tm_])
        tq = S.op("vector", lambda e: e.tensor_add(dus_bf[:], bc_b[0:96, :], bc_a[0:96, :]),
                  reads=[r_bca, r_bcb], writes=[r_dus], waits=[to_, tp_])
        tr_ = S.op("vector", lambda e, pcc=pcc: e.scalar_tensor_tensor(
                       du_bf[:], dus_bf[:], 0.001, u_ch[pcc][:, :], ALU.mult, ALU.mult),
                   reads=[r_dus, r_u[pcc]], writes=[r_dub], waits=[tq])
        ts_ = S.op("vector", lambda e, pcc=pcc: e.tensor_mul(du_bf[:], du_bf[:], m96k[pcc][:]),
                   reads=[r_m96[pcc]], writes=[r_dub], waits=[tr_])

        tt = S.op("tensor", lambda e: e.matmul(pb[6][:, :], selBbf[:], dbc_bf[:],
                                               start=True, stop=True),
                  reads=[r_dbc], writes=[r_pb[6]], waits=[tk_] + CW)
        tu_ = S.op("scalar", lambda e, pcc=pcc: e.activation(Bp_bc[pcc][:], pb[6][:, :],
                                                             AF.Copy),
                   reads=[r_pb[6]], writes=[r_Bp[pcc]], waits=[tt])
        if pc >= 2:
            tv = S.op("tensor", lambda e: e.matmul(pb[6][:, :], selCbf[:], dbc_bf[:],
                                                   start=True, stop=True),
                      reads=[r_dbc], writes=[r_pb[6]], waits=[tu_])
            tw_ = S.op("scalar", lambda e, pcc=pcc: e.activation(C_bc[pcc][:], pb[6][:, :],
                                                                 AF.Copy),
                       reads=[r_pb[6]], writes=[r_C[pcc]], waits=[tv])

        y_last = None
        S.phase = "jloop"
        for j in range(NJ):
            jb = j % 2
            bs = 3 if jb == 0 else 6
            bs2 = 4 if jb == 0 else 7
            tj1 = S.op("tensor", lambda e, j=j, bs=bs: e.matmul(
                           pb[bs][:, :], selDbf[:, 128 * j:128 * (j + 1)], dus_bf[:],
                           start=True, stop=True),
                       reads=[r_dus], writes=[r_pb[bs]], waits=[tq])
            tj2 = S.op("scalar", lambda e, j=j, jb=jb, bs=bs: e.activation(
                           dA_b[jb][:], pb[bs][:, :], AF.Exp, scale=A2[:, j:j + 1]),
                       reads=[r_pb[bs]], writes=[r_dA[jb]], waits=[tj1, t_A2])
            tj3 = S.op("tensor", lambda e, j=j, bs2=bs2: e.matmul(
                           pb[bs2][:, :], selDbf[:, 128 * j:128 * (j + 1)], du_bf[:],
                           start=True, stop=True),
                       reads=[r_dub], writes=[r_pb[bs2]], waits=[ts_])
            tj4 = S.op("vector", lambda e, jb=jb, pcc=pcc, bs2=bs2: e.tensor_mul(
                           dBu_b[jb][:], pb[bs2][:, :], Bp_bc[pcc][:]),
                       reads=[r_pb[bs2], r_Bp[pcc]], writes=[r_dBu[jb]], waits=[tj3, tu_])
            init = 0.0 if pc == 0 else state_cols[:, j:j + 1]
            tj5 = S.op("vector", lambda e, jb=jb, init=init: e.tensor_tensor_scan(
                           xs_b[jb][:], dA_b[jb][:], dBu_b[jb][:], init, ALU.mult, ALU.add),
                       reads=[r_dA[jb], r_dBu[jb]] + ([r_state[j]] if pc > 0 else []),
                       writes=[r_xs[jb]], waits=[tj2, tj4])
            tj6 = S.op("vector", lambda e, j=j, jb=jb: e.tensor_copy(
                           state_cols[:, j:j + 1], xs_b[jb][:, 511:512]),
                       reads=[r_xs[jb]], writes=[r_state[j]], waits=[tj5])
            if pc >= 2:
                tj7 = S.op("vector", lambda e, jb=jb, pcc=pcc: e.tensor_mul(
                               prod_b[jb][:], xs_b[jb][:], C_bc[pcc][:]),
                           reads=[r_xs[jb], r_C[pcc]], writes=[r_dA[jb]], waits=[tw_, tj6])
                tj8 = S.op("tensor", lambda e, j=j, jb=jb: e.matmul(
                               pb[2][0:96, :], selR_sb[:, 96 * j:96 * (j + 1)], prod_b[jb][:],
                               start=(j == 0), stop=(j == NJ - 1)),
                           reads=[r_dA[jb]], writes=[r_pb[2]] if j == 0 else [],
                           waits=[tj7] + CW)
                if j > 0:
                    r_pb[2].w = tj8
                y_last = tj8

        if pc >= 2:
            off = 512 * (pc - 2)
            ty1 = S.op("vector", lambda e, off=off, pcc=pcc: e.scalar_tensor_tensor(
                           y_f[:, off:off + 512], u_ch[pcc][:, :], D_sb[:],
                           pb[2][0:96, :], ALU.mult, ALU.add),
                       reads=[r_pb[2], r_u[pcc]], writes=[r_yf], waits=[y_last] + CW)
            S.op("scalar", lambda e, off=off: e.activation(
                     y_bf[:, off:off + 512], y_f[:, off:off + 512], AF.Copy),
                 reads=[r_yf], writes=[r_ybf], waits=[ty1])
    tap("yf", y_f[0:96, :], [r_ybf.w], reads=[r_yf])

    S.phase = "O"
    S.fence()

    def stats_tile(src_f32, first, last, wtok, par=0):
        cb = cast_bf if par == 0 else cast_bf2
        rc = r_cast if par == 0 else r_cast2
        tca = S.op("scalar", lambda e, s=src_f32, cb=cb: e.activation(cb[0][:], s, AF.Copy),
                   writes=[rc[0]], waits=wtok)
        tcb = S.op("scalar", lambda e, s=src_f32, cb=cb: e.activation(cb[1][:], s, AF.Square),
                   writes=[rc[1]], waits=wtok)
        ts1 = S.op("tensor", lambda e, first=first, last=last, cb=cb: e.matmul(
                       pb[0][0:1, :], ones128bf[:], cb[0][:], start=first, stop=last),
                   reads=[rc[0]], writes=[r_pb[0]] if first else [], waits=[tca] + CW)
        if not first:
            r_pb[0].w = ts1
        ts2 = S.op("tensor", lambda e, first=first, last=last, cb=cb: e.matmul(
                       pb[1][0:1, :], ones128bf[:], cb[1][:], start=first, stop=last),
                   reads=[rc[1]], writes=[r_pb[1]] if first else [], waits=[tcb] + CW)
        if not first:
            r_pb[1].w = ts2
        return ts1, ts2

    def rstd_rows(dst_r_res, dst_r, dst_nm_res, dst_nm, rowset):
        mu, m2, v, t1 = rowset
        tA = S.op("scalar", lambda e: e.activation(mu, pb[0][0:1, :], AF.Copy, scale=INV_D),
                  reads=[r_pb[0]], writes=[r_rows[R_MU]])
        tB = S.op("scalar", lambda e: e.activation(m2, pb[1][0:1, :], AF.Copy, scale=INV_D),
                  reads=[r_pb[1]], writes=[r_rows[R_M2]])
        tC = S.op("vector", lambda e: e.tensor_mul(t1, mu, mu),
                  reads=[r_rows[R_MU]], writes=[r_rows[R_T1]], waits=[tA])
        tD = S.op("vector", lambda e: e.tensor_sub(v, m2, t1),
                  reads=[r_rows[R_M2], r_rows[R_T1]], writes=[r_rows[R_V]], waits=[tB, tC])
        tE = S.op("scalar", lambda e: e.activation(t1, v, AF.Ln, bias=EPS),
                  reads=[r_rows[R_V]], writes=[r_rows[R_T1]], waits=[tD])
        tF = S.op("scalar", lambda e: e.activation(dst_r, t1, AF.Exp, scale=-0.5),
                  reads=[r_rows[R_T1]], writes=[dst_r_res], waits=[tE])
        tG = S.op("vector", lambda e: e.scalar_tensor_tensor(dst_nm, mu, -1.0, dst_r,
                                                             ALU.mult, ALU.mult),
                  reads=[r_rows[R_MU], dst_r_res], writes=[dst_nm_res], waits=[tF])
        return tF, tG

    def bcast(dstbuf, dst_res, src_row, src_res, wtoks, bank, npart=128):
        tb1 = S.op("tensor", lambda e: e.matmul(pb[bank][0:npart, :], ones1x128[0:1, 0:npart],
                                                src_row, start=True, stop=True),
                   reads=[src_res], writes=[r_pb[bank]], waits=wtoks + CW)
        tb2 = S.op("scalar", lambda e: e.activation(dstbuf[0:npart, :], pb[bank][0:npart, :],
                                                    AF.Copy),
                   reads=[r_pb[bank]], writes=[dst_res], waits=[tb1])
        return tb2

    xh3 = x_half.rearrange("p (k t) -> p k t", t=512)
    ln4_toks = []

    for hc in range(2):
        off = 512 * hc
        t_xh = S.dma("sync", lambda e, off=off: e.dma_start(
                         out=xh3[:, :, :], in_=x3[:, :, 1024 + off:1024 + off + 512]),
                     writes=[r_xhalf])
        t_r1b = bcast(bc_a, r_bca, r1own[:, off:off + 512], r_r1own, [], 3)
        for mc in range(KC):
            bank = 2 if mc % 2 == 0 else 5

            def mm_op1(e, mc=mc, off=off, bank=bank):
                e.matmul(pb[bank][:, :], outTbf[:, 128 * mc:128 * (mc + 1)],
                         y_bf[:, off:off + 512], start=True, stop=False)
                return e.matmul(pb[bank][:, :], ones1x128bf[:], nm1r1bf[:, off:off + 512],
                                start=False, stop=True)
            t_op = S.op("tensor", mm_op1, reads=[r_ybf, r_nm1r1], writes=[r_pb[bank]],
                        waits=CW)
            bcb_i = bc_b if mc % 2 == 0 else bc_b2
            r_bcb_i = r_bcb if mc % 2 == 0 else r_bcb2
            t_s = S.op("vector", lambda e, mc=mc, bcb_i=bcb_i: e.tensor_mul(
                           bcb_i[:, :], x_half[:, 512 * mc:512 * (mc + 1)], bc_a[:, :]),
                       reads=[r_xhalf, r_bca], writes=[r_bcb_i], waits=[t_xh, t_r1b])
            t_mr = S.op("vector", lambda e, mc=mc, bank=bank, bcb_i=bcb_i: e.tensor_add(
                            mr_sb[:, 512 * mc:512 * (mc + 1)], pb[bank][:, :], bcb_i[:, :]),
                        reads=[r_pb[bank], r_bcb_i], writes=[r_mr] if mc == 0 else [],
                        waits=[t_op, t_s])
            if mc > 0:
                r_mr.w = t_mr
            stats_tile(mr_sb[:, 512 * mc:512 * (mc + 1)], mc == 0, mc == KC - 1, [t_mr],
                       par=mc % 2)
        tap(f"mr{hc}", mr_sb[:, :], [r_mr.w], reads=[r_mr])
        t_r3, t_nm3 = rstd_rows(r_rows[R_R1], row(R_R1), r_rows[R_T2], row(R_T2),
                                (row(R_MU), row(R_M2), row(R_V), row(R_T1)))
        t_r3b = bcast(bc_a, r_bca, row(R_R1), r_rows[R_R1], [t_r3], 3)
        t_nm3b = bcast(bc_b, r_bcb, row(R_T2), r_rows[R_T2], [t_nm3], 4)
        for k in range(KC):
            kk = k % 2
            tx1 = S.op("vector", lambda e, k=k: e.tensor_mul(
                           mr_sb[:, 512 * k:512 * (k + 1)], mr_sb[:, 512 * k:512 * (k + 1)],
                           bc_a[:, :]),
                       reads=[r_bca], writes=[r_mr], waits=[t_r3b])
            tx2 = S.op("gpsimd", lambda e, k=k, kk=kk: e.tensor_add(
                           x2t[kk][:], x_half[:, 512 * k:512 * (k + 1)], bc_b[:, :]),
                       reads=[r_bcb, r_xhalf], writes=[r_x2t[kk]], waits=[t_nm3b, t_xh])
            tx3 = S.op("vector", lambda e, k=k, kk=kk: e.tensor_add(
                           x2t[kk][:], mr_sb[:, 512 * k:512 * (k + 1)], x2t[kk][:]),
                       reads=[r_mr], writes=[r_x2t[kk]], waits=[tx1, tx2])
            tst = S.dma("scalar", lambda e, k=k, kk=kk, off=off: e.dma_start(
                            out=x2_st[k, :, off:off + 512], in_=x2t[kk][:]),
                        reads=[r_x2t[kk]], waits=[tx3])
            r_x2st[k] = tst if hc == 0 else tst
            cb4 = cast_bf if k % 2 == 0 else cast_bf2
            rc4 = r_cast if k % 2 == 0 else r_cast2
            tca = S.op("scalar", lambda e, kk=kk, cb4=cb4: e.activation(
                           cb4[0][:], x2t[kk][:], AF.Copy),
                       reads=[r_x2t[kk]], writes=[rc4[0]], waits=[tx3])
            tcb = S.op("scalar", lambda e, kk=kk, cb4=cb4: e.activation(
                           cb4[1][:], x2t[kk][:], AF.Square),
                       reads=[r_x2t[kk]], writes=[rc4[1]], waits=[tx3])
            ts1 = S.op("tensor", lambda e, k=k, cb4=cb4: e.matmul(
                           pb[6][0:1, :], ones128bf[:], cb4[0][:],
                           start=(k == 0), stop=(k == KC - 1)),
                       reads=[rc4[0]], writes=[r_pb[6]] if k == 0 else [], waits=[tca] + CW)
            if k > 0:
                r_pb[6].w = ts1
            ts2 = S.op("tensor", lambda e, k=k, cb4=cb4: e.matmul(
                           pb[7][0:1, :], ones128bf[:], cb4[1][:],
                           start=(k == 0), stop=(k == KC - 1)),
                       reads=[rc4[1]], writes=[r_pb[7]] if k == 0 else [], waits=[tcb] + CW)
            if k > 0:
                r_pb[7].w = ts2
        def row4(i):
            return rows4[:, 512 * i:512 * (i + 1)]
        s_r4, s_nm4 = 2 * hc, 2 * hc + 1
        tA = S.op("scalar", lambda e: e.activation(row(R_MU), pb[6][0:1, :], AF.Copy,
                                                   scale=INV_D),
                  reads=[r_pb[6]], writes=[r_rows[R_MU]], waits=[ts1])
        tB = S.op("scalar", lambda e: e.activation(row(R_M2), pb[7][0:1, :], AF.Copy,
                                                   scale=INV_D),
                  reads=[r_pb[7]], writes=[r_rows[R_M2]], waits=[ts2])
        tC = S.op("vector", lambda e: e.tensor_mul(row(R_T1), row(R_MU), row(R_MU)),
                  reads=[r_rows[R_MU]], writes=[r_rows[R_T1]], waits=[tA])
        tD = S.op("vector", lambda e: e.tensor_sub(row(R_V), row(R_M2), row(R_T1)),
                  reads=[r_rows[R_M2], r_rows[R_T1]], writes=[r_rows[R_V]], waits=[tB, tC])
        tE = S.op("scalar", lambda e: e.activation(row(R_T1), row(R_V), AF.Ln, bias=EPS),
                  reads=[r_rows[R_V]], writes=[r_rows[R_T1]], waits=[tD])
        tF = S.op("scalar", lambda e, s_r4=s_r4, row4=row4: e.activation(
                      row4(s_r4), row(R_T1), AF.Exp, scale=-0.5),
                  reads=[r_rows[R_T1]], writes=[r_rows4[s_r4]], waits=[tE])
        tG = S.op("vector", lambda e, s_r4=s_r4, s_nm4=s_nm4, row4=row4:
                  e.scalar_tensor_tensor(
                      row4(s_nm4), row(R_MU), -1.0, row4(s_r4), ALU.mult, ALU.mult),
                  reads=[r_rows[R_MU], r_rows4[s_r4]], writes=[r_rows4[s_nm4]], waits=[tF])
        ln4_toks.append((tF, tG))

    S.fence()
    for hc in range(2):
        off = 512 * hc
        s_r4, s_nm4 = 2 * hc, 2 * hc + 1
        t_r4b = bcast(bc_a, r_bca, rows4[:, 512 * s_r4:512 * (s_r4 + 1)],
                      r_rows4[s_r4], [ln4_toks[hc][0]], 3)
        t_nm4b = bcast(bc_b, r_bcb, rows4[:, 512 * s_nm4:512 * (s_nm4 + 1)],
                       r_rows4[s_nm4], [ln4_toks[hc][1]], 4)
        for k in range(KC):
            kk = k % 3
            trl = S.dma("scalar" if k % 2 == 0 else "sync",
                        lambda e, k=k, kk=kk, off=off: e.dma_start(
                            out=x2t[kk][:], in_=x2_st[k, :, off:off + 512]),
                        writes=[r_x2t[kk]], waits=[r_x2st[k]])
            tz1 = S.op("vector", lambda e, kk=kk: e.tensor_mul(
                           x2t[kk][:], x2t[kk][:], bc_a[:, :]),
                       reads=[r_bca], writes=[r_x2t[kk]], waits=[trl, t_r4b])
            tz2 = S.op("vector", lambda e, k=k, kk=kk, off=off: e.tensor_add(
                           z4_sb[:, 1024 * k + off:1024 * k + off + 512], x2t[kk][:],
                           bc_b[:, :]),
                       reads=[r_x2t[kk], r_bcb], writes=[r_z4] if (hc == 0 and k == 0) else [],
                       waits=[tz1, t_nm4b])
            if not (hc == 0 and k == 0):
                r_z4.w = tz2
    tap("x2st", x2_st[:, :, :].rearrange("a b c -> (a b) c"), list(r_x2st))
    tap("z4t", z4_sb[:, :], [r_z4.w], reads=[r_z4], q="gpsimd")

    S.phase = "router"
    snapO = S.snapshot()
    S.fence_from(snapO)
    t_lge = [None, None]
    for hc in range(2):
        off = 512 * hc

        def mm_lg(e, off=off):
            for k in range(KC):
                i = e.matmul(pb[5][0:8, :], routerTbf[:, 8 * k:8 * (k + 1)],
                             z4_sb[:, 1024 * k + off:1024 * k + off + 512],
                             start=(k == 0), stop=(k == KC - 1))
            return i
        t_lg = S.op("tensor", mm_lg, reads=[r_z4], writes=[r_pb[5]], waits=CW)
        t_lge[hc] = S.op("scalar", lambda e, off=off: e.activation(
                             lgt_sb[:, off:off + 512], pb[5][0:8, :], AF.Copy),
                         reads=[r_pb[5]], writes=[r_lgt], waits=[t_lg])
    for tt in range(8):
        t_tr = S.op("tensor", lambda e, tt=tt: e.transpose(
                        pb[7][:, 0:8], lgt_sb[:, 128 * tt:128 * (tt + 1)], ident[0:8, 0:8]),
                    reads=[r_lgt], writes=[r_pb[7]], waits=t_lge + CW)
        S.op("scalar", lambda e, tt=tt: e.activation(lgt_tm[:, 8 * tt:8 * (tt + 1)],
                                                     pb[7][:, 0:8], AF.Copy),
             reads=[r_pb[7]], writes=[r_lgttm])
    for tt in range(8):
        lg8 = lgt_tm[:, 8 * tt:8 * (tt + 1)]
        tm1 = S.op("vector", lambda e, lg8=lg8, tt=tt: e.reduce_max(
                       rt_m1[:, tt:tt + 1], lg8, axis=mybir.AxisListType.X),
                   reads=[r_lgttm], writes=[r_rt], waits=[])
        tmsk = S.op("vector", lambda e, lg8=lg8, tt=tt: e.tensor_scalar(
                        rt_t[:], lg8, rt_m1[:, tt:tt + 1], None, ALU.is_equal),
                    reads=[r_lgttm], writes=[r_rt], waits=[tm1])
        tmv = S.op("vector", lambda e, lg8=lg8: e.scalar_tensor_tensor(
                       rt_t[:], rt_t[:], -1e30, lg8, ALU.mult, ALU.add),
                   reads=[r_lgttm], writes=[r_rt], waits=[tmsk])
        tm2 = S.op("vector", lambda e: e.reduce_max(rt_t2[:, 0:1], rt_t[:],
                                                    axis=mybir.AxisListType.X),
                   writes=[r_rt], waits=[tmv])
        tdl = S.op("vector", lambda e, tt=tt: e.tensor_sub(rt_t2[:, 1:2], rt_t2[:, 0:1],
                                                           rt_m1[:, tt:tt + 1]),
                   writes=[r_rt], waits=[tm2])
        tsg = S.op("scalar", lambda e: e.activation(rt_t2[:, 2:3], rt_t2[:, 1:2], AF.Exp),
                   reads=[r_rt], writes=[r_rt], waits=[tdl])
        tT_ = S.op("vector", lambda e: e.tensor_scalar_add(rt_t2[:, 3:4], rt_t2[:, 2:3], 1.0),
                   writes=[r_rt], waits=[tsg])
        tin = S.op("vector", lambda e: e.reciprocal(rt_t2[:, 4:5], rt_t2[:, 3:4]),
                   writes=[r_rt], waits=[tT_])
        tcm = S.op("vector", lambda e, tt=tt: e.scalar_tensor_tensor(
                       comb_tm[:, tt:tt + 1], rt_t2[:, 2:3], rt_t2[:, 4:5], rt_t2[:, 4:5],
                       ALU.mult, ALU.add),
                   reads=[r_rt], writes=[r_combtm], waits=[tin])
    t_ct = S.op("tensor", lambda e: e.transpose(pb[7][0:8, 0:128], comb_tm[:], ident[:]),
                reads=[r_combtm], writes=[r_pb[7]], waits=CW)
    t_ce = S.op("scalar", lambda e: e.activation(cmb8[:], pb[7][0:8, 0:128], AF.Copy),
                reads=[r_pb[7]], writes=[r_cmb8], waits=[t_ct])
    for half in range(2):
        def mm_cb(e, half=half):
            for q in range(4):
                tt = 4 * half + q
                i = e.matmul(pb[3][:, 128 * q:128 * (q + 1)],
                             sel8_sb[:, 128 * tt:128 * (tt + 1)], cmb8[:],
                             start=True, stop=True)
            return i
        t_cb = S.op("tensor", mm_cb, reads=[r_cmb8], writes=[r_pb[3]], waits=[t_ce] + CW)
        S.op("scalar", lambda e, half=half: e.activation(
                 comb_bc[:, 512 * half:512 * (half + 1)], pb[3][:, :], AF.Copy),
             reads=[r_pb[3]], writes=[r_combbc])

    tap("combt", comb_bc[:, :], [r_combbc.w], reads=[r_combbc])
    tap("lgtt", lgt_tm[:, :], [r_lgttm.w], reads=[r_lgttm])

    S.phase = "routed"
    S.fence_from(snapO)
    t_wrg = [None] * NRI
    t_wru = [None] * NRI

    def sched_rdma(ir):
        ib = ir % 2
        t_wrg[ir] = S.dma("gpsimd", lambda e, ir=ir, ib=ib: e.dma_start(
                              out=gbuf[ib][:], in_=Wrg_h[ir]), writes=[r_gbuf[ib]])
        t_wru[ir] = S.dma("gpsimd", lambda e, ir=ir, ib=ib: e.dma_start(
                              out=ubuf[ib][:], in_=Wru_h[ir]), writes=[r_ubuf[ib]])

    sched_rdma(0)
    sched_rdma(1)
    for ir in range(NRI):
        ib = ir % 2
        for half in range(2):
            bg = (0 if ib == 0 else 4) + half
            bu = (2 if ib == 0 else 6) + half
            hof = 512 * half

            def mm_rg(e, ib=ib, hof=hof, bg=bg):
                for k in range(KC):
                    i = e.matmul(pb[bg][:, :], gbuf[ib][:, 128 * k:128 * (k + 1)],
                                 z4_sb[:, 1024 * k + hof:1024 * k + hof + 512],
                                 start=(k == 0), stop=(k == KC - 1))
                return i
            t_g = S.op("tensor", mm_rg, reads=[r_gbuf[ib], r_z4], writes=[r_pb[bg]],
                       waits=[t_wrg[ir]])

            def mm_ru(e, ib=ib, hof=hof, bu=bu):
                for k in range(KC):
                    i = e.matmul(pb[bu][:, :], ubuf[ib][:, 128 * k:128 * (k + 1)],
                                 z4_sb[:, 1024 * k + hof:1024 * k + hof + 512],
                                 start=(k == 0), stop=(k == KC - 1))
                return i
            t_u = S.op("tensor", mm_ru, reads=[r_ubuf[ib], r_z4], writes=[r_pb[bu]],
                       waits=[t_wru[ir]])
            tsl = S.op("scalar", lambda e, half=half, bg=bg: e.activation(
                           gs_t[half][:], pb[bg][:, :], AF.Silu),
                       reads=[r_pb[bg]], writes=[r_gs[half]], waits=[t_g])
            tuc = S.op("scalar", lambda e, half=half, bu=bu: e.activation(
                           us_t[half][:], pb[bu][:, :], AF.Copy),
                       reads=[r_pb[bu]], writes=[r_us[half]], waits=[t_u])
            tmul = S.op("gpsimd", lambda e, half=half, hof=hof: e.tensor_mul(
                            rh_tmp[:, hof:hof + 512], gs_t[half][:], us_t[half][:]),
                        reads=[r_gs[half], r_us[half]], writes=[r_rhtmp], waits=[tsl, tuc])
            S.op("vector", lambda e, ir=ir, hof=hof: e.tensor_mul(
                     r_h_sb[:, 1024 * ir + hof:1024 * ir + hof + 512],
                     rh_tmp[:, hof:hof + 512], comb_bc[:, hof:hof + 512]),
                 reads=[r_rhtmp, r_combbc], writes=[r_rh[ir]], waits=[tmul])
        if ir + 2 < NRI:
            sched_rdma(ir + 2)
    tap("rht", r_h_sb[:, :], [r_rh[NRI - 1].w], reads=list(r_rh), q="gpsimd")
    S.fence_from(snapO)
    for dt in range(KC):
        db = dt % 2
        t_wrd = S.dma("gpsimd", lambda e, dt=dt, db=db: e.dma_start(
                          out=rdbuf[db][:], in_=Wrd_h[dt]), writes=[r_rdbuf[db]])
        for half in range(2):
            bank = 2 * db + half
            hof = 512 * half

            def mm_rd(e, db=db, hof=hof, bank=bank):
                for ikr in range(NRI):
                    i = e.matmul(pb[bank][:, :], rdbuf[db][:, 128 * ikr:128 * (ikr + 1)],
                                 r_h_sb[:, 1024 * ikr + hof:1024 * ikr + hof + 512],
                                 start=(ikr == 0), stop=(ikr == NRI - 1))
                return i
            t_rd = S.op("tensor", mm_rd,
                        reads=[r_rdbuf[db]] + [r_rh[i_] for i_ in range(NRI)],
                        writes=[r_pb[bank]], waits=[t_wrd])
            t_ev = S.op("scalar", lambda e, db=db, hof=hof, bank=bank: e.activation(
                            frstage[db][:, hof:hof + 512], pb[bank][:, :], AF.Copy),
                        reads=[r_pb[bank]], writes=[r_frstage[db]] if half == 0 else [],
                        waits=[t_rd])
            if half == 1:
                r_frstage[db].w = t_ev
        r_frst[dt] = S.dma("scalar", lambda e, dt=dt, db=db: e.dma_start(
                               out=fr_st[dt], in_=frstage[db][:]),
                           reads=[r_frstage[db]], waits=[])

    tap("frst", fr_st[:, :, :].rearrange("a b c -> (a b) c"), list(r_frst))

    S.phase = "gu"
    S.fence()
    t_stash_done = r_x2st[KC - 1]
    t_wg = [None] * NI
    t_wu = [None] * NI

    def sched_wdma(i):
        ib = i % 2
        t_wg[i] = S.dma("gpsimd", lambda e, i=i, ib=ib: e.dma_start(
                            out=gbuf[ib][:], in_=Wg_h[i]), writes=[r_gbuf[ib]])
        t_wu[i] = S.dma("gpsimd", lambda e, i=i, ib=ib: e.dma_start(
                            out=ubuf[ib][:], in_=Wu_h[i]), writes=[r_ubuf[ib]])

    sched_wdma(0)
    sched_wdma(1)
    for i in range(NI):
        ib = i % 2
        for half in range(2):
            bg = (0 if ib == 0 else 4) + half
            bu = (2 if ib == 0 else 6) + half
            hof = 512 * half

            def mm_g(e, ib=ib, hof=hof, bg=bg):
                for k in range(KC):
                    i2 = e.matmul(pb[bg][:, :], gbuf[ib][:, 128 * k:128 * (k + 1)],
                                  z4_sb[:, 1024 * k + hof:1024 * k + hof + 512],
                                  start=(k == 0), stop=(k == KC - 1))
                return i2
            t_g = S.op("tensor", mm_g, reads=[r_gbuf[ib], r_z4], writes=[r_pb[bg]],
                       waits=[t_wg[i]])

            def mm_u(e, ib=ib, hof=hof, bu=bu):
                for k in range(KC):
                    i2 = e.matmul(pb[bu][:, :], ubuf[ib][:, 128 * k:128 * (k + 1)],
                                  z4_sb[:, 1024 * k + hof:1024 * k + hof + 512],
                                  start=(k == 0), stop=(k == KC - 1))
                return i2
            t_u = S.op("tensor", mm_u, reads=[r_ubuf[ib], r_z4], writes=[r_pb[bu]],
                       waits=[t_wu[i]])
            tsl = S.op("scalar", lambda e, half=half, bg=bg: e.activation(
                           gs_t[half][:], pb[bg][:, :], AF.Silu),
                       reads=[r_pb[bg]], writes=[r_gs[half]], waits=[t_g])
            tuc = S.op("scalar", lambda e, half=half, bu=bu: e.activation(
                           us_t[half][:], pb[bu][:, :], AF.Copy),
                       reads=[r_pb[bu]], writes=[r_us[half]], waits=[t_u])
            S.op("gpsimd", lambda e, i=i, hof=hof, half=half: e.tensor_mul(
                     h_all[:, 1024 * i + hof:1024 * i + hof + 512],
                     gs_t[half][:], us_t[half][:]),
                 reads=[r_gs[half], r_us[half]], writes=[r_h[i]], waits=[tsl, tuc])
        if i + 2 < NI:
            sched_wdma(i + 2)

    S.phase = "down"
    S.fence()
    t_wd = [[None, None] for _ in range(KC)]
    t_fr = [None] * KC

    def sched_ddma(dt, hh):
        t_wd[dt][hh] = S.dma("gpsimd", lambda e, dt=dt, hh=hh: e.dma_start(
                                 out=dbuf[hh][:], in_=Wd_h[dt, :, 4096 * hh:4096 * (hh + 1)]),
                             writes=[r_dbuf[hh]])

    def sched_fdma(dt):
        db = dt % 2
        t_fr[dt] = S.dma("scalar", lambda e, dt=dt, db=db: e.dma_start(
                             out=frbuf[db][:], in_=fr_st[dt]),
                         writes=[r_frbuf[db]], waits=[r_frst[dt]])

    sched_ddma(0, 0)
    sched_ddma(0, 1)
    sched_fdma(0)
    sched_fdma(1)
    for dt in range(KC):
        db = dt % 2
        b0, b1 = 2 * db, 2 * db + 1

        def mm_dA(e, b0=b0, b1=b1):
            for ik in range(32):
                e.matmul(pb[b0][:, :], dbuf[0][:, 128 * ik:128 * (ik + 1)],
                         h_all[:, 1024 * ik:1024 * ik + 512],
                         start=(ik == 0), stop=False)
                i2 = e.matmul(pb[b1][:, :], dbuf[0][:, 128 * ik:128 * (ik + 1)],
                              h_all[:, 1024 * ik + 512:1024 * (ik + 1)],
                              start=(ik == 0), stop=False)
            return i2
        t_dA = S.op("tensor", mm_dA,
                    reads=[r_dbuf[0]] + [r_h[i_] for i_ in range(32)],
                    writes=[r_pb[b0], r_pb[b1]], waits=[t_wd[dt][0]])
        if dt + 1 < KC:
            sched_ddma(dt + 1, 0)

        def mm_dB(e, b0=b0, b1=b1):
            for ik in range(32, NI):
                e.matmul(pb[b0][:, :], dbuf[1][:, 128 * (ik - 32):128 * (ik - 31)],
                         h_all[:, 1024 * ik:1024 * ik + 512],
                         start=False, stop=(ik == NI - 1))
                i2 = e.matmul(pb[b1][:, :], dbuf[1][:, 128 * (ik - 32):128 * (ik - 31)],
                              h_all[:, 1024 * ik + 512:1024 * (ik + 1)],
                              start=False, stop=(ik == NI - 1))
            return i2
        t_dB = S.op("tensor", mm_dB,
                    reads=[r_dbuf[1]] + [r_h[i_] for i_ in range(32, NI)],
                    writes=[], waits=[t_wd[dt][1], t_dA])
        r_pb[b0].w = t_dB
        r_pb[b1].w = t_dB
        if dt + 1 < KC:
            sched_ddma(dt + 1, 1)
        for half in range(2):
            bank = 2 * db + half
            hof = 512 * half
            t_ev = S.op("vector", lambda e, db=db, hof=hof, bank=bank: e.tensor_add(
                            f_stage[db][:, hof:hof + 512], pb[bank][:, :],
                            frbuf[db][:, hof:hof + 512]),
                        reads=[r_pb[bank], r_frbuf[db]],
                        writes=[r_fstage[db]] if half == 0 else [],
                        waits=[t_dB, t_fr[dt]])
            if half == 1:
                r_fstage[db].w = t_ev
            tca = S.op("scalar", lambda e, db=db, hof=hof: e.activation(
                           f_bf[:, hof:hof + 512], f_stage[db][:, hof:hof + 512], AF.Copy),
                       reads=[r_fstage[db]], writes=[r_fbf], waits=[t_ev])
            tcb = S.op("scalar", lambda e, db=db, hof=hof: e.activation(
                           f2_bf[:, hof:hof + 512], f_stage[db][:, hof:hof + 512], AF.Square),
                       reads=[r_fstage[db]], writes=[r_f2bf], waits=[t_ev])
            ts1 = S.op("tensor", lambda e, dt=dt, hof=hof, half=half: e.matmul(
                           pb[4 + half][0:1, :], ones128bf[:], f_bf[:, hof:hof + 512],
                           start=(dt == 0), stop=(dt == KC - 1)),
                       reads=[r_fbf], writes=[r_pb[4 + half]] if dt == 0 else [],
                       waits=[tca] + CW)
            if dt > 0:
                r_pb[4 + half].w = ts1
            ts2 = S.op("tensor", lambda e, dt=dt, hof=hof, half=half: e.matmul(
                           pb[6 + half][0:1, :], ones128bf[:], f2_bf[:, hof:hof + 512],
                           start=(dt == 0), stop=(dt == KC - 1)),
                       reads=[r_f2bf], writes=[r_pb[6 + half]] if dt == 0 else [],
                       waits=[tcb] + CW)
            if dt > 0:
                r_pb[6 + half].w = ts2
        r_fst[dt] = S.dma("scalar", lambda e, dt=dt, db=db: e.dma_start(
                              out=f_st[dt], in_=f_stage[db][:]),
                          reads=[r_fstage[db]], waits=[])
        if dt + 2 < KC:
            sched_fdma(dt + 2)

    S.phase = "ep"
    S.fence()
    ep_r5 = [eprow(0), eprow(1)]
    ep_nm5 = [eprow(2), eprow(3)]
    for half in range(2):
        tA = S.op("scalar", lambda e, half=half: e.activation(
                      eprow(4), pb[4 + half][0:1, :], AF.Copy, scale=INV_D),
                  reads=[r_pb[4 + half]], writes=[r_eprows[4]])
        tB = S.op("scalar", lambda e, half=half: e.activation(
                      eprow(5), pb[6 + half][0:1, :], AF.Copy, scale=INV_D),
                  reads=[r_pb[6 + half]], writes=[r_eprows[5]])
        tC = S.op("vector", lambda e: e.tensor_mul(eprow(6), eprow(4), eprow(4)),
                  reads=[r_eprows[4]], writes=[r_eprows[6]], waits=[tA])
        tD = S.op("vector", lambda e: e.tensor_sub(eprow(7), eprow(5), eprow(6)),
                  reads=[r_eprows[5], r_eprows[6]], writes=[r_eprows[7]], waits=[tB, tC])
        tE = S.op("scalar", lambda e: e.activation(eprow(6), eprow(7), AF.Ln, bias=EPS),
                  reads=[r_eprows[7]], writes=[r_eprows[6]], waits=[tD])
        tF = S.op("scalar", lambda e, half=half: e.activation(
                      ep_r5[half], eprow(6), AF.Exp, scale=-0.5),
                  reads=[r_eprows[6]], writes=[r_eprows[half]], waits=[tE])
        S.op("vector", lambda e, half=half: e.scalar_tensor_tensor(
                 ep_nm5[half], eprow(4), -1.0, ep_r5[half], ALU.mult, ALU.mult),
             reads=[r_eprows[4], r_eprows[half]], writes=[r_eprows[2 + half]], waits=[tF])
    r5_bc = comb_bc
    nm5_bc = frbuf[0]
    for half in range(2):
        hof = 512 * half
        tb1 = S.op("tensor", lambda e, half=half: e.matmul(
                       pb[0][:, :], ones1x128[:], ep_r5[half], start=True, stop=True),
                   reads=[r_eprows[half]], writes=[r_pb[0]], waits=CW)
        S.op("scalar", lambda e, hof=hof: e.activation(r5_bc[:, hof:hof + 512],
                                                       pb[0][:, :], AF.Copy),
             reads=[r_pb[0]], writes=[r_combbc], waits=[tb1])
        tb2 = S.op("tensor", lambda e, half=half: e.matmul(
                       pb[1][:, :], ones1x128[:], ep_nm5[half], start=True, stop=True),
                   reads=[r_eprows[2 + half]], writes=[r_pb[1]], waits=CW)
        S.op("scalar", lambda e, hof=hof: e.activation(nm5_bc[:, hof:hof + 512],
                                                       pb[1][:, :], AF.Copy),
             reads=[r_pb[1]], writes=[r_frbuf[0]], waits=[tb2])
    for k in range(KC):
        kk = k % 3
        ko = k % 2
        t_lx = S.dma("scalar", lambda e, k=k, kk=kk: e.dma_start(
                         out=ep_x2[kk][:], in_=x2_st[k]), writes=[r_epx2[kk]],
                     waits=[r_x2st[k]])
        fsrc, rfs = ep_f[kk], r_epf[kk]
        t_lf = S.dma("gpsimd", lambda e, k=k, kk=kk: e.dma_start(
                         out=ep_f[kk][:], in_=f_st[k]), writes=[r_epf[kk]],
                     waits=[r_fst[k]])
        t_e1 = S.op("vector", lambda e, kk=kk, fsrc=fsrc: e.tensor_mul(
                        ep_f[kk][:], fsrc[:], r5_bc[:, :]),
                    reads=[r_combbc, rfs], writes=[r_epf[kk]], waits=[t_lf])
        t_e2 = S.op("gpsimd", lambda e, kk=kk, ko=ko: e.tensor_add(
                        ep_o[ko][:], ep_x2[kk][:], nm5_bc[:, :]),
                    reads=[r_frbuf[0], r_epx2[kk]], writes=[r_epo[ko]], waits=[t_lx])
        t_e3 = S.op("vector", lambda e, kk=kk, ko=ko: e.tensor_add(
                        ep_o[ko][:], ep_f[kk][:], ep_o[ko][:]),
                    reads=[r_epf[kk]], writes=[r_epo[ko]], waits=[t_e1, t_e2])
        out_toks.append(S.dma("sync", lambda e, k=k, ko=ko: e.dma_start(
                                  out=out_dev[128 * k:128 * (k + 1), :], in_=ep_o[ko][:]),
                              reads=[r_epo[ko]], waits=[t_e3]))

    final_waits = {}
    for (s, v, _) in out_toks:
        final_waits[s] = max(final_waits.get(s, 0), v)
    fw = list(final_waits.items())

    def final_op(e, fw=fw):
        for s, v in fw:
            e.wait_ge(_SEMS[s], v)
        return e.nop()
    S.cnt["s_sy"] = S.cnt.get("s_sy", 0) + 1
    S.ops.append(("sync", lambda e: final_op(e).then_inc(_SEMS["s_sy"], 1)))

    with nc.Block() as block:
        @block.sync
        def _(e):
            for eng, fn in S.ops:
                if eng == "sync":
                    fn(e)

        @block.gpsimd
        def _(e):
            for eng, fn in S.ops:
                if eng == "gpsimd":
                    fn(e)

        @block.tensor
        def _(e):
            for eng, fn in S.ops:
                if eng == "tensor":
                    fn(e)

        @block.vector
        def _(e):
            for eng, fn in S.ops:
                if eng == "vector":
                    fn(e)

        @block.scalar
        def _(e):
            for eng, fn in S.ops:
                if eng == "scalar":
                    fn(e)

    es.close()
    nc._dbg_sched = S
    return nc


_PROG_CACHE = {}


def kernel(**inputs):
    key = tuple(sorted(n for n, _ in _DEBUG_TAPS))
    if key not in _PROG_CACHE:
        _PROG_CACHE[key] = build_program(list(_DEBUG_TAPS))
    nc = _PROG_CACHE[key]
    g = _host_shared(inputs)
    x = inputs["L0_attn_feat_patch"]
    in_maps = []
    for c in range(8):
        m = dict(g)
        m.update(_host_percore(x, c))
        in_maps.append(m)
    res = run_bass_kernel_spmd(nc, in_maps, core_ids=list(range(8)))
    out = np.empty((4, 2048, 2048), np.float32)
    for c in range(8):
        b, h = c // 2, c % 2
        out[b, 1024 * h:1024 * (h + 1), :] = res.results[c]["out"].T
    if _DEBUG_TAPS:
        kernel._dbg = [{k: v for k, v in r.items() if k.startswith("dbg_")}
                       for r in res.results]
    return out
